# version 05
# baseline (speedup 1.0000x reference)
import contextlib
import os
import numpy as np
import concourse.bass as bass
import concourse.mybir as mybir
from concourse.bass_utils import run_bass_kernel_spmd
from concourse.aot_env import allow_ici_all_to_all

F32 = mybir.dt.float32
F16 = mybir.dt.float16
ALU = mybir.AluOpType
ACT = mybir.ActivationFunctionType
AXL = mybir.AxisListType

NCORES = 8
EPS_LN = 1e-5
EPS_RMS = 1e-6
MAX_THR = 0.125


class Cfg:
    def __init__(self, H=4096, DIN=8192, NTOK=8192, R=256, N=16, E=8, RR=16,
                 KC=4, NB=2):
        self.H, self.DIN, self.NTOK, self.R, self.N = H, DIN, NTOK, R, N
        self.E, self.RR, self.KC, self.NB = E, RR, KC, NB
        self.TPC = NTOK // NCORES
        self.DSL = DIN // NCORES
        self.LB = NTOK // NB
        self.NT = self.TPC // 128
        self.NDT = self.DSL // 128
        self.KH = H // 128
        self.CPB = self.LB // self.TPC
        self.NCOL = 2 * DIN
        self.ERT = E * RR
        self.RC = R // 128
        self.BCW = 2 * N
        self.TH = self.TPC // 512
        self.CW = min(1024, self.TPC)
        self.NTC = self.LB // self.CW
        self.C6 = self.CW // 512
        self.HCH = 256
        self.NHC = H // self.HCH
        self.NDG = DIN // 128
        assert self.TPC % 512 == 0 and self.DSL % 128 == 0 and self.ERT == 128


CFG_FULL = Cfg()


def build_nc(cfg: Cfg):
    H, DIN, R, N, E, RR, KC = cfg.H, cfg.DIN, cfg.R, cfg.N, cfg.E, cfg.RR, cfg.KC
    TPC, DSL, NT, NDT, KH = cfg.TPC, cfg.DSL, cfg.NT, cfg.NDT, cfg.KH
    NB, CPB, NCOL, ERT, RC, BCW = cfg.NB, cfg.CPB, cfg.NCOL, cfg.ERT, cfg.RC, cfg.BCW
    TH, CW, NTC, C6, HCH, NHC = cfg.TH, cfg.CW, cfg.NTC, cfg.C6, cfg.HCH, cfg.NHC
    NTOK, LB, NDG = cfg.NTOK, cfg.LB, cfg.NDG
    W288 = R + 2 * N
    NCC = NCOL // 512
    EW3 = [128, 128, BCW]

    nc = bass.Bass("TRN2", target_bir_lowering=False, debug=False)

    x_in = nc.dram_tensor("x", [TPC, H], F32, kind="ExternalInput")
    win_in = nc.dram_tensor("in_proj_w", [H, NCOL], F32, kind="ExternalInput")
    convw_in = nc.dram_tensor("conv_w_sl", [128, NDT, KC], F32, kind="ExternalInput")
    convb_in = nc.dram_tensor("conv_b_sl", [128, NDT], F32, kind="ExternalInput")
    xproj_in = nc.dram_tensor("x_proj_sl", [128, 3, DSL], F32, kind="ExternalInput")
    dtproj_in = nc.dram_tensor("dt_proj_sl", [128, NDT, R], F32, kind="ExternalInput")
    dtbias_in = nc.dram_tensor("dt_bias_sl", [128, NDT], F32, kind="ExternalInput")
    alog_in = nc.dram_tensor("A_log_sl", [128, NDT * N], F32, kind="ExternalInput")
    dparam_in = nc.dram_tensor("D_sl", [128, NDT], F32, kind="ExternalInput")
    wout_in = nc.dram_tensor("out_proj_w", [H, DIN], F32, kind="ExternalInput")
    dtlnw_in = nc.dram_tensor("dt_ln_w", [128, RC], F32, kind="ExternalInput")
    bclnw_in = nc.dram_tensor("bc_ln_w", [128], F32, kind="ExternalInput")
    wbase_in = nc.dram_tensor("W_base", [H, H], F32, kind="ExternalInput")
    bbase_in = nc.dram_tensor("b_base", [128, H], F32, kind="ExternalInput")
    wgate_in = nc.dram_tensor("W_gate", [128, H], F32, kind="ExternalInput")
    wthr_in = nc.dram_tensor("W_thr", [128, KH], F32, kind="ExternalInput")
    bthr_in = nc.dram_tensor("b_thr", [128], F32, kind="ExternalInput")
    aw_in = nc.dram_tensor("A_w", [128, KH, 128], F32, kind="ExternalInput")
    bw_in = nc.dram_tensor("B_w", [128, KH, E, RR], F32, kind="ExternalInput")
    out_ext = nc.dram_tensor("out", [TPC, H], F32, kind="ExternalOutput")

    xh_sp = nc.dram_tensor("xh_sp", [DIN, TPC], F16, kind="Internal")
    z_sp = nc.dram_tensor("z_sp", [DIN, TPC], F16, kind="Internal")
    xh_scan = nc.dram_tensor("xh_scan", [NCORES, DSL, TPC], F16, kind="Internal")
    xc_sp = nc.dram_tensor("xc_sp", [DSL, NTOK], F16, kind="Internal")
    ssm_sp = nc.dram_tensor("ssm_sp", [NTOK, W288], F32, kind="Internal")
    ssm_red = nc.dram_tensor("ssm_red", [TPC, W288], F32, kind="Internal")
    dtrT_loc = nc.dram_tensor("dtrT_loc", [W288, TPC], F16, kind="Internal")
    dtr_all = nc.dram_tensor("dtr_all", [NCORES, W288, TPC], F16, kind="Internal")
    g_sp = nc.dram_tensor("g_sp", [TPC, H], F16, kind="Internal")
    y_sp = nc.dram_tensor("y_sp", [NCORES, DSL, TPC], F16, kind="Internal")
    y_own = nc.dram_tensor("y_own", [NCORES, DSL, TPC], F16, kind="Internal")

    ident16 = nc.inline_tensor(np.eye(128, dtype=np.float16), name="ident16")
    selm = np.zeros((128, BCW * 128), dtype=np.float16)
    for _n in range(BCW):
        selm[_n, _n * 128:(_n + 1) * 128] = 1.0
    selmat_c = nc.inline_tensor(selm, name="selmat_c")
    ident32 = nc.inline_tensor(np.eye(128, dtype=np.float32), name="ident32")

    A = nc.alloc_sbuf_tensor
    id16 = A("id16", [128, 128], F16)
    id32 = A("id32", [128, 128], F32)
    bthr_bc = A("bthr_bc", [128, 1], F32)
    convw_sb = A("convw_sb", [128, NDT, KC], F32)
    convb_sb = A("convb_sb", [128, NDT], F32)
    dparam_sb = A("dparam_sb", [128, NDT], F32)
    dtbias_sb = A("dtbias_sb", [128, NDT], F32)
    dtbias_ng = A("dtbias_ng", [128, NDT], F32)
    a_sb = A("a_sb", [128, NDT * N], F32)
    xprojT = A("xprojT", [128, NDT, W288], F16)
    dtprojT = A("dtprojT", [128, RC, NDT, 128], F16)
    dtlnw_sb = A("dtlnw_sb", [128, RC], F32)
    bclnw_sb = A("bclnw_sb", [128, 1], F32)
    bthr_ng = A("bthr_ng", [128, 1], F32)
    selmat = A("selmat", [128, BCW * 128], F16)

    ctx = contextlib.ExitStack()

    class Sem:
        def __init__(self, name):
            self.h = ctx.enter_context(nc.semaphore(name))
            self.v = 0

    const_vals = [EPS_LN, EPS_RMS, -0.5, -1.0, 1.0 / R, 1.0 / N]
    const_tensors = []
    for ci, cv in enumerate(const_vals):
        cta = A(f"cst{ci}", [128, 1], F32)
        const_tensors.append((cta, float(cv)))
        nc.const_aps.register(F32, float(cv), cta[:, :])

    s_ld = Sem("s_ld")
    s_g = Sem("s_g")
    s_pe = Sem("s_pe")
    s_v = Sem("s_v")
    s_sc = Sem("s_sc")
    s_st = Sem("s_st")
    s_cc = Sem("s_cc")
    s_xv = Sem("s_xv")
    s_xs = Sem("s_xs")

    class Ring:

        def __init__(self, name, n):
            self.sems = [Sem(f"{name}{i}") for i in range(n)]
            self.n = n
            self.i = 0

        def issue(self, eng, out, in_, **kw):
            s = self.sems[self.i % self.n]
            if s.v > 0:
                eng.wait_ge(s.h, s.v)
            eng.dma_start(out=out, in_=in_, **kw).then_inc(s.h, 16)
            s.v += 16
            self.i += 1
            return (s.h, s.v)

        def wait_all(self, eng):
            for s in self.sems:
                if s.v > 0:
                    eng.wait_ge(s.h, s.v)

    R_ld = Ring("rld", 8)
    R_g = Ring("rg", 4)
    R_st = Ring("rst", 4)

    def edge(eng, instr):
        sx = s_xs if eng.engine == mybir.EngineType.Activation else s_xv
        sx.v += 1
        instr.then_inc(sx.h, 1)
        eng.wait_ge(sx.h, sx.v)

    GQ = 4

    def gdma(g, out, in_, **kw):
        if s_g.v >= GQ * 16:
            g.wait_ge(s_g.h, s_g.v - (GQ - 1) * 16)
        g.dma_start(out=out, in_=in_, **kw).then_inc(s_g.h, 16)
        s_g.v += 16

    ps = [ctx.enter_context(nc.psum_tensor(f"ps{i}", [128, 512], F32))
          for i in range(7)]
    pst = ctx.enter_context(nc.psum_tensor("pst", [128, 512], F16))

    RG = [list(range(NCORES))]

    KPH = int(os.environ.get("KPH", "99"))

    class _Stop(Exception):
        pass

    def stop_after(n):
        if KPH <= n:
            raise _Stop()

    with ctx:
      try:
          with nc.Block("p0_load") as blk:
              @blk.vector
              def _(v):
                  for cta, cv in const_tensors:
                      v.memset(cta[:, :], cv)

              @blk.sync
              def _(sync):
                  for dst, src in [
                      (id16[:, :], ident16[:, :]),
                      (id32[:, :], ident32[:, :]),
                      (bthr_bc[:, :], bthr_in[:].unsqueeze(1)),
                      (selmat[:, :], selmat_c[:, :]),
                      (convw_sb[:, :, :], convw_in[:, :, :]),
                      (convb_sb[:, :], convb_in[:, :]),
                      (dparam_sb[:, :], dparam_in[:, :]),
                      (dtbias_sb[:, :], dtbias_in[:, :]),
                      (dtlnw_sb[:, :], dtlnw_in[:, :]),
                      (bclnw_sb[:, :], bclnw_in[:].unsqueeze(1)),
                  ][:int(os.environ.get("KL0", "99"))]:
                      R_ld.issue(sync, dst, src)
                  R_ld.wait_all(sync)

          stop_after(-3)
          ctx2 = contextlib.ExitStack()
          xnT = ctx2.enter_context(nc.sbuf_tensor("xnT", [128, KH, TPC], F16))
          awT = ctx2.enter_context(nc.sbuf_tensor("awT", [128, KH, ERT], F16))
          bwT = ctx2.enter_context(nc.sbuf_tensor("bwT", [ERT, H], F16))
          wg9T = ctx2.enter_context(nc.sbuf_tensor("wg9T", [128, KH, E + 1], F16))
          bbase_sb = ctx2.enter_context(nc.sbuf_tensor("bbase_sb", [128, H], F16))
          hwT = ctx2.enter_context(nc.sbuf_tensor("hwT", [128, TPC], F16))

          with ctx2:
              with (
                  nc.sbuf_tensor("xp_raw", [128, 3, DSL], F16) as xp_raw,
                  nc.sbuf_tensor("dtp_raw", [128, NDT, R], F16) as dtp_raw,
                  nc.sbuf_tensor("aw_raw", [128, KH, 128], F16) as aw_raw,
                  nc.sbuf_tensor("bw_raw", [128, KH, E, RR], F16) as bw_raw,
                  nc.sbuf_tensor("wg_raw", [128, H], F16) as wg_raw,
                  nc.sbuf_tensor("wthr_raw", [128, KH], F16) as wthr_raw,
                  nc.sbuf_tensor("alog_raw", [128, NDT * N], F32) as alog_raw,
                  nc.sbuf_tensor("a_tmp", [128, NDT * N], F32) as a_tmp,
              ):
                  with nc.Block("p0_rawload") as blk:
                      @blk.gpsimd
                      def _(g):
                          for dst, src in [
                              (xp_raw[:, :, :], xproj_in[:, :, :]),
                              (dtp_raw[:, :, :], dtproj_in[:, :, :]),
                              (aw_raw[:, :, :], aw_in[:, :, :]),
                              (bw_raw[:, :, :, :], bw_in[:, :, :, :]),
                              (wg_raw[:, :], wgate_in[:, :]),
                              (wthr_raw[:, :], wthr_in[:, :]),
                              (bbase_sb[:, :], bbase_in[:, :]),
                          ][:int(os.environ.get("KL1", "99"))]:
                              R_g.issue(g, dst, src)
                          R_g.wait_all(g)

                      @blk.sync
                      def _(sync):
                          R_ld.issue(sync, alog_raw[:, :], alog_in[:, :])
                          R_ld.wait_all(sync)

                  stop_after(-2)
                  jobs = []
                  for ec in range(3):
                      er = EW3[ec]
                      for i in range(NDT):
                          jobs.append((xp_raw[:, ec, i * 128:(i + 1) * 128],
                                       xprojT[:, i, ec * 128:ec * 128 + er], None))
                  for i in range(NDT):
                      for rc in range(RC):
                          jobs.append((dtp_raw[:, i, rc * 128:(rc + 1) * 128],
                                       dtprojT[:, rc, i, :], ("pp", dtlnw_sb, rc)))
                  for k in range(KH):
                      jobs.append((aw_raw[:, k, :], awT[:, k, :], None))
                  for ob in range(KH):
                      jobs.append((bw_raw[:, ob, :, :].rearrange("p a b -> p (a b)"),
                                   bwT[:, ob * 128:(ob + 1) * 128], ("c", 2.0)))
                  for k in range(KH):
                      jobs.append((wg_raw[:, k * 128:(k + 1) * 128],
                                   wg9T[:, k, 0:E], None))
                  jobs = jobs[:int(os.environ.get("KJOBS", "9999"))]
                  NJ = len(jobs)
                  pe0, v0, sc0 = s_pe.v, s_v.v, s_sc.v

                  with nc.Block("p0_tr") as blk:
                      @blk.tensor
                      def _(t):
                          for j, (src, dst, scl) in enumerate(jobs):
                              assert src.shape[0] == 128 and src.shape[-1] == 128
                              if j >= 4:
                                  t.wait_ge(s_v.h, v0 + j - 3)
                              sl = j % 4
                              t.transpose(ps[sl][:, :].bitcast(F16)[:, 0:128],
                                          src, id16[:, :]).then_inc(s_pe.h, 1)
                          s_pe.v = pe0 + NJ

                      @blk.scalar
                      def _(s):
                          if os.environ.get("KSC", "1") == "1":
                              s.activation(a_tmp[:, :], alog_raw[:, :],
                                           ACT.Exp).then_inc(s_sc.h, 1)
                          else:
                              s.nop().then_inc(s_sc.h, 1)
                          s_sc.v = sc0 + 1

                      @blk.vector
                      def _(v):
                          for j, (src, dst, scl) in enumerate(jobs):
                              v.wait_ge(s_pe.h, pe0 + j + 1)
                              sl = j % 4
                              blkap = ps[sl][:, :].bitcast(F16)[:, 0:dst.shape[-1]]
                              if scl is None:
                                  v.tensor_copy(dst, blkap).then_inc(s_v.h, 1)
                              elif scl[0] == "c":
                                  v.tensor_scalar(dst, blkap, scl[1], None,
                                                  ALU.mult).then_inc(s_v.h, 1)
                              else:
                                  _, tsb, rc = scl
                                  v.tensor_scalar(dst, blkap, tsb[:, rc:rc + 1], None,
                                                  ALU.mult).then_inc(s_v.h, 1)
                          if os.environ.get("KVT", "1") == "1":
                              for k in range(KH):
                                  v.tensor_copy(wg9T[:, k, E:E + 1], wthr_raw[:, k:k + 1])
                              v.tensor_scalar(dtbias_ng[:, :], dtbias_sb[:, :], -1.0, None,
                                              ALU.mult)
                              v.tensor_scalar(bthr_ng[:, :], bthr_bc[:, :], -1.0, None,
                                              ALU.mult)
                              v.wait_ge(s_sc.h, sc0 + 1)
                              v.tensor_scalar(a_sb[:, :], a_tmp[:, :], -1.0, None, ALU.mult)
                          s_v.v = v0 + NJ

              stop_after(1)
              with (
                  nc.sbuf_tensor("xt1", [128, H], F32) as xt1,
                  nc.sbuf_tensor("sq1", [128, H], F32) as sq1,
                  nc.sbuf_tensor("xn16", [128, H], F16) as xn16,
                  nc.sbuf_tensor("mbuf", [128, 8], F32) as mbuf,
              ):
                  for tt in range(NT):
                      v0, sc0, pe0 = s_v.v, s_sc.v, s_pe.v
                      hx = {}
                      with nc.Block(f"p1_{tt}") as blk:
                          @blk.sync
                          def _(sync):
                              hx["x"] = R_ld.issue(sync, xt1[:, :],
                                                   x_in[tt * 128:(tt + 1) * 128, :])

                          @blk.scalar
                          def _(s):
                              s.wait_ge(*hx["x"])
                              s.activation(sq1[:, :], xt1[:, :],
                                           ACT.Square).then_inc(s_sc.h, 1)
                              s.wait_ge(s_v.h, v0 + 2)
                              edge(s, s.activation(mbuf[:, 6:7], mbuf[:, 5:6], ACT.Ln,
                                                   bias=float(EPS_LN)))
                              s.activation(mbuf[:, 7:8], mbuf[:, 6:7], ACT.Exp,
                                           scale=-0.5).then_inc(s_sc.h, 1)
                              s_sc.v = sc0 + 2

                          @blk.vector
                          def _(v):
                              v.wait_ge(*hx["x"])
                              v.tensor_reduce(mbuf[:, 0:1], xt1[:, :], AXL.X, ALU.add)
                              v.wait_ge(s_sc.h, sc0 + 1)
                              v.tensor_reduce(mbuf[:, 1:2], sq1[:, :], AXL.X, ALU.add
                                              ).then_inc(s_v.h, 1)
                              v.wait_ge(s_v.h, v0 + 1)
                              edge(v, v.tensor_scalar(mbuf[:, 2:3], mbuf[:, 0:1], 1.0 / H, None, ALU.mult))
                              edge(v, v.tensor_scalar(mbuf[:, 3:4], mbuf[:, 1:2], 1.0 / H, None, ALU.mult))
                              edge(v, v.tensor_tensor(mbuf[:, 4:5], mbuf[:, 2:3], mbuf[:, 2:3], ALU.mult))
                              v.tensor_tensor(mbuf[:, 5:6], mbuf[:, 3:4], mbuf[:, 4:5],
                                              ALU.subtract).then_inc(s_v.h, 1)
                              v.wait_ge(s_v.h, v0 + 2)
                              v.wait_ge(s_sc.h, sc0 + 2)
                              v.tensor_scalar(xn16[:, :], xt1[:, :],
                                              mbuf[:, 2:3], mbuf[:, 7:8],
                                              ALU.subtract, ALU.mult).then_inc(s_v.h, 1)
                              for kk in range(KH):
                                  v.wait_ge(s_pe.h, pe0 + kk + 1)
                                  sl = kk % 4
                                  v.tensor_copy(xnT[:, kk, tt * 128:(tt + 1) * 128],
                                                ps[sl][:, :].bitcast(F16)[:, 0:128]
                                                ).then_inc(s_v.h, 1)
                              s_v.v = v0 + 3 + KH

                          @blk.tensor
                          def _(t):
                              t.wait_ge(s_v.h, v0 + 3)
                              for kk in range(KH):
                                  if kk >= 4:
                                      t.wait_ge(s_v.h, v0 + 3 + kk - 3)
                                  t.transpose(ps[kk % 4][:, :].bitcast(F16)[:, 0:128],
                                              xn16[:, kk * 128:(kk + 1) * 128],
                                              id16[:, :]).then_inc(s_pe.h, 1)
                              s_pe.v = pe0 + KH

              stop_after(2)
              with (
                  nc.sbuf_tensor("slab", [128, 2, KH, 512], F16) as slab,
                  nc.sbuf_tensor("stage2", [128, 2, TPC], F16) as stage2,
              ):
                  v0, g0, pe0 = s_v.v, s_g.v, s_pe.v
                  hslab = {}
                  hspill = {}
                  with nc.Block("p2") as blk:
                      @blk.gpsimd
                      def _(g):
                          for cc in range(NCC):
                              if cc >= 2:
                                  g.wait_ge(s_pe.h, pe0 + (cc - 1) * 4 * TH)
                              for k in range(KH):
                                  hslab[(cc, k)] = R_g.issue(
                                      g, slab[:, cc % 2, k, :],
                                      win_in[k * 128:(k + 1) * 128,
                                             cc * 512:(cc + 1) * 512])

                      @blk.sync
                      def _(sync):
                          for j2 in range(NCC * 4):
                              sync.wait_ge(s_v.h, v0 + (j2 + 1) * TH)
                              col0 = j2 * 128
                              if col0 < DIN:
                                  dst = xh_sp[col0:col0 + 128, :]
                              else:
                                  dst = z_sp[col0 - DIN:col0 - DIN + 128, :]
                              hspill[j2] = R_ld.issue(sync, dst, stage2[:, j2 % 2, :])

                      @blk.tensor
                      def _(t):
                          for cc in range(NCC):
                              for k in range(KH):
                                  t.wait_ge(*hslab[(cc, k)])
                              for ct in range(4):
                                  for th in range(TH):
                                      idx = (cc * 4 + ct) * TH + th
                                      bank = idx % 4
                                      if idx >= 4:
                                          t.wait_ge(s_v.h, v0 + idx - 3)
                                      for k in range(KH):
                                          mm = t.matmul(ps[bank][:, :],
                                                        slab[:, cc % 2, k, ct * 128:(ct + 1) * 128],
                                                        xnT[:, k, th * 512:(th + 1) * 512],
                                                        start=(k == 0), stop=(k == KH - 1))
                                      mm.then_inc(s_pe.h, 1)
                          s_pe.v = pe0 + NCC * 4 * TH

                      @blk.vector
                      def _(v):
                          for cc in range(NCC):
                              for ct in range(4):
                                  j2 = cc * 4 + ct
                                  for th in range(TH):
                                      idx = j2 * TH + th
                                      v.wait_ge(s_pe.h, pe0 + idx + 1)
                                      if th == 0 and j2 >= 2:
                                          v.wait_ge(*hspill[j2 - 2])
                                      v.tensor_copy(stage2[:, j2 % 2, th * 512:(th + 1) * 512],
                                                    ps[idx % 4][:, :]).then_inc(s_v.h, 1)
                          s_v.v = v0 + NCC * 4 * TH

              stop_after(3)
              with nc.Block("coll1") as blk:
                  @blk.gpsimd
                  def _(g):
                      R_ld.wait_all(g)
                      with allow_ici_all_to_all():
                          g.collective_compute(
                              "AllToAll", ALU.bypass, replica_groups=RG,
                              ins=[xh_sp.ap().opt()], outs=[xh_scan.ap().opt()],
                          ).then_inc(s_cc.h, 1)
                      s_cc.v += 1
                      g.wait_ge(s_cc.h, s_cc.v)

              stop_after(4)
              with (
                  nc.sbuf_tensor("xpad", [128, NDT, 3 + TPC], F16) as xpad,
                  nc.sbuf_tensor("xcb", [128, NDT, TPC], F16) as xcb,
                  nc.sbuf_tensor("cva", [128, TPC], F32) as cva,
                  nc.sbuf_tensor("cvb", [128, TPC], F32) as cvb,
                  nc.sbuf_tensor("ssm_ev", [128, 3, TPC], F32) as ssm_ev,
                  nc.sbuf_tensor("ssm_tr", [128, NT, W288], F32) as ssm_tr,
              ):
                  hxcsp_prev = [None] * NDT
                  hssm_prev = None
                  for r in range(NCORES):
                      v0, g0, pe0, sc0 = s_v.v, s_g.v, s_pe.v, s_sc.v
                      halo = (r % CPB != 0)
                      VC = NDT + 3 * TH + NT * 3
                      hload = {}
                      hxcsp = {}
                      hssm = {}
                      hssm_prev_cap = hssm_prev
                      with nc.Block(f"p3_{r}") as blk:
                          @blk.sync
                          def _(sync):
                              for i in range(NDT):
                                  h1 = R_ld.issue(sync, xpad[:, i, 3:],
                                                  xh_scan[r, i * 128:(i + 1) * 128, :])
                                  if halo:
                                      h2 = R_ld.issue(sync, xpad[:, i, 0:3],
                                                      xh_scan[r - 1, i * 128:(i + 1) * 128,
                                                              TPC - 3:TPC])
                                      hload[i] = [h1, h2]
                                  else:
                                      hload[i] = [h1]
                              for i in range(NDT):
                                  sync.wait_ge(s_sc.h, sc0 + i + 1)
                                  hxcsp[i] = R_ld.issue(
                                      sync, xc_sp[i * 128:(i + 1) * 128,
                                                  r * TPC:(r + 1) * TPC],
                                      xcb[:, i, :])
                              sync.wait_ge(s_v.h, v0 + VC)
                              hssm[0] = R_ld.issue(
                                  sync,
                                  ssm_sp[r * TPC:(r + 1) * TPC, :].rearrange(
                                      "(tt p) e -> p tt e", p=128),
                                  ssm_tr[:, :, :])

                          @blk.vector
                          def _(v):
                              for i in range(NDT):
                                  for hh in hload[i]:
                                      v.wait_ge(*hh)
                                  if i >= 1:
                                      v.wait_ge(s_sc.h, sc0 + i)
                                  if r == 0 and i == 0:
                                      v.memset(ssm_ev[:, 2, :], 0.0)
                                  xp = xpad[:, i, :]
                                  if not halo:
                                      edge(v, v.memset(xp[:, 0:3], 0.0))
                                  w = [convw_sb[:, i, kk:kk + 1] for kk in range(KC)]
                                  edge(v, v.tensor_scalar(cva[:, :], xp[:, 0:TPC], w[0],
                                                          convb_sb[:, i:i + 1], ALU.mult, ALU.add))
                                  edge(v, v.scalar_tensor_tensor(cvb[:, :], xp[:, 1:1 + TPC], w[1],
                                                                 cva[:, :], ALU.mult, ALU.add))
                                  edge(v, v.scalar_tensor_tensor(cva[:, :], xp[:, 2:2 + TPC], w[2],
                                                                 cvb[:, :], ALU.mult, ALU.add))
                                  v.scalar_tensor_tensor(cvb[:, :], xp[:, 3:3 + TPC], w[3],
                                                         cva[:, :], ALU.mult, ALU.add
                                                         ).then_inc(s_v.h, 1)
                              v.wait_ge(s_pe.h, pe0 + 1)
                              for ec in range(3):
                                  for th in range(TH):
                                      v.tensor_copy(ssm_ev[0:EW3[ec], ec, th * 512:(th + 1) * 512],
                                                    ps[ec * TH + th][0:EW3[ec], :]
                                                    ).then_inc(s_v.h, 1)
                              if hssm_prev_cap is not None:
                                  v.wait_ge(*hssm_prev_cap)
                              for tt in range(NT):
                                  for ec in range(3):
                                      jj = tt * 3 + ec
                                      v.wait_ge(s_pe.h, pe0 + 1 + jj + 1)
                                      rb = (ps[6][:, 0:EW3[ec]] if jj % 2 == 0 else
                                            pst[:, :].bitcast(F32)[:, 0:EW3[ec]])
                                      v.tensor_copy(ssm_tr[:, tt, ec * 128:ec * 128 + EW3[ec]],
                                                    rb).then_inc(s_v.h, 1)
                              s_v.v = v0 + VC

                          @blk.scalar
                          def _(s):
                              for i in range(NDT):
                                  s.wait_ge(s_v.h, v0 + i + 1)
                                  if hxcsp_prev[i] is not None:
                                      s.wait_ge(*hxcsp_prev[i])
                                  s.activation(xcb[:, i, :], cvb[:, :],
                                               ACT.Silu).then_inc(s_sc.h, 1)
                              s_sc.v = sc0 + NDT

                          @blk.tensor
                          def _(t):
                              for i in range(NDT):
                                  t.wait_ge(s_sc.h, sc0 + i + 1)
                                  for ec in range(3):
                                      for th in range(TH):
                                          mm = t.matmul(ps[ec * TH + th][0:EW3[ec], :],
                                                        xprojT[:, i, ec * 128:ec * 128 + EW3[ec]],
                                                        xcb[:, i, th * 512:(th + 1) * 512],
                                                        start=(i == 0), stop=(i == NDT - 1))
                              mm.then_inc(s_pe.h, 1)
                              for tt in range(NT):
                                  for ec in range(3):
                                      jj = tt * 3 + ec
                                      t.wait_ge(s_v.h, v0 + NDT + ec * TH + TH)
                                      if jj >= 2:
                                          t.wait_ge(s_v.h, v0 + NDT + 3 * TH + jj - 1)
                                      tb = (ps[6][:, 0:128] if jj % 2 == 0 else
                                            pst[:, :].bitcast(F32)[:, 0:128])
                                      t.transpose(
                                          tb,
                                          ssm_ev[:, ec, tt * 128:(tt + 1) * 128],
                                          id32[:, :]
                                      ).then_inc(s_pe.h, 1)
                              s_pe.v = pe0 + 1 + NT * 3
                      hxcsp_prev = [hxcsp[i] for i in range(NDT)]
                      hssm_prev = hssm[0]

              stop_after(5)
              with nc.Block("coll2") as blk:
                  @blk.gpsimd
                  def _(g):
                      R_ld.wait_all(g)
                      g.collective_compute(
                          "ReduceScatter", ALU.add, replica_groups=RG,
                          ins=[ssm_sp.ap().opt()], outs=[ssm_red.ap().opt()],
                      ).then_inc(s_cc.h, 1)
                      s_cc.v += 1
                      g.wait_ge(s_cc.h, s_cc.v)

              stop_after(6)
              with (
                  nc.sbuf_tensor("p4x", [128, W288], F32) as p4x,
                  nc.sbuf_tensor("p4sq", [128, W288], F32) as p4sq,
                  nc.sbuf_tensor("p4o", [128, W288], F16) as p4o,
                  nc.sbuf_tensor("p4m", [128, 8], F32) as p4m,
                  nc.sbuf_tensor("p4st", [128, 3 * 128], F16) as p4st,
              ):
                  GRP = [(0, R, R), (R, R + N, N), (R + N, W288, N)]
                  hst_prev = None
                  for tt in range(NT):
                      v0, sc0, pe0 = s_v.v, s_sc.v, s_pe.v
                      hx4 = {}
                      hst4 = {}
                      hst_prev_cap = hst_prev
                      with nc.Block(f"p4_{tt}") as blk:
                          @blk.sync
                          def _(sync):
                              hx4["x"] = R_ld.issue(sync, p4x[:, :],
                                                    ssm_red[tt * 128:(tt + 1) * 128, :])
                              sync.wait_ge(s_v.h, v0 + 2 + 3)
                              for ec in range(3):
                                  hst4[ec] = R_ld.issue(
                                      sync,
                                      dtrT_loc[ec * 128:ec * 128 + EW3[ec],
                                               tt * 128:(tt + 1) * 128],
                                      p4st[0:EW3[ec], ec * 128:(ec + 1) * 128])

                          @blk.scalar
                          def _(s):
                              s.wait_ge(*hx4["x"])
                              s.activation(p4sq[:, :], p4x[:, :],
                                           ACT.Square).then_inc(s_sc.h, 1)
                              s.wait_ge(s_v.h, v0 + 1)
                              for gi, (a0, b0, w0) in enumerate(GRP):
                                  edge(s, s.activation(p4m[:, 6:7], p4m[:, gi:gi + 1],
                                                       ACT.Ln, bias=float(EPS_RMS), scale=1.0 / w0))
                                  ii = s.activation(p4m[:, 3 + gi:4 + gi],
                                                    p4m[:, 6:7],
                                                    ACT.Exp, scale=-0.5)
                                  if gi < 2:
                                      edge(s, ii)
                              ii.then_inc(s_sc.h, 1)
                              s_sc.v = sc0 + 2

                          @blk.vector
                          def _(v):
                              if hst_prev_cap is not None:
                                  for hh in hst_prev_cap.values():
                                      v.wait_ge(*hh)
                              v.wait_ge(s_sc.h, sc0 + 1)
                              for gi, (a0, b0, w0) in enumerate(GRP):
                                  rr = v.tensor_reduce(p4m[:, gi:gi + 1], p4sq[:, a0:b0],
                                                       AXL.X, ALU.add)
                              rr.then_inc(s_v.h, 1)
                              v.wait_ge(s_sc.h, sc0 + 2)
                              v.wait_ge(s_v.h, v0 + 1)
                              for gi, (a0, b0, w0) in enumerate(GRP):
                                  vv = v.tensor_scalar(p4o[:, a0:b0], p4x[:, a0:b0],
                                                       p4m[:, 3 + gi:4 + gi], None,
                                                       ALU.mult)
                              vv.then_inc(s_v.h, 1)
                              for ec in range(3):
                                  v.wait_ge(s_pe.h, pe0 + ec + 1)
                                  v.tensor_copy(p4st[0:EW3[ec], ec * 128:(ec + 1) * 128],
                                                ps[ec][:, :].bitcast(F16)[0:EW3[ec], 0:128]
                                                ).then_inc(s_v.h, 1)
                              s_v.v = v0 + 5

                          @blk.tensor
                          def _(t):
                              t.wait_ge(s_v.h, v0 + 2)
                              for ec in range(3):
                                  t.transpose(ps[ec][:, :].bitcast(F16)[0:EW3[ec], 0:128],
                                              p4o[:, ec * 128:ec * 128 + EW3[ec]],
                                              id16[:, :]).then_inc(s_pe.h, 1)
                              s_pe.v = pe0 + 3
                      hst_prev = hst4

              stop_after(7)
              with nc.Block("coll3") as blk:
                  @blk.gpsimd
                  def _(g):
                      R_ld.wait_all(g)
                      g.collective_compute(
                          "AllGather", ALU.bypass, replica_groups=RG,
                          ins=[dtrT_loc.ap().opt()], outs=[dtr_all.ap().opt()],
                      ).then_inc(s_cc.h, 1)
                      s_cc.v += 1
                      g.wait_ge(s_cc.h, s_cc.v)

              stop_after(8)
              with (
                  nc.sbuf_tensor("gl", [128, E + 1], F32) as gl,
                  nc.sbuf_tensor("gw", [128, E], F32) as gw,
                  nc.sbuf_tensor("gwe", [128, E], F32) as gwe,
                  nc.sbuf_tensor("gw3", [128, E], F32) as gw3,
                  nc.sbuf_tensor("gm", [128, 8], F32) as gm,
                  nc.sbuf_tensor("hw16", [128, ERT], F16) as hw16,
              ):
                  for tt in range(NT):
                      v0, sc0, pe0 = s_v.v, s_sc.v, s_pe.v
                      with nc.Block(f"p5a_{tt}") as blk:
                          @blk.tensor
                          def _(t):
                              for k in range(KH):
                                  mm = t.matmul(ps[0][:, 0:E + 1],
                                                xnT[:, k, tt * 128:(tt + 1) * 128],
                                                wg9T[:, k, :],
                                                start=(k == 0), stop=(k == KH - 1))
                              mm.then_inc(s_pe.h, 1)
                              for k in range(KH):
                                  mm = t.matmul(ps[1][:, 0:ERT],
                                                xnT[:, k, tt * 128:(tt + 1) * 128],
                                                awT[:, k, :],
                                                start=(k == 0), stop=(k == KH - 1))
                              mm.then_inc(s_pe.h, 1)
                              t.wait_ge(s_v.h, v0 + 10)
                              t.transpose(pst[:, 0:128], hw16[:, :], id16[:, :]
                                          ).then_inc(s_pe.h, 1)
                              s_pe.v = pe0 + 3

                          @blk.vector
                          def _(v):
                              v.wait_ge(s_pe.h, pe0 + 1)
                              edge(v, v.tensor_copy(gl[:, :], ps[0][:, 0:E + 1]))
                              v.tensor_reduce(gm[:, 0:1], gl[:, 0:E], AXL.X, ALU.max
                                              ).then_inc(s_v.h, 1)
                              v.wait_ge(s_v.h, v0 + 1)
                              v.tensor_scalar(gw[:, :], gl[:, 0:E], gm[:, 0:1], None,
                                              ALU.subtract).then_inc(s_v.h, 1)
                              v.wait_ge(s_sc.h, sc0 + 1)
                              v.tensor_reduce(gm[:, 1:2], gwe[:, :], AXL.X, ALU.add
                                              ).then_inc(s_v.h, 1)
                              v.wait_ge(s_v.h, v0 + 3)
                              v.reciprocal(gm[:, 2:3], gm[:, 1:2])
                              v.tensor_scalar(gm[:, 4:5], gm[:, 3:4], 1.0, None, ALU.add
                                              ).then_inc(s_v.h, 1)
                              v.wait_ge(s_v.h, v0 + 4)
                              v.reciprocal(gm[:, 5:6], gm[:, 4:5]).then_inc(s_v.h, 1)
                              v.wait_ge(s_v.h, v0 + 5)
                              v.tensor_scalar(gm[:, 6:7], gm[:, 5:6], MAX_THR, None,
                                              ALU.mult).then_inc(s_v.h, 1)
                              v.wait_ge(s_v.h, v0 + 6)
                              edge(v, v.tensor_scalar(gw3[:, :], gwe[:, :], gm[:, 2:3], gm[:, 6:7],
                                                      ALU.mult, ALU.subtract))
                              edge(v, v.tensor_scalar_max(gw[:, :], gw3[:, :], 0.0))
                              v.tensor_reduce(gm[:, 1:2], gw[:, :], AXL.X, ALU.add
                                              ).then_inc(s_v.h, 1)
                              v.wait_ge(s_v.h, v0 + 7)
                              edge(v, v.tensor_single_scalar(gm[:, 4:5], gm[:, 1:2], 0.0,
                                                             ALU.is_equal))
                              v.tensor_tensor(gm[:, 5:6], gm[:, 1:2], gm[:, 4:5], ALU.add
                                              ).then_inc(s_v.h, 1)
                              v.wait_ge(s_v.h, v0 + 8)
                              v.reciprocal(gm[:, 2:3], gm[:, 5:6]).then_inc(s_v.h, 1)
                              v.wait_ge(s_v.h, v0 + 9)
                              v.wait_ge(s_pe.h, pe0 + 2)
                              for e in range(E):
                                  hh16 = v.tensor_scalar(hw16[:, e * RR:(e + 1) * RR],
                                                         ps[1][:, e * RR:(e + 1) * RR],
                                                         gw[:, e:e + 1], gm[:, 2:3],
                                                         ALU.mult, ALU.mult)
                              hh16.then_inc(s_v.h, 1)
                              v.wait_ge(s_pe.h, pe0 + 3)
                              v.tensor_copy(hwT[:, tt * 128:(tt + 1) * 128], pst[:, 0:128])
                              s_v.v = v0 + 10

                          @blk.scalar
                          def _(s):
                              s.wait_ge(s_v.h, v0 + 2)
                              s.activation(gwe[:, :], gw[:, :], ACT.Exp)
                              s.activation(gm[:, 3:4], gl[:, E:E + 1], ACT.Exp,
                                           scale=-1.0, bias=bthr_ng[:, 0:1]
                                           ).then_inc(s_sc.h, 1)
                              s_sc.v = sc0 + 1

              stop_after(9)
              with (
                  nc.sbuf_tensor("wb_raw", [128, 4, H], F16) as wb_raw,
                  nc.sbuf_tensor("wslab", [128, KH, 512], F16) as wslab,
                  nc.sbuf_tensor("gstage", [128, 2, 512], F16) as gstage,
              ):
                  hgsp_prev = None
                  for oc in range(H // 512):
                      v0, g0, pe0, sc0 = s_v.v, s_g.v, s_pe.v, s_sc.v
                      NTR = 4 * KH
                      hwb = {}
                      hgsp = {}
                      hgsp_prev_cap = hgsp_prev
                      with nc.Block(f"p5b_{oc}") as blk:
                          @blk.gpsimd
                          def _(g):
                              for osub in range(4):
                                  hwb[osub] = R_g.issue(
                                      g, wb_raw[:, osub, :],
                                      wbase_in[oc * 512 + osub * 128:
                                               oc * 512 + (osub + 1) * 128, :])

                          @blk.sync
                          def _(sync):
                              for tt in range(NT):
                                  sync.wait_ge(s_sc.h, sc0 + tt + 1)
                                  hgsp[tt] = R_ld.issue(
                                      sync, g_sp[tt * 128:(tt + 1) * 128,
                                                 oc * 512:(oc + 1) * 512],
                                      gstage[:, tt % 2, :])

                          @blk.tensor
                          def _(t):
                              for osub in range(4):
                                  t.wait_ge(*hwb[osub])
                              for jj in range(NTR):
                                  osub, k = divmod(jj, KH)
                                  if jj >= 4:
                                      t.wait_ge(s_v.h, v0 + jj - 3)
                                  sl = jj % 4
                                  t.transpose(ps[2 + sl][:, :].bitcast(F16)[:, 0:128],
                                              wb_raw[:, osub, k * 128:(k + 1) * 128],
                                              id16[:, :]).then_inc(s_pe.h, 1)
                              for tt in range(NT):
                                  t.wait_ge(s_v.h, v0 + NTR)
                                  if tt >= 2:
                                      t.wait_ge(s_sc.h, sc0 + tt - 1)
                                  bank = tt % 2
                                  for k in range(KH):
                                      t.matmul(ps[bank][:, :],
                                               xnT[:, k, tt * 128:(tt + 1) * 128],
                                               wslab[:, k, :],
                                               start=(k == 0), stop=False)
                                  t.matmul(ps[bank][:, :], id16[:, :],
                                           bbase_sb[:, oc * 512:(oc + 1) * 512],
                                           start=False, stop=False)
                                  t.matmul(ps[bank][:, :],
                                           hwT[:, tt * 128:(tt + 1) * 128],
                                           bwT[:, oc * 512:(oc + 1) * 512],
                                           start=False, stop=True).then_inc(s_pe.h, 1)
                              s_pe.v = pe0 + NTR + NT

                          @blk.vector
                          def _(v):
                              for jj in range(NTR):
                                  osub, k = divmod(jj, KH)
                                  v.wait_ge(s_pe.h, pe0 + jj + 1)
                                  sl = jj % 4
                                  v.tensor_copy(wslab[:, k, osub * 128:(osub + 1) * 128],
                                                ps[2 + sl][:, :].bitcast(F16)[:, 0:128]
                                                ).then_inc(s_v.h, 1)
                              s_v.v = v0 + NTR

                          @blk.scalar
                          def _(s):
                              if hgsp_prev_cap is not None:
                                  for hh in list(hgsp_prev_cap.values())[-2:]:
                                      s.wait_ge(*hh)
                              for tt in range(NT):
                                  s.wait_ge(s_pe.h, pe0 + NTR + tt + 1)
                                  if tt >= 2:
                                      s.wait_ge(*hgsp[tt - 2])
                                  s.activation(gstage[:, tt % 2, :], ps[tt % 2][:, :],
                                               ACT.Silu).then_inc(s_sc.h, 1)
                              s_sc.v = sc0 + NT
                      hgsp_prev = hgsp

          stop_after(10)
          ctx6 = contextlib.ExitStack()
          _e6 = ctx6.enter_context
          dtrT_b = _e6(nc.sbuf_tensor("dtrT_b", [128, RC, LB], F16))
          bcT_raw = _e6(nc.sbuf_tensor("bcT_raw", [128, LB], F16))
          bcT_b = _e6(nc.sbuf_tensor("bcT_b", [128, LB], F16))
          st_sb = _e6(nc.sbuf_tensor("st_sb", [128, NDT * N], F32))
          dt_sb = _e6(nc.sbuf_tensor("dt_sb", [128, NDT, CW], F16))
          dtx_sb = _e6(nc.sbuf_tensor("dtx_sb", [128, NDT, CW], F16))
          xc_sb = _e6(nc.sbuf_tensor("xc_sb", [128, NDT, CW], F16))
          ys_sb = _e6(nc.sbuf_tensor("ys_sb", [128, 2, NDT, CW], F16))
          bbc = _e6(nc.sbuf_tensor("bbc", [128, CW], F16))
          cbc = _e6(nc.sbuf_tensor("cbc", [128, CW], F16))
          dabuf = _e6(nc.sbuf_tensor("dabuf", [128, 2, CW], F16))
          dbxbuf = _e6(nc.sbuf_tensor("dbxbuf", [128, CW], F16))
          hbuf = _e6(nc.sbuf_tensor("hbuf", [128, CW], F16))
          tmpbuf = _e6(nc.sbuf_tensor("tmpbuf", [128, CW], F16))
          ybuf = _e6(nc.sbuf_tensor("ybuf", [128, 2, CW], F16))
          sp_a = _e6(nc.sbuf_tensor("sp_a", [128, 512], F32))
          sp_b = _e6(nc.sbuf_tensor("sp_b", [128, 512], F32))
          sp_u = _e6(nc.sbuf_tensor("sp_u", [128, 512], F32))
          sp_d = _e6(nc.sbuf_tensor("sp_d", [128, 512], F32))
          sp_e = _e6(nc.sbuf_tensor("sp_e", [128, 512], F32))
          with ctx6:
              hy6_prev = {}
              for b in range(NB):
                  v0 = s_v.v
                  h6l = []
                  with nc.Block(f"p6load_{b}") as blk:
                      @blk.sync
                      def _(sync):
                          for j in range(CPB):
                              src = dtr_all[b * CPB + j, :, :]
                              h6l.append(R_ld.issue(sync, dtrT_b[:, 0, j * TPC:(j + 1) * TPC],
                                                    src[0:128, :]))
                              h6l.append(R_ld.issue(sync, dtrT_b[:, 1, j * TPC:(j + 1) * TPC],
                                                    src[128:256, :]))
                              h6l.append(R_ld.issue(sync, bcT_raw[0:BCW, j * TPC:(j + 1) * TPC],
                                                    src[256:W288, :]))

                      @blk.vector
                      def _(v):
                          if b == 0:
                              edge(v, v.memset(bcT_raw[32:64, :], 0.0))
                              edge(v, v.memset(bcT_raw[64:128, :], 0.0))
                          for hh in h6l:
                              v.wait_ge(*hh)
                          v.tensor_scalar(bcT_b[:, :], bcT_raw[:, :],
                                          bclnw_sb[:, 0:1], None, ALU.mult)
                          v.memset(st_sb[:, :], 0.0)
                          s_v.v = v0

                  for tc in range(NTC):
                      v0, pe0, sc0 = s_v.v, s_pe.v, s_sc.v
                      hxc6 = {}
                      hy6 = {}
                      hy6_prev_cap = dict(hy6_prev)
                      gtc = b * CPB + (tc * CW) // TPC
                      toff = b * LB + tc * CW
                      VS1 = NDT * (C6 * 2 + 1)
                      VPN = 1 + NDT
                      VSN = N * VPN

                      def scanmark(kk):
                          return (kk // NDT) * VPN + 1 + (kk % NDT) + 1

                      with nc.Block(f"p6_{b}_{tc}") as blk:
                          @blk.sync
                          def _(sync):
                              for i in range(NDT):
                                  hxc6[i] = R_ld.issue(
                                      sync, xc_sb[:, i, :],
                                      xc_sp[i * 128:(i + 1) * 128, toff:toff + CW])
                              for i in range(NDT):
                                  sync.wait_ge(s_v.h, v0 + VS1 + VSN + i + 1)
                                  hy6[i] = R_ld.issue(
                                      sync,
                                      y_sp[gtc, i * 128:(i + 1) * 128,
                                           (tc * CW) % TPC:(tc * CW) % TPC + CW],
                                      ybuf[:, i % 2, :])

                          @blk.tensor
                          def _(t):
                              for i in range(NDT):
                                  for th in range(C6):
                                      if i >= 1:
                                          t.wait_ge(s_sc.h, sc0 + 2 * ((i - 1) * C6 + th) + 1)
                                      for rc in range(RC):
                                          mm = t.matmul(ps[th][:, :],
                                                        dtprojT[:, rc, i, :],
                                                        dtrT_b[:, rc, tc * CW + th * 512:
                                                               tc * CW + (th + 1) * 512],
                                                        start=(rc == 0), stop=(rc == RC - 1))
                                      mm.then_inc(s_pe.h, 1)
                              for n in range(N):
                                  if n >= 1:
                                      t.wait_ge(s_v.h, v0 + VS1 + (n - 1) * VPN + 1)
                                  for th in range(C6):
                                      rhs6 = bcT_b[:, tc * CW + th * 512:
                                                   tc * CW + (th + 1) * 512]
                                      t.matmul(ps[2 + th][:, :],
                                               selmat[:, n * 128:(n + 1) * 128],
                                               rhs6, start=True, stop=True)
                                      mm = t.matmul(ps[4 + th][:, :],
                                                    selmat[:, (N + n) * 128:(N + n + 1) * 128],
                                                    rhs6, start=True, stop=True)
                                  mm.then_inc(s_pe.h, 1)
                              s_pe.v = pe0 + NDT * C6 + N

                          @blk.scalar
                          def _(s):
                              for i in range(NDT):
                                  for th in range(C6):
                                      kk1 = i * C6 + th
                                      s.wait_ge(s_pe.h, pe0 + kk1 + 1)
                                      if kk1 >= 1:
                                          pi, pth = (i, th - 1) if th >= 1 else (i - 1, C6 - 1)
                                          s.wait_ge(s_v.h, v0 + pi * (2 * C6 + 1) + 2 * pth + 2)
                                      s.activation(sp_a[:, :], ps[th][:, :], ACT.Relu,
                                                   bias=dtbias_sb[:, i:i + 1])
                                      s.activation(sp_b[:, :], ps[th][:, :], ACT.Relu,
                                                   scale=-1.0, bias=dtbias_ng[:, i:i + 1]
                                                   ).then_inc(s_sc.h, 1)
                                      s.wait_ge(s_v.h, v0 + i * (2 * C6 + 1) + 2 * th + 1)
                                      edge(s, s.activation(sp_d[:, :], sp_u[:, :], ACT.Exp,
                                                           scale=-1.0))
                                      s.activation(sp_e[:, :], sp_d[:, :], ACT.Ln,
                                                   bias=1.0).then_inc(s_sc.h, 1)
                              for n in range(N):
                                  for i in range(NDT):
                                      kk = n * NDT + i
                                      s.wait_ge(s_v.h, v0 + i * (2 * C6 + 1) + 2 * C6)
                                      if kk >= 2:
                                          s.wait_ge(s_v.h,
                                                    v0 + VS1 + scanmark(kk - 2))
                                      s.activation(dabuf[:, kk % 2, :], dt_sb[:, i, :],
                                                   ACT.Exp,
                                                   scale=a_sb[:, i * N + n:i * N + n + 1]
                                                   ).then_inc(s_sc.h, 1)
                              s_sc.v = sc0 + NDT * C6 * 2 + N * NDT

                          @blk.vector
                          def _(v):
                              for i in range(NDT):
                                  for th in range(C6):
                                      v.wait_ge(s_sc.h, sc0 + 2 * (i * C6 + th) + 1)
                                      v.tensor_tensor(sp_u[:, :], sp_a[:, :], sp_b[:, :],
                                                      ALU.add).then_inc(s_v.h, 1)
                                      v.wait_ge(s_sc.h, sc0 + 2 * (i * C6 + th) + 2)
                                      v.wait_ge(s_v.h, v0 + i * (2 * C6 + 1) + 2 * th + 1)
                                      v.tensor_tensor(dt_sb[:, i, th * 512:(th + 1) * 512],
                                                      sp_a[:, :], sp_e[:, :], ALU.add
                                                      ).then_inc(s_v.h, 1)
                                  v.wait_ge(*hxc6[i])
                                  v.wait_ge(s_v.h, v0 + i * (2 * C6 + 1) + 2 * C6)
                                  v.tensor_tensor(dtx_sb[:, i, :], dt_sb[:, i, :],
                                                  xc_sb[:, i, :], ALU.mult
                                                  ).then_inc(s_v.h, 1)
                              for n in range(N):
                                  v.wait_ge(s_pe.h, pe0 + NDT * C6 + n + 1)
                                  if n >= 1:
                                      v.wait_ge(s_v.h, v0 + VS1 + n * VPN)
                                  for th in range(C6):
                                      v.tensor_copy(bbc[:, th * 512:(th + 1) * 512],
                                                    ps[2 + th][:, :])
                                      cpop = v.tensor_copy(cbc[:, th * 512:(th + 1) * 512],
                                                           ps[4 + th][:, :])
                                  cpop.then_inc(s_v.h, 1)
                                  for i in range(NDT):
                                      kk = n * NDT + i
                                      v.wait_ge(s_sc.h,
                                                sc0 + NDT * C6 * 2 + kk + 1)
                                      v.wait_ge(s_v.h, v0 + VS1 + n * VPN + 1)
                                      edge(v, v.tensor_tensor(dbxbuf[:, :], dtx_sb[:, i, :],
                                                              bbc[:, :], ALU.mult))
                                      edge(v, v.tensor_tensor_scan(hbuf[:, :], dabuf[:, kk % 2, :],
                                                                   dbxbuf[:, :],
                                                                   st_sb[:, i * N + n:i * N + n + 1],
                                                                   ALU.mult, ALU.add))
                                      v.tensor_copy(st_sb[:, i * N + n:i * N + n + 1],
                                                    hbuf[:, CW - 1:CW])
                                      if n == 0:
                                          v.tensor_tensor(ys_sb[:, 0, i, :], hbuf[:, :],
                                                          cbc[:, :], ALU.mult
                                                          ).then_inc(s_v.h, 1)
                                      else:
                                          edge(v, v.tensor_tensor(tmpbuf[:, :], hbuf[:, :],
                                                                  cbc[:, :], ALU.mult))
                                          v.wait_ge(s_v.h,
                                                    v0 + VS1 + (n - 1) * VPN + 1 + i + 1)
                                          v.tensor_tensor(ys_sb[:, n % 2, i, :],
                                                          ys_sb[:, (n - 1) % 2, i, :],
                                                          tmpbuf[:, :], ALU.add
                                                          ).then_inc(s_v.h, 1)
                              for i in range(NDT):
                                  v.wait_ge(s_v.h, v0 + VS1 + (N - 1) * VPN + 1 + i + 1)
                                  if i >= 2:
                                      v.wait_ge(*hy6[i - 2])
                                  elif i in hy6_prev_cap:
                                      v.wait_ge(*hy6_prev_cap[i])
                                  v.scalar_tensor_tensor(ybuf[:, i % 2, :],
                                                         xc_sb[:, i, :],
                                                         dparam_sb[:, i:i + 1],
                                                         ys_sb[:, (N - 1) % 2, i, :],
                                                         ALU.mult, ALU.add
                                                         ).then_inc(s_v.h, 1)
                              s_v.v = v0 + VS1 + VSN + NDT
                      hy6_prev = {}
                      for k in range(min(2, NDT)):
                          ii = NDT - 1 - k
                          hy6_prev[ii % 2] = hy6[ii]

              stop_after(11)
              with nc.Block("coll4") as blk:
                  @blk.gpsimd
                  def _(g):
                      R_ld.wait_all(g)
                      with allow_ici_all_to_all():
                          g.collective_compute(
                              "AllToAll", ALU.bypass, replica_groups=RG,
                              ins=[y_sp.ap().opt()], outs=[y_own.ap().opt()],
                          ).then_inc(s_cc.h, 1)
                      s_cc.v += 1
                      g.wait_ge(s_cc.h, s_cc.v)

          stop_after(12)
          with nc.sbuf_tensor("y_sb", [128, NDG, TPC], F16) as y_sb:
            with (
              nc.sbuf_tensor("yt7", [128, 2, TPC], F16) as yt7,
              nc.sbuf_tensor("zt7", [128, 2, TPC], F16) as zt7,
              nc.sbuf_tensor("zs7", [128, 2, TPC], F16) as zs7,
            ):
              v0, sc0 = s_v.v, s_sc.v
              h7 = {}
              with nc.Block("p7a") as blk:
                  @blk.sync
                  def _(sync):
                      for dtg in range(NDG):
                          if dtg >= 2:
                              sync.wait_ge(s_v.h, v0 + dtg - 1)
                          rsl, roff = divmod(dtg, NDT)
                          hy = R_ld.issue(sync, yt7[:, dtg % 2, :],
                                          y_own[rsl, roff * 128:(roff + 1) * 128, :])
                          hz = R_ld.issue(sync, zt7[:, dtg % 2, :],
                                          z_sp[dtg * 128:(dtg + 1) * 128, :])
                          h7[dtg] = (hy, hz)

                  @blk.scalar
                  def _(s):
                      for dtg in range(NDG):
                          s.wait_ge(*h7[dtg][0])
                          s.wait_ge(*h7[dtg][1])
                          if dtg >= 2:
                              s.wait_ge(s_v.h, v0 + dtg - 1)
                          s.activation(zs7[:, dtg % 2, :], zt7[:, dtg % 2, :],
                                       ACT.Silu).then_inc(s_sc.h, 1)
                      s_sc.v = sc0 + NDG

                  @blk.vector
                  def _(v):
                      for dtg in range(NDG):
                          v.wait_ge(s_sc.h, sc0 + dtg + 1)
                          v.tensor_tensor(y_sb[:, dtg, :], yt7[:, dtg % 2, :],
                                          zs7[:, dtg % 2, :], ALU.mult
                                          ).then_inc(s_v.h, 1)
                      s_v.v = v0 + NDG

            with (
                  nc.sbuf_tensor("wo_ld", [128, DIN // 2], F16) as wo_ld,
                  nc.sbuf_tensor("woutT", [128, NDG, HCH], F16) as woutT,
                  nc.sbuf_tensor("gtile", [128, 2, HCH], F16) as gtile,
                  nc.sbuf_tensor("xtile", [128, 2, HCH], F32) as xtile,
                  nc.sbuf_tensor("sgbuf", [128, HCH], F32) as sgbuf,
                  nc.sbuf_tensor("otile", [128, 2, HCH], F32) as otile,
            ):
                  hout_prev = None
                  for hc in range(NHC):
                      v0, g0, pe0, st0 = s_v.v, s_g.v, s_pe.v, s_st.v
                      HR = HCH // 128
                      NTR = HR * NDG
                      hwo = {}
                      hgx = {}
                      hout = {}
                      hout_prev_cap = hout_prev
                      with nc.Block(f"p7b_{hc}") as blk:
                          NDH = NDG // 2

                          @blk.gpsimd
                          def _(g):
                              for ch in range(2 * HR):
                                  hr, dh = divmod(ch, 2)
                                  if ch >= 1:
                                      g.wait_ge(s_pe.h, pe0 + ch * NDH)
                                  hwo[ch] = R_g.issue(
                                      g, wo_ld[:, :],
                                      wout_in[hc * HCH + hr * 128:
                                              hc * HCH + (hr + 1) * 128,
                                              dh * (DIN // 2):(dh + 1) * (DIN // 2)])

                          @blk.sync
                          def _(sync):
                              for tt in range(NT):
                                  if tt >= 2:
                                      sync.wait_ge(s_v.h, v0 + NTR + 2 * (tt - 1))
                                  hg = R_ld.issue(sync, gtile[:, tt % 2, :],
                                                  g_sp[tt * 128:(tt + 1) * 128,
                                                       hc * HCH:(hc + 1) * HCH])
                                  hx = R_ld.issue(sync, xtile[:, tt % 2, :],
                                                  x_in[tt * 128:(tt + 1) * 128,
                                                       hc * HCH:(hc + 1) * HCH])
                                  hgx[tt] = (hg, hx)
                                  sync.wait_ge(s_v.h, v0 + NTR + 2 * (tt + 1))
                                  hout[tt] = R_st.issue(
                                      sync, out_ext[tt * 128:(tt + 1) * 128,
                                                    hc * HCH:(hc + 1) * HCH],
                                      otile[:, tt % 2, :])

                          @blk.tensor
                          def _(t):
                              for jj in range(NTR):
                                  hr, dtg = divmod(jj, NDG)
                                  ch = hr * 2 + dtg // NDH
                                  t.wait_ge(*hwo[ch])
                                  if jj >= 4:
                                      t.wait_ge(s_v.h, v0 + jj - 3)
                                  sl = jj % 4
                                  t.transpose(ps[2 + sl][:, :].bitcast(F16)[:, 0:128],
                                              wo_ld[:, (dtg % NDH) * 128:
                                                    (dtg % NDH + 1) * 128],
                                              id16[:, :]).then_inc(s_pe.h, 1)
                              for tt in range(NT):
                                  t.wait_ge(s_v.h, v0 + NTR)
                                  if tt >= 2:
                                      t.wait_ge(s_v.h, v0 + NTR + 2 * (tt - 1))
                                  bank = tt % 2
                                  for dtg in range(NDG):
                                      mm = t.matmul(ps[bank][:, 0:HCH],
                                                    y_sb[:, dtg, tt * 128:(tt + 1) * 128],
                                                    woutT[:, dtg, :],
                                                    start=(dtg == 0), stop=(dtg == NDG - 1))
                                  mm.then_inc(s_pe.h, 1)
                              s_pe.v = pe0 + NTR + NT

                          @blk.vector
                          def _(v):
                              for jj in range(NTR):
                                  hr, dtg = divmod(jj, NDG)
                                  v.wait_ge(s_pe.h, pe0 + jj + 1)
                                  sl = jj % 4
                                  v.tensor_copy(woutT[:, dtg, hr * 128:(hr + 1) * 128],
                                                ps[2 + sl][:, :].bitcast(F16)[:, 0:128]
                                                ).then_inc(s_v.h, 1)
                              for tt in range(NT):
                                  v.wait_ge(*hgx[tt][0])
                                  v.wait_ge(*hgx[tt][1])
                                  v.wait_ge(s_pe.h, pe0 + NTR + tt + 1)
                                  if tt >= 2:
                                      v.wait_ge(*hout[tt - 2])
                                  elif hout_prev_cap is not None:
                                      v.wait_ge(*hout_prev_cap[NT - 2 + tt])
                                  if tt >= 1:
                                      v.wait_ge(s_v.h, v0 + NTR + 2 * tt)
                                  v.tensor_tensor(sgbuf[:, :], ps[tt % 2][:, 0:HCH],
                                                  gtile[:, tt % 2, :], ALU.mult
                                                  ).then_inc(s_v.h, 1)
                                  v.wait_ge(s_v.h, v0 + NTR + 2 * tt + 1)
                                  v.tensor_tensor(otile[:, tt % 2, :], sgbuf[:, :],
                                                  xtile[:, tt % 2, :], ALU.add
                                                  ).then_inc(s_v.h, 1)
                              s_v.v = v0 + NTR + 2 * NT
                      hout_prev = hout

          with nc.Block("fin") as blk:
              @blk.sync
              def _(sync):
                  R_st.wait_all(sync)
                  R_ld.wait_all(sync)

      except _Stop:
          pass
    return nc


def _p6_marks(cfg):
    pass


def host_arrange(cfg, inputs, c):
    d0, d1 = c * cfg.DSL, (c + 1) * cfg.DSL
    NDT, N, KC, RC, KH, E, RR = (cfg.NDT, cfg.N, cfg.KC, cfg.RC, cfg.KH,
                                 cfg.E, cfg.RR)

    def dmaj(a):
        return np.ascontiguousarray(
            a.reshape(NDT, 128, *a.shape[1:]).transpose(
                1, 0, *range(2, a.ndim + 1)))

    out = {}
    out["conv_w_sl"] = dmaj(inputs["L0_attn_conv_w"][d0:d1])
    out["conv_b_sl"] = dmaj(inputs["L0_attn_conv_b"][d0:d1])
    out["D_sl"] = dmaj(inputs["L0_attn_D_param"][d0:d1])
    out["dt_bias_sl"] = dmaj(inputs["L0_attn_dt_bias"][d0:d1])
    out["A_log_sl"] = np.ascontiguousarray(
        dmaj(inputs["L0_attn_A_log"][d0:d1]).reshape(128, NDT * N))
    out["dt_proj_sl"] = dmaj(inputs["L0_attn_dt_proj_w"][d0:d1])
    xp = inputs["L0_attn_x_proj_w"][:, d0:d1]
    xpr = np.zeros((3, 128, cfg.DSL), dtype=np.float32)
    xpr[0] = xp[0:128]
    xpr[1] = xp[128:256]
    xpr[2, 0:2 * N] = xp[256:256 + 2 * N]
    out["x_proj_sl"] = np.ascontiguousarray(xpr.transpose(1, 0, 2))
    out["dt_ln_w"] = np.ascontiguousarray(
        inputs["L0_attn_dt_ln_w"].reshape(RC, 128).T)
    out["W_thr"] = np.ascontiguousarray(
        inputs["L0_ffn_W_thr"].reshape(KH, 128).T)
    out["A_w"] = np.ascontiguousarray(
        inputs["L0_ffn_A_w"].reshape(E * RR, KH, 128))
    out["B_w"] = np.ascontiguousarray(
        inputs["L0_ffn_B_w"].reshape(E, KH, 128, RR).transpose(2, 1, 0, 3))
    return out


def prep_core_inputs(cfg, inputs, c, x2d, bcln, aw2d):
    t0, t1 = c * cfg.TPC, (c + 1) * cfg.TPC
    m = {
        "x": x2d[t0:t1],
        "in_proj_w": inputs["L0_attn_in_proj_w"],
        "out_proj_w": inputs["L0_attn_out_proj_w"],
        "bc_ln_w": np.ascontiguousarray(np.concatenate(
            [bcln, np.zeros(128 - bcln.shape[0], dtype=np.float32)])),
        "W_base": inputs["L0_ffn_W_base"],
        "b_base": np.ascontiguousarray(np.broadcast_to(
            inputs["L0_ffn_b_base"], (128, inputs["L0_ffn_b_base"].shape[0]))),
        "W_gate": np.ascontiguousarray(np.concatenate(
            [inputs["L0_ffn_W_gate"],
             np.zeros((128 - inputs["L0_ffn_W_gate"].shape[0],
                       inputs["L0_ffn_W_gate"].shape[1]), dtype=np.float32)],
            axis=0)),
        "b_thr": np.ascontiguousarray(
            np.broadcast_to(inputs["L0_ffn_b_thr"].reshape(1), (128,))),
    }
    m.update(host_arrange(cfg, inputs, c))
    return m


def kernel(**inputs):
    cfg = CFG_FULL
    hs = inputs["L0_attn_hidden_states"]
    x2d = np.ascontiguousarray(hs.reshape(cfg.NTOK, cfg.H))
    bcln = np.ascontiguousarray(
        np.concatenate([inputs["L0_attn_B_ln_w"], inputs["L0_attn_C_ln_w"]]))
    aw2d = np.ascontiguousarray(
        inputs["L0_ffn_A_w"].reshape(cfg.E * cfg.RR, cfg.H))

    nc = build_nc(cfg)

    in_maps = []
    for c in range(NCORES):
        t0, t1 = c * cfg.TPC, (c + 1) * cfg.TPC
        d0, d1 = c * cfg.DSL, (c + 1) * cfg.DSL
        in_maps.append(prep_core_inputs(cfg, inputs, c, x2d, bcln, aw2d))

    res = run_bass_kernel_spmd(nc, in_maps, core_ids=list(range(NCORES)))
    out = np.concatenate([r["out"] for r in res.results], axis=0)
    return out.reshape(cfg.NB, cfg.LB, cfg.H).astype(np.float32)


# version 06
# speedup vs baseline: 1.0240x; 1.0240x over previous
import contextlib
import os
import numpy as np
import concourse.bass as bass
import concourse.mybir as mybir
from concourse.bass_utils import run_bass_kernel_spmd
from concourse.aot_env import allow_ici_all_to_all

F32 = mybir.dt.float32
F16 = mybir.dt.float16
ALU = mybir.AluOpType
ACT = mybir.ActivationFunctionType
AXL = mybir.AxisListType

NCORES = 8
EPS_LN = 1e-5
EPS_RMS = 1e-6
MAX_THR = 0.125


class Cfg:
    def __init__(self, H=4096, DIN=8192, NTOK=8192, R=256, N=16, E=8, RR=16,
                 KC=4, NB=2):
        self.H, self.DIN, self.NTOK, self.R, self.N = H, DIN, NTOK, R, N
        self.E, self.RR, self.KC, self.NB = E, RR, KC, NB
        self.TPC = NTOK // NCORES
        self.DSL = DIN // NCORES
        self.LB = NTOK // NB
        self.NT = self.TPC // 128
        self.NDT = self.DSL // 128
        self.KH = H // 128
        self.CPB = self.LB // self.TPC
        self.NCOL = 2 * DIN
        self.ERT = E * RR
        self.RC = R // 128
        self.BCW = 2 * N
        self.TH = self.TPC // 512
        self.CW = min(1024, self.TPC)
        self.NTC = self.LB // self.CW
        self.C6 = self.CW // 512
        self.HCH = 256
        self.NHC = H // self.HCH
        self.NDG = DIN // 128
        assert self.TPC % 512 == 0 and self.DSL % 128 == 0 and self.ERT == 128


CFG_FULL = Cfg()


def build_nc(cfg: Cfg):
    H, DIN, R, N, E, RR, KC = cfg.H, cfg.DIN, cfg.R, cfg.N, cfg.E, cfg.RR, cfg.KC
    TPC, DSL, NT, NDT, KH = cfg.TPC, cfg.DSL, cfg.NT, cfg.NDT, cfg.KH
    NB, CPB, NCOL, ERT, RC, BCW = cfg.NB, cfg.CPB, cfg.NCOL, cfg.ERT, cfg.RC, cfg.BCW
    TH, CW, NTC, C6, HCH, NHC = cfg.TH, cfg.CW, cfg.NTC, cfg.C6, cfg.HCH, cfg.NHC
    NTOK, LB, NDG = cfg.NTOK, cfg.LB, cfg.NDG
    W288 = R + 2 * N
    NCC = NCOL // 512
    EW3 = [128, 128, BCW]

    nc = bass.Bass("TRN2", target_bir_lowering=False, debug=False)

    x_in = nc.dram_tensor("x", [TPC, H], F32, kind="ExternalInput")
    win_in = nc.dram_tensor("in_proj_w", [H, NCOL], F32, kind="ExternalInput")
    convw_in = nc.dram_tensor("conv_w_sl", [128, NDT, KC], F32, kind="ExternalInput")
    convb_in = nc.dram_tensor("conv_b_sl", [128, NDT], F32, kind="ExternalInput")
    xproj_in = nc.dram_tensor("x_proj_sl", [128, 3, DSL], F32, kind="ExternalInput")
    dtproj_in = nc.dram_tensor("dt_proj_sl", [128, NDT, R], F32, kind="ExternalInput")
    dtbias_in = nc.dram_tensor("dt_bias_sl", [128, NDT], F32, kind="ExternalInput")
    alog_in = nc.dram_tensor("A_log_sl", [128, NDT * N], F32, kind="ExternalInput")
    dparam_in = nc.dram_tensor("D_sl", [128, NDT], F32, kind="ExternalInput")
    wout_in = nc.dram_tensor("out_proj_w", [DIN, H], F32, kind="ExternalInput")
    dtlnw_in = nc.dram_tensor("dt_ln_w", [128, RC], F32, kind="ExternalInput")
    bclnw_in = nc.dram_tensor("bc_ln_w", [128], F32, kind="ExternalInput")
    wbase_in = nc.dram_tensor("W_base", [H, H], F32, kind="ExternalInput")
    bbase_in = nc.dram_tensor("b_base", [128, H], F32, kind="ExternalInput")
    wgate_in = nc.dram_tensor("W_gate", [128, H], F32, kind="ExternalInput")
    wthr_in = nc.dram_tensor("W_thr", [128, KH], F32, kind="ExternalInput")
    bthr_in = nc.dram_tensor("b_thr", [128], F32, kind="ExternalInput")
    aw_in = nc.dram_tensor("A_w", [128, KH, 128], F32, kind="ExternalInput")
    bw_in = nc.dram_tensor("B_w", [128, KH, E, RR], F32, kind="ExternalInput")
    out_ext = nc.dram_tensor("out", [TPC, H], F32, kind="ExternalOutput")

    xh_sp = nc.dram_tensor("xh_sp", [DIN, TPC], F16, kind="Internal")
    z_sp = nc.dram_tensor("z_sp", [DIN, TPC], F16, kind="Internal")
    xh_scan = nc.dram_tensor("xh_scan", [NCORES, DSL, TPC], F16, kind="Internal")
    xc_sp = nc.dram_tensor("xc_sp", [DSL, NTOK], F16, kind="Internal")
    ssm_sp = nc.dram_tensor("ssm_sp", [NTOK, W288], F32, kind="Internal")
    ssm_red = nc.dram_tensor("ssm_red", [TPC, W288], F32, kind="Internal")
    dtrT_loc = nc.dram_tensor("dtrT_loc", [W288, TPC], F16, kind="Internal")
    dtr_all = nc.dram_tensor("dtr_all", [NCORES, W288, TPC], F16, kind="Internal")
    g_sp = nc.dram_tensor("g_sp", [TPC, H], F16, kind="Internal")
    y_sp = nc.dram_tensor("y_sp", [NCORES, DSL, TPC], F16, kind="Internal")
    y_own = nc.dram_tensor("y_own", [NCORES, DSL, TPC], F16, kind="Internal")

    ident16 = nc.inline_tensor(np.eye(128, dtype=np.float16), name="ident16")
    selm = np.zeros((128, BCW * 128), dtype=np.float16)
    for _n in range(BCW):
        selm[_n, _n * 128:(_n + 1) * 128] = 1.0
    selmat_c = nc.inline_tensor(selm, name="selmat_c")
    ident32 = nc.inline_tensor(np.eye(128, dtype=np.float32), name="ident32")

    A = nc.alloc_sbuf_tensor
    id16 = A("id16", [128, 128], F16)
    id32 = A("id32", [128, 128], F32)
    bthr_bc = A("bthr_bc", [128, 1], F32)
    convw_sb = A("convw_sb", [128, NDT, KC], F32)
    convb_sb = A("convb_sb", [128, NDT], F32)
    dparam_sb = A("dparam_sb", [128, NDT], F32)
    dtbias_sb = A("dtbias_sb", [128, NDT], F32)
    dtbias_ng = A("dtbias_ng", [128, NDT], F32)
    a_sb = A("a_sb", [128, NDT * N], F32)
    xprojT = A("xprojT", [128, NDT, W288], F16)
    dtprojT = A("dtprojT", [128, RC, NDT, 128], F16)
    dtlnw_sb = A("dtlnw_sb", [128, RC], F32)
    bclnw_sb = A("bclnw_sb", [128, 1], F32)
    bthr_ng = A("bthr_ng", [128, 1], F32)
    selmat = A("selmat", [128, BCW * 128], F16)

    ctx = contextlib.ExitStack()

    class Sem:
        def __init__(self, name):
            self.h = ctx.enter_context(nc.semaphore(name))
            self.v = 0

    const_vals = [EPS_LN, EPS_RMS, -0.5, -1.0, 1.0 / R, 1.0 / N]
    const_tensors = []
    for ci, cv in enumerate(const_vals):
        cta = A(f"cst{ci}", [128, 1], F32)
        const_tensors.append((cta, float(cv)))
        nc.const_aps.register(F32, float(cv), cta[:, :])

    s_ld = Sem("s_ld")
    s_g = Sem("s_g")
    s_pe = Sem("s_pe")
    s_v = Sem("s_v")
    s_sc = Sem("s_sc")
    s_st = Sem("s_st")
    s_cc = Sem("s_cc")
    s_xv = Sem("s_xv")
    s_xs = Sem("s_xs")

    class Ring:

        def __init__(self, name, n):
            self.sems = [Sem(f"{name}{i}") for i in range(n)]
            self.n = n
            self.i = 0

        def issue(self, eng, out, in_, **kw):
            s = self.sems[self.i % self.n]
            if s.v > 0:
                eng.wait_ge(s.h, s.v)
            eng.dma_start(out=out, in_=in_, **kw).then_inc(s.h, 16)
            s.v += 16
            self.i += 1
            return (s.h, s.v)

        def wait_all(self, eng):
            for s in self.sems:
                if s.v > 0:
                    eng.wait_ge(s.h, s.v)

    R_ld = Ring("rld", 8)
    R_g = Ring("rg", 4)
    R_st = Ring("rst", 4)

    def edge(eng, instr):
        sx = s_xs if eng.engine == mybir.EngineType.Activation else s_xv
        sx.v += 1
        instr.then_inc(sx.h, 1)
        eng.wait_ge(sx.h, sx.v)

    GQ = 4

    def gdma(g, out, in_, **kw):
        if s_g.v >= GQ * 16:
            g.wait_ge(s_g.h, s_g.v - (GQ - 1) * 16)
        g.dma_start(out=out, in_=in_, **kw).then_inc(s_g.h, 16)
        s_g.v += 16

    ps = [ctx.enter_context(nc.psum_tensor(f"ps{i}", [128, 512], F32))
          for i in range(7)]
    pst = ctx.enter_context(nc.psum_tensor("pst", [128, 512], F16))

    RG = [list(range(NCORES))]

    KPH = int(os.environ.get("KPH", "99"))

    class _Stop(Exception):
        pass

    def stop_after(n):
        if KPH <= n:
            raise _Stop()

    with ctx:
      try:
          with nc.Block("p0_load") as blk:
              @blk.vector
              def _(v):
                  for cta, cv in const_tensors:
                      v.memset(cta[:, :], cv)

              @blk.sync
              def _(sync):
                  for dst, src in [
                      (id16[:, :], ident16[:, :]),
                      (id32[:, :], ident32[:, :]),
                      (bthr_bc[:, :], bthr_in[:].unsqueeze(1)),
                      (selmat[:, :], selmat_c[:, :]),
                      (convw_sb[:, :, :], convw_in[:, :, :]),
                      (convb_sb[:, :], convb_in[:, :]),
                      (dparam_sb[:, :], dparam_in[:, :]),
                      (dtbias_sb[:, :], dtbias_in[:, :]),
                      (dtlnw_sb[:, :], dtlnw_in[:, :]),
                      (bclnw_sb[:, :], bclnw_in[:].unsqueeze(1)),
                  ][:int(os.environ.get("KL0", "99"))]:
                      R_ld.issue(sync, dst, src)
                  R_ld.wait_all(sync)

          stop_after(-3)
          ctx2 = contextlib.ExitStack()
          xnT = ctx2.enter_context(nc.sbuf_tensor("xnT", [128, KH, TPC], F16))
          awT = ctx2.enter_context(nc.sbuf_tensor("awT", [128, KH, ERT], F16))
          bwT = ctx2.enter_context(nc.sbuf_tensor("bwT", [ERT, H], F16))
          wg9T = ctx2.enter_context(nc.sbuf_tensor("wg9T", [128, KH, E + 1], F16))
          bbase_sb = ctx2.enter_context(nc.sbuf_tensor("bbase_sb", [128, H], F16))
          hwT = ctx2.enter_context(nc.sbuf_tensor("hwT", [128, TPC], F16))

          with ctx2:
              with (
                  nc.sbuf_tensor("xp_raw", [128, 3, DSL], F16) as xp_raw,
                  nc.sbuf_tensor("dtp_raw", [128, NDT, R], F16) as dtp_raw,
                  nc.sbuf_tensor("aw_raw", [128, KH, 128], F16) as aw_raw,
                  nc.sbuf_tensor("bw_raw", [128, KH, E, RR], F16) as bw_raw,
                  nc.sbuf_tensor("wg_raw", [128, H], F16) as wg_raw,
                  nc.sbuf_tensor("wthr_raw", [128, KH], F16) as wthr_raw,
                  nc.sbuf_tensor("alog_raw", [128, NDT * N], F32) as alog_raw,
                  nc.sbuf_tensor("a_tmp", [128, NDT * N], F32) as a_tmp,
              ):
                  with nc.Block("p0_rawload") as blk:
                      @blk.gpsimd
                      def _(g):
                          for dst, src in [
                              (xp_raw[:, :, :], xproj_in[:, :, :]),
                              (dtp_raw[:, :, :], dtproj_in[:, :, :]),
                              (aw_raw[:, :, :], aw_in[:, :, :]),
                              (bw_raw[:, :, :, :], bw_in[:, :, :, :]),
                              (wg_raw[:, :], wgate_in[:, :]),
                              (wthr_raw[:, :], wthr_in[:, :]),
                              (bbase_sb[:, :], bbase_in[:, :]),
                          ][:int(os.environ.get("KL1", "99"))]:
                              R_g.issue(g, dst, src)
                          R_g.wait_all(g)

                      @blk.sync
                      def _(sync):
                          R_ld.issue(sync, alog_raw[:, :], alog_in[:, :])
                          R_ld.wait_all(sync)

                  stop_after(-2)
                  jobs = []
                  for ec in range(3):
                      er = EW3[ec]
                      for i in range(NDT):
                          jobs.append((xp_raw[:, ec, i * 128:(i + 1) * 128],
                                       xprojT[:, i, ec * 128:ec * 128 + er], None))
                  for i in range(NDT):
                      for rc in range(RC):
                          jobs.append((dtp_raw[:, i, rc * 128:(rc + 1) * 128],
                                       dtprojT[:, rc, i, :], ("pp", dtlnw_sb, rc)))
                  for k in range(KH):
                      jobs.append((aw_raw[:, k, :], awT[:, k, :], None))
                  for ob in range(KH):
                      jobs.append((bw_raw[:, ob, :, :].rearrange("p a b -> p (a b)"),
                                   bwT[:, ob * 128:(ob + 1) * 128], ("c", 2.0)))
                  for k in range(KH):
                      jobs.append((wg_raw[:, k * 128:(k + 1) * 128],
                                   wg9T[:, k, 0:E], None))
                  jobs = jobs[:int(os.environ.get("KJOBS", "9999"))]
                  NJ = len(jobs)
                  pe0, v0, sc0 = s_pe.v, s_v.v, s_sc.v

                  with nc.Block("p0_tr") as blk:
                      @blk.tensor
                      def _(t):
                          for j, (src, dst, scl) in enumerate(jobs):
                              assert src.shape[0] == 128 and src.shape[-1] == 128
                              if j >= 4:
                                  t.wait_ge(s_v.h, v0 + j - 3)
                              sl = j % 4
                              t.transpose(ps[sl][:, :].bitcast(F16)[:, 0:128],
                                          src, id16[:, :]).then_inc(s_pe.h, 1)
                          s_pe.v = pe0 + NJ

                      @blk.scalar
                      def _(s):
                          if os.environ.get("KSC", "1") == "1":
                              s.activation(a_tmp[:, :], alog_raw[:, :],
                                           ACT.Exp).then_inc(s_sc.h, 1)
                          else:
                              s.nop().then_inc(s_sc.h, 1)
                          s_sc.v = sc0 + 1

                      @blk.vector
                      def _(v):
                          for j, (src, dst, scl) in enumerate(jobs):
                              v.wait_ge(s_pe.h, pe0 + j + 1)
                              sl = j % 4
                              blkap = ps[sl][:, :].bitcast(F16)[:, 0:dst.shape[-1]]
                              if scl is None:
                                  v.tensor_copy(dst, blkap).then_inc(s_v.h, 1)
                              elif scl[0] == "c":
                                  v.tensor_scalar(dst, blkap, scl[1], None,
                                                  ALU.mult).then_inc(s_v.h, 1)
                              else:
                                  _, tsb, rc = scl
                                  v.tensor_scalar(dst, blkap, tsb[:, rc:rc + 1], None,
                                                  ALU.mult).then_inc(s_v.h, 1)
                          if os.environ.get("KVT", "1") == "1":
                              for k in range(KH):
                                  v.tensor_copy(wg9T[:, k, E:E + 1], wthr_raw[:, k:k + 1])
                              v.tensor_scalar(dtbias_ng[:, :], dtbias_sb[:, :], -1.0, None,
                                              ALU.mult)
                              v.tensor_scalar(bthr_ng[:, :], bthr_bc[:, :], -1.0, None,
                                              ALU.mult)
                              v.wait_ge(s_sc.h, sc0 + 1)
                              v.tensor_scalar(a_sb[:, :], a_tmp[:, :], -1.0, None, ALU.mult)
                          s_v.v = v0 + NJ

              stop_after(1)
              with (
                  nc.sbuf_tensor("xt1", [128, H], F32) as xt1,
                  nc.sbuf_tensor("sq1", [128, H], F32) as sq1,
                  nc.sbuf_tensor("xn16", [128, H], F16) as xn16,
                  nc.sbuf_tensor("mbuf", [128, 8], F32) as mbuf,
              ):
                  for tt in range(NT):
                      v0, sc0, pe0 = s_v.v, s_sc.v, s_pe.v
                      hx = {}
                      with nc.Block(f"p1_{tt}") as blk:
                          @blk.sync
                          def _(sync):
                              hx["x"] = R_ld.issue(sync, xt1[:, :],
                                                   x_in[tt * 128:(tt + 1) * 128, :])

                          @blk.scalar
                          def _(s):
                              s.wait_ge(*hx["x"])
                              s.activation(sq1[:, :], xt1[:, :],
                                           ACT.Square).then_inc(s_sc.h, 1)
                              s.wait_ge(s_v.h, v0 + 2)
                              edge(s, s.activation(mbuf[:, 6:7], mbuf[:, 5:6], ACT.Ln,
                                                   bias=float(EPS_LN)))
                              s.activation(mbuf[:, 7:8], mbuf[:, 6:7], ACT.Exp,
                                           scale=-0.5).then_inc(s_sc.h, 1)
                              s_sc.v = sc0 + 2

                          @blk.vector
                          def _(v):
                              v.wait_ge(*hx["x"])
                              v.tensor_reduce(mbuf[:, 0:1], xt1[:, :], AXL.X, ALU.add)
                              v.wait_ge(s_sc.h, sc0 + 1)
                              v.tensor_reduce(mbuf[:, 1:2], sq1[:, :], AXL.X, ALU.add
                                              ).then_inc(s_v.h, 1)
                              v.wait_ge(s_v.h, v0 + 1)
                              edge(v, v.tensor_scalar(mbuf[:, 2:3], mbuf[:, 0:1], 1.0 / H, None, ALU.mult))
                              edge(v, v.tensor_scalar(mbuf[:, 3:4], mbuf[:, 1:2], 1.0 / H, None, ALU.mult))
                              edge(v, v.tensor_tensor(mbuf[:, 4:5], mbuf[:, 2:3], mbuf[:, 2:3], ALU.mult))
                              v.tensor_tensor(mbuf[:, 5:6], mbuf[:, 3:4], mbuf[:, 4:5],
                                              ALU.subtract).then_inc(s_v.h, 1)
                              v.wait_ge(s_v.h, v0 + 2)
                              v.wait_ge(s_sc.h, sc0 + 2)
                              v.tensor_scalar(xn16[:, :], xt1[:, :],
                                              mbuf[:, 2:3], mbuf[:, 7:8],
                                              ALU.subtract, ALU.mult).then_inc(s_v.h, 1)
                              for kk in range(KH):
                                  v.wait_ge(s_pe.h, pe0 + kk + 1)
                                  sl = kk % 4
                                  v.tensor_copy(xnT[:, kk, tt * 128:(tt + 1) * 128],
                                                ps[sl][:, :].bitcast(F16)[:, 0:128]
                                                ).then_inc(s_v.h, 1)
                              s_v.v = v0 + 3 + KH

                          @blk.tensor
                          def _(t):
                              t.wait_ge(s_v.h, v0 + 3)
                              for kk in range(KH):
                                  if kk >= 4:
                                      t.wait_ge(s_v.h, v0 + 3 + kk - 3)
                                  t.transpose(ps[kk % 4][:, :].bitcast(F16)[:, 0:128],
                                              xn16[:, kk * 128:(kk + 1) * 128],
                                              id16[:, :]).then_inc(s_pe.h, 1)
                              s_pe.v = pe0 + KH

              stop_after(2)
              with (
                  nc.sbuf_tensor("slab", [128, 2, KH, 512], F16) as slab,
                  nc.sbuf_tensor("stage2", [128, 2, TPC], F16) as stage2,
              ):
                  v0, g0, pe0 = s_v.v, s_g.v, s_pe.v
                  hslab = {}
                  hspill = {}
                  with nc.Block("p2") as blk:
                      @blk.gpsimd
                      def _(g):
                          for cc in range(NCC):
                              if cc >= 2:
                                  g.wait_ge(s_pe.h, pe0 + (cc - 1) * 4 * TH)
                              for k in range(KH):
                                  hslab[(cc, k)] = R_g.issue(
                                      g, slab[:, cc % 2, k, :],
                                      win_in[k * 128:(k + 1) * 128,
                                             cc * 512:(cc + 1) * 512])

                      @blk.sync
                      def _(sync):
                          for j2 in range(NCC * 4):
                              sync.wait_ge(s_v.h, v0 + (j2 + 1) * TH)
                              col0 = j2 * 128
                              if col0 < DIN:
                                  dst = xh_sp[col0:col0 + 128, :]
                              else:
                                  dst = z_sp[col0 - DIN:col0 - DIN + 128, :]
                              hspill[j2] = R_ld.issue(sync, dst, stage2[:, j2 % 2, :])

                      @blk.tensor
                      def _(t):
                          for cc in range(NCC):
                              for k in range(KH):
                                  t.wait_ge(*hslab[(cc, k)])
                              for ct in range(4):
                                  for th in range(TH):
                                      idx = (cc * 4 + ct) * TH + th
                                      bank = idx % 4
                                      if idx >= 4:
                                          t.wait_ge(s_v.h, v0 + idx - 3)
                                      for k in range(KH):
                                          mm = t.matmul(ps[bank][:, :],
                                                        slab[:, cc % 2, k, ct * 128:(ct + 1) * 128],
                                                        xnT[:, k, th * 512:(th + 1) * 512],
                                                        start=(k == 0), stop=(k == KH - 1))
                                      mm.then_inc(s_pe.h, 1)
                          s_pe.v = pe0 + NCC * 4 * TH

                      @blk.vector
                      def _(v):
                          for cc in range(NCC):
                              for ct in range(4):
                                  j2 = cc * 4 + ct
                                  for th in range(TH):
                                      idx = j2 * TH + th
                                      v.wait_ge(s_pe.h, pe0 + idx + 1)
                                      if th == 0 and j2 >= 2:
                                          v.wait_ge(*hspill[j2 - 2])
                                      v.tensor_copy(stage2[:, j2 % 2, th * 512:(th + 1) * 512],
                                                    ps[idx % 4][:, :]).then_inc(s_v.h, 1)
                          s_v.v = v0 + NCC * 4 * TH

              stop_after(3)
              with nc.Block("coll1") as blk:
                  @blk.gpsimd
                  def _(g):
                      R_ld.wait_all(g)
                      with allow_ici_all_to_all():
                          g.collective_compute(
                              "AllToAll", ALU.bypass, replica_groups=RG,
                              ins=[xh_sp.ap().opt()], outs=[xh_scan.ap().opt()],
                          ).then_inc(s_cc.h, 1)
                      s_cc.v += 1
                      g.wait_ge(s_cc.h, s_cc.v)

              stop_after(4)
              with (
                  nc.sbuf_tensor("xpad", [128, NDT, 3 + TPC], F16) as xpad,
                  nc.sbuf_tensor("xcb", [128, NDT, TPC], F16) as xcb,
                  nc.sbuf_tensor("cva", [128, TPC], F32) as cva,
                  nc.sbuf_tensor("cvb", [128, TPC], F32) as cvb,
                  nc.sbuf_tensor("ssm_ev", [128, 3, TPC], F32) as ssm_ev,
                  nc.sbuf_tensor("ssm_tr", [128, NT, W288], F32) as ssm_tr,
              ):
                  hxcsp_prev = [None] * NDT
                  hssm_prev = None
                  for r in range(NCORES):
                      v0, g0, pe0, sc0 = s_v.v, s_g.v, s_pe.v, s_sc.v
                      halo = (r % CPB != 0)
                      VC = NDT + 3 * TH + NT * 3
                      hload = {}
                      hxcsp = {}
                      hssm = {}
                      hssm_prev_cap = hssm_prev
                      with nc.Block(f"p3_{r}") as blk:
                          @blk.sync
                          def _(sync):
                              for i in range(NDT):
                                  h1 = R_ld.issue(sync, xpad[:, i, 3:],
                                                  xh_scan[r, i * 128:(i + 1) * 128, :])
                                  if halo:
                                      h2 = R_ld.issue(sync, xpad[:, i, 0:3],
                                                      xh_scan[r - 1, i * 128:(i + 1) * 128,
                                                              TPC - 3:TPC])
                                      hload[i] = [h1, h2]
                                  else:
                                      hload[i] = [h1]
                              for i in range(NDT):
                                  sync.wait_ge(s_sc.h, sc0 + i + 1)
                                  hxcsp[i] = R_ld.issue(
                                      sync, xc_sp[i * 128:(i + 1) * 128,
                                                  r * TPC:(r + 1) * TPC],
                                      xcb[:, i, :])
                              sync.wait_ge(s_v.h, v0 + VC)
                              hssm[0] = R_ld.issue(
                                  sync,
                                  ssm_sp[r * TPC:(r + 1) * TPC, :].rearrange(
                                      "(tt p) e -> p tt e", p=128),
                                  ssm_tr[:, :, :])

                          @blk.vector
                          def _(v):
                              for i in range(NDT):
                                  for hh in hload[i]:
                                      v.wait_ge(*hh)
                                  if i >= 1:
                                      v.wait_ge(s_sc.h, sc0 + i)
                                  if r == 0 and i == 0:
                                      v.memset(ssm_ev[:, 2, :], 0.0)
                                  xp = xpad[:, i, :]
                                  if not halo:
                                      edge(v, v.memset(xp[:, 0:3], 0.0))
                                  w = [convw_sb[:, i, kk:kk + 1] for kk in range(KC)]
                                  edge(v, v.tensor_scalar(cva[:, :], xp[:, 0:TPC], w[0],
                                                          convb_sb[:, i:i + 1], ALU.mult, ALU.add))
                                  edge(v, v.scalar_tensor_tensor(cvb[:, :], xp[:, 1:1 + TPC], w[1],
                                                                 cva[:, :], ALU.mult, ALU.add))
                                  edge(v, v.scalar_tensor_tensor(cva[:, :], xp[:, 2:2 + TPC], w[2],
                                                                 cvb[:, :], ALU.mult, ALU.add))
                                  v.scalar_tensor_tensor(cvb[:, :], xp[:, 3:3 + TPC], w[3],
                                                         cva[:, :], ALU.mult, ALU.add
                                                         ).then_inc(s_v.h, 1)
                              v.wait_ge(s_pe.h, pe0 + 1)
                              for ec in range(3):
                                  for th in range(TH):
                                      v.tensor_copy(ssm_ev[0:EW3[ec], ec, th * 512:(th + 1) * 512],
                                                    ps[ec * TH + th][0:EW3[ec], :]
                                                    ).then_inc(s_v.h, 1)
                              if hssm_prev_cap is not None:
                                  v.wait_ge(*hssm_prev_cap)
                              for tt in range(NT):
                                  for ec in range(3):
                                      jj = tt * 3 + ec
                                      v.wait_ge(s_pe.h, pe0 + 1 + jj + 1)
                                      rb = (ps[6][:, 0:EW3[ec]] if jj % 2 == 0 else
                                            pst[:, :].bitcast(F32)[:, 0:EW3[ec]])
                                      v.tensor_copy(ssm_tr[:, tt, ec * 128:ec * 128 + EW3[ec]],
                                                    rb).then_inc(s_v.h, 1)
                              s_v.v = v0 + VC

                          @blk.scalar
                          def _(s):
                              for i in range(NDT):
                                  s.wait_ge(s_v.h, v0 + i + 1)
                                  if hxcsp_prev[i] is not None:
                                      s.wait_ge(*hxcsp_prev[i])
                                  s.activation(xcb[:, i, :], cvb[:, :],
                                               ACT.Silu).then_inc(s_sc.h, 1)
                              s_sc.v = sc0 + NDT

                          @blk.tensor
                          def _(t):
                              for i in range(NDT):
                                  t.wait_ge(s_sc.h, sc0 + i + 1)
                                  for ec in range(3):
                                      for th in range(TH):
                                          mm = t.matmul(ps[ec * TH + th][0:EW3[ec], :],
                                                        xprojT[:, i, ec * 128:ec * 128 + EW3[ec]],
                                                        xcb[:, i, th * 512:(th + 1) * 512],
                                                        start=(i == 0), stop=(i == NDT - 1))
                              mm.then_inc(s_pe.h, 1)
                              for tt in range(NT):
                                  for ec in range(3):
                                      jj = tt * 3 + ec
                                      t.wait_ge(s_v.h, v0 + NDT + ec * TH + TH)
                                      if jj >= 2:
                                          t.wait_ge(s_v.h, v0 + NDT + 3 * TH + jj - 1)
                                      tb = (ps[6][:, 0:128] if jj % 2 == 0 else
                                            pst[:, :].bitcast(F32)[:, 0:128])
                                      t.transpose(
                                          tb,
                                          ssm_ev[:, ec, tt * 128:(tt + 1) * 128],
                                          id32[:, :]
                                      ).then_inc(s_pe.h, 1)
                              s_pe.v = pe0 + 1 + NT * 3
                      hxcsp_prev = [hxcsp[i] for i in range(NDT)]
                      hssm_prev = hssm[0]

              stop_after(5)
              with nc.Block("coll2") as blk:
                  @blk.gpsimd
                  def _(g):
                      R_ld.wait_all(g)
                      g.collective_compute(
                          "ReduceScatter", ALU.add, replica_groups=RG,
                          ins=[ssm_sp.ap().opt()], outs=[ssm_red.ap().opt()],
                      ).then_inc(s_cc.h, 1)
                      s_cc.v += 1
              cc2val = s_cc.v

              stop_after(6)
              with (
                  nc.sbuf_tensor("gl", [128, E + 1], F32) as gl,
                  nc.sbuf_tensor("gw", [128, E], F32) as gw,
                  nc.sbuf_tensor("gwe", [128, E], F32) as gwe,
                  nc.sbuf_tensor("gw3", [128, E], F32) as gw3,
                  nc.sbuf_tensor("gm", [128, 8], F32) as gm,
                  nc.sbuf_tensor("hw16", [128, ERT], F16) as hw16,
              ):
                  for tt in range(NT):
                      v0, sc0, pe0 = s_v.v, s_sc.v, s_pe.v
                      with nc.Block(f"p5a_{tt}") as blk:
                          @blk.tensor
                          def _(t):
                              for k in range(KH):
                                  mm = t.matmul(ps[0][:, 0:E + 1],
                                                xnT[:, k, tt * 128:(tt + 1) * 128],
                                                wg9T[:, k, :],
                                                start=(k == 0), stop=(k == KH - 1))
                              mm.then_inc(s_pe.h, 1)
                              for k in range(KH):
                                  mm = t.matmul(ps[1][:, 0:ERT],
                                                xnT[:, k, tt * 128:(tt + 1) * 128],
                                                awT[:, k, :],
                                                start=(k == 0), stop=(k == KH - 1))
                              mm.then_inc(s_pe.h, 1)
                              t.wait_ge(s_v.h, v0 + 10)
                              t.transpose(pst[:, 0:128], hw16[:, :], id16[:, :]
                                          ).then_inc(s_pe.h, 1)
                              s_pe.v = pe0 + 3

                          @blk.vector
                          def _(v):
                              v.wait_ge(s_pe.h, pe0 + 1)
                              edge(v, v.tensor_copy(gl[:, :], ps[0][:, 0:E + 1]))
                              v.tensor_reduce(gm[:, 0:1], gl[:, 0:E], AXL.X, ALU.max
                                              ).then_inc(s_v.h, 1)
                              v.wait_ge(s_v.h, v0 + 1)
                              v.tensor_scalar(gw[:, :], gl[:, 0:E], gm[:, 0:1], None,
                                              ALU.subtract).then_inc(s_v.h, 1)
                              v.wait_ge(s_sc.h, sc0 + 1)
                              v.tensor_reduce(gm[:, 1:2], gwe[:, :], AXL.X, ALU.add
                                              ).then_inc(s_v.h, 1)
                              v.wait_ge(s_v.h, v0 + 3)
                              v.reciprocal(gm[:, 2:3], gm[:, 1:2])
                              v.tensor_scalar(gm[:, 4:5], gm[:, 3:4], 1.0, None, ALU.add
                                              ).then_inc(s_v.h, 1)
                              v.wait_ge(s_v.h, v0 + 4)
                              v.reciprocal(gm[:, 5:6], gm[:, 4:5]).then_inc(s_v.h, 1)
                              v.wait_ge(s_v.h, v0 + 5)
                              v.tensor_scalar(gm[:, 6:7], gm[:, 5:6], MAX_THR, None,
                                              ALU.mult).then_inc(s_v.h, 1)
                              v.wait_ge(s_v.h, v0 + 6)
                              edge(v, v.tensor_scalar(gw3[:, :], gwe[:, :], gm[:, 2:3], gm[:, 6:7],
                                                      ALU.mult, ALU.subtract))
                              edge(v, v.tensor_scalar_max(gw[:, :], gw3[:, :], 0.0))
                              v.tensor_reduce(gm[:, 1:2], gw[:, :], AXL.X, ALU.add
                                              ).then_inc(s_v.h, 1)
                              v.wait_ge(s_v.h, v0 + 7)
                              edge(v, v.tensor_single_scalar(gm[:, 4:5], gm[:, 1:2], 0.0,
                                                             ALU.is_equal))
                              v.tensor_tensor(gm[:, 5:6], gm[:, 1:2], gm[:, 4:5], ALU.add
                                              ).then_inc(s_v.h, 1)
                              v.wait_ge(s_v.h, v0 + 8)
                              v.reciprocal(gm[:, 2:3], gm[:, 5:6]).then_inc(s_v.h, 1)
                              v.wait_ge(s_v.h, v0 + 9)
                              v.wait_ge(s_pe.h, pe0 + 2)
                              for e in range(E):
                                  hh16 = v.tensor_scalar(hw16[:, e * RR:(e + 1) * RR],
                                                         ps[1][:, e * RR:(e + 1) * RR],
                                                         gw[:, e:e + 1], gm[:, 2:3],
                                                         ALU.mult, ALU.mult)
                              hh16.then_inc(s_v.h, 1)
                              v.wait_ge(s_pe.h, pe0 + 3)
                              v.tensor_copy(hwT[:, tt * 128:(tt + 1) * 128], pst[:, 0:128])
                              s_v.v = v0 + 10

                          @blk.scalar
                          def _(s):
                              s.wait_ge(s_v.h, v0 + 2)
                              s.activation(gwe[:, :], gw[:, :], ACT.Exp)
                              s.activation(gm[:, 3:4], gl[:, E:E + 1], ACT.Exp,
                                           scale=-1.0, bias=bthr_ng[:, 0:1]
                                           ).then_inc(s_sc.h, 1)
                              s_sc.v = sc0 + 1

              stop_after(9)
              with (
                  nc.sbuf_tensor("wslab", [128, 2, KH, 512], F16) as wslab,
                  nc.sbuf_tensor("gstage", [128, 2, 512], F16) as gstage,
              ):
                  hgsp_prev = None
                  hwb = {}
                  for oc in range(H // 512):
                      v0, g0, pe0, sc0 = s_v.v, s_g.v, s_pe.v, s_sc.v
                      hgsp = {}
                      hgsp_prev_cap = hgsp_prev
                      with nc.Block(f"p5b_{oc}") as blk:
                          @blk.gpsimd
                          def _(g):
                              for ld_oc in [x for x in
                                            ([0, 1] if oc == 0 else [oc + 1])
                                            if x < H // 512]:
                                  if ld_oc >= 2:
                                      g.wait_ge(s_pe.h, pe0)
                                  hwb[ld_oc] = R_g.issue(
                                      g, wslab[:, ld_oc % 2, :, :],
                                      wbase_in[:, ld_oc * 512:(ld_oc + 1) * 512]
                                      .rearrange("(k p) c -> p k c", p=128))

                          @blk.sync
                          def _(sync):
                              for tt in range(NT):
                                  sync.wait_ge(s_sc.h, sc0 + tt + 1)
                                  hgsp[tt] = R_ld.issue(
                                      sync, g_sp[tt * 128:(tt + 1) * 128,
                                                 oc * 512:(oc + 1) * 512],
                                      gstage[:, tt % 2, :])

                          @blk.tensor
                          def _(t):
                              t.wait_ge(*hwb[oc])
                              for tt in range(NT):
                                  if tt >= 2:
                                      t.wait_ge(s_sc.h, sc0 + tt - 1)
                                  bank = tt % 2
                                  for k in range(KH):
                                      t.matmul(ps[bank][:, :],
                                               xnT[:, k, tt * 128:(tt + 1) * 128],
                                               wslab[:, oc % 2, k, :],
                                               start=(k == 0), stop=False)
                                  t.matmul(ps[bank][:, :], id16[:, :],
                                           bbase_sb[:, oc * 512:(oc + 1) * 512],
                                           start=False, stop=False)
                                  t.matmul(ps[bank][:, :],
                                           hwT[:, tt * 128:(tt + 1) * 128],
                                           bwT[:, oc * 512:(oc + 1) * 512],
                                           start=False, stop=True).then_inc(s_pe.h, 1)
                              s_pe.v = pe0 + NT

                          @blk.scalar
                          def _(s):
                              if hgsp_prev_cap is not None:
                                  for hh in list(hgsp_prev_cap.values())[-2:]:
                                      s.wait_ge(*hh)
                              for tt in range(NT):
                                  s.wait_ge(s_pe.h, pe0 + tt + 1)
                                  if tt >= 2:
                                      s.wait_ge(*hgsp[tt - 2])
                                  s.activation(gstage[:, tt % 2, :], ps[tt % 2][:, :],
                                               ACT.Silu).then_inc(s_sc.h, 1)
                              s_sc.v = sc0 + NT
                      hgsp_prev = hgsp

              with (
                  nc.sbuf_tensor("p4x", [128, W288], F32) as p4x,
                  nc.sbuf_tensor("p4sq", [128, W288], F32) as p4sq,
                  nc.sbuf_tensor("p4o", [128, W288], F16) as p4o,
                  nc.sbuf_tensor("p4m", [128, 8], F32) as p4m,
                  nc.sbuf_tensor("p4st", [128, 3 * 128], F16) as p4st,
              ):
                  GRP = [(0, R, R), (R, R + N, N), (R + N, W288, N)]
                  hst_prev = None
                  for tt in range(NT):
                      v0, sc0, pe0 = s_v.v, s_sc.v, s_pe.v
                      hx4 = {}
                      hst4 = {}
                      hst_prev_cap = hst_prev
                      with nc.Block(f"p4_{tt}") as blk:
                          @blk.sync
                          def _(sync):
                              if tt == 0:
                                  sync.wait_ge(s_cc.h, cc2val)
                              hx4["x"] = R_ld.issue(sync, p4x[:, :],
                                                    ssm_red[tt * 128:(tt + 1) * 128, :])
                              sync.wait_ge(s_v.h, v0 + 2 + 3)
                              for ec in range(3):
                                  hst4[ec] = R_ld.issue(
                                      sync,
                                      dtrT_loc[ec * 128:ec * 128 + EW3[ec],
                                               tt * 128:(tt + 1) * 128],
                                      p4st[0:EW3[ec], ec * 128:(ec + 1) * 128])

                          @blk.scalar
                          def _(s):
                              s.wait_ge(*hx4["x"])
                              s.activation(p4sq[:, :], p4x[:, :],
                                           ACT.Square).then_inc(s_sc.h, 1)
                              s.wait_ge(s_v.h, v0 + 1)
                              for gi, (a0, b0, w0) in enumerate(GRP):
                                  edge(s, s.activation(p4m[:, 6:7], p4m[:, gi:gi + 1],
                                                       ACT.Ln, bias=float(EPS_RMS), scale=1.0 / w0))
                                  ii = s.activation(p4m[:, 3 + gi:4 + gi],
                                                    p4m[:, 6:7],
                                                    ACT.Exp, scale=-0.5)
                                  if gi < 2:
                                      edge(s, ii)
                              ii.then_inc(s_sc.h, 1)
                              s_sc.v = sc0 + 2

                          @blk.vector
                          def _(v):
                              if hst_prev_cap is not None:
                                  for hh in hst_prev_cap.values():
                                      v.wait_ge(*hh)
                              v.wait_ge(s_sc.h, sc0 + 1)
                              for gi, (a0, b0, w0) in enumerate(GRP):
                                  rr = v.tensor_reduce(p4m[:, gi:gi + 1], p4sq[:, a0:b0],
                                                       AXL.X, ALU.add)
                              rr.then_inc(s_v.h, 1)
                              v.wait_ge(s_sc.h, sc0 + 2)
                              v.wait_ge(s_v.h, v0 + 1)
                              for gi, (a0, b0, w0) in enumerate(GRP):
                                  vv = v.tensor_scalar(p4o[:, a0:b0], p4x[:, a0:b0],
                                                       p4m[:, 3 + gi:4 + gi], None,
                                                       ALU.mult)
                              vv.then_inc(s_v.h, 1)
                              for ec in range(3):
                                  v.wait_ge(s_pe.h, pe0 + ec + 1)
                                  v.tensor_copy(p4st[0:EW3[ec], ec * 128:(ec + 1) * 128],
                                                ps[ec][:, :].bitcast(F16)[0:EW3[ec], 0:128]
                                                ).then_inc(s_v.h, 1)
                              s_v.v = v0 + 5

                          @blk.tensor
                          def _(t):
                              t.wait_ge(s_v.h, v0 + 2)
                              for ec in range(3):
                                  t.transpose(ps[ec][:, :].bitcast(F16)[0:EW3[ec], 0:128],
                                              p4o[:, ec * 128:ec * 128 + EW3[ec]],
                                              id16[:, :]).then_inc(s_pe.h, 1)
                              s_pe.v = pe0 + 3
                      hst_prev = hst4

              stop_after(7)
              with nc.Block("coll3") as blk:
                  @blk.gpsimd
                  def _(g):
                      R_ld.wait_all(g)
                      g.collective_compute(
                          "AllGather", ALU.bypass, replica_groups=RG,
                          ins=[dtrT_loc.ap().opt()], outs=[dtr_all.ap().opt()],
                      ).then_inc(s_cc.h, 1)
                      s_cc.v += 1
                      g.wait_ge(s_cc.h, s_cc.v)

              stop_after(8)

          stop_after(10)
          ctx6 = contextlib.ExitStack()
          _e6 = ctx6.enter_context
          dtrT_b = _e6(nc.sbuf_tensor("dtrT_b", [128, RC, LB], F16))
          bcT_raw = _e6(nc.sbuf_tensor("bcT_raw", [128, LB], F16))
          bcT_b = _e6(nc.sbuf_tensor("bcT_b", [128, LB], F16))
          st_sb = _e6(nc.sbuf_tensor("st_sb", [128, NDT * N], F32))
          dt_sb = _e6(nc.sbuf_tensor("dt_sb", [128, NDT, CW], F16))
          dtx_sb = _e6(nc.sbuf_tensor("dtx_sb", [128, NDT, CW], F16))
          xc_sb = _e6(nc.sbuf_tensor("xc_sb", [128, NDT, CW], F16))
          ys_sb = _e6(nc.sbuf_tensor("ys_sb", [128, 2, NDT, CW], F16))
          bbc = _e6(nc.sbuf_tensor("bbc", [128, CW], F16))
          cbc = _e6(nc.sbuf_tensor("cbc", [128, CW], F16))
          dabuf = _e6(nc.sbuf_tensor("dabuf", [128, 2, CW], F16))
          dbxbuf = _e6(nc.sbuf_tensor("dbxbuf", [128, 2, CW], F16))
          hbuf = _e6(nc.sbuf_tensor("hbuf", [128, CW], F16))
          tmpbuf = _e6(nc.sbuf_tensor("tmpbuf", [128, CW], F16))
          ybuf = _e6(nc.sbuf_tensor("ybuf", [128, 2, CW], F16))
          sp_a = _e6(nc.sbuf_tensor("sp_a", [128, 512], F32))
          sp_b = _e6(nc.sbuf_tensor("sp_b", [128, 512], F32))
          sp_u = _e6(nc.sbuf_tensor("sp_u", [128, 512], F32))
          sp_d = _e6(nc.sbuf_tensor("sp_d", [128, 512], F32))
          sp_e = _e6(nc.sbuf_tensor("sp_e", [128, 512], F32))
          with ctx6:
              hy6_prev = {}
              for b in range(NB):
                  v0 = s_v.v
                  h6l = []
                  with nc.Block(f"p6load_{b}") as blk:
                      @blk.sync
                      def _(sync):
                          for j in range(CPB):
                              src = dtr_all[b * CPB + j, :, :]
                              h6l.append(R_ld.issue(sync, dtrT_b[:, 0, j * TPC:(j + 1) * TPC],
                                                    src[0:128, :]))
                              h6l.append(R_ld.issue(sync, dtrT_b[:, 1, j * TPC:(j + 1) * TPC],
                                                    src[128:256, :]))
                              h6l.append(R_ld.issue(sync, bcT_raw[0:BCW, j * TPC:(j + 1) * TPC],
                                                    src[256:W288, :]))

                      @blk.vector
                      def _(v):
                          if b == 0:
                              edge(v, v.memset(bcT_raw[32:64, :], 0.0))
                              edge(v, v.memset(bcT_raw[64:128, :], 0.0))
                          for hh in h6l:
                              v.wait_ge(*hh)
                          v.tensor_scalar(bcT_b[:, :], bcT_raw[:, :],
                                          bclnw_sb[:, 0:1], None, ALU.mult)
                          v.memset(st_sb[:, :], 0.0)
                          s_v.v = v0

                  for tc in range(NTC):
                      v0, pe0, sc0, g0 = s_v.v, s_pe.v, s_sc.v, s_g.v
                      hxc6 = {}
                      hy6 = {}
                      hy6_prev_cap = dict(hy6_prev)
                      gtc = b * CPB + (tc * CW) // TPC
                      toff = b * LB + tc * CW
                      VS1 = NDT * (C6 * 2 + 1)
                      VPN = 1 + NDT
                      VSN = N * VPN

                      def scanmark(kk):
                          return (kk // NDT) * VPN + 1 + (kk % NDT) + 1

                      with nc.Block(f"p6_{b}_{tc}") as blk:
                          @blk.gpsimd
                          def _(g):
                              for n in range(N):
                                  for i in range(NDT):
                                      kk = n * NDT + i
                                      g.wait_ge(s_v.h, v0 + VS1 + n * VPN + 1)
                                      if kk >= 2:
                                          g.wait_ge(s_v.h,
                                                    v0 + VS1 + scanmark(kk - 2))
                                      g.tensor_tensor(dbxbuf[:, kk % 2, :],
                                                      dtx_sb[:, i, :], bbc[:, :],
                                                      ALU.mult).then_inc(s_g.h, 1)
                              s_g.v = g0 + N * NDT

                          @blk.sync
                          def _(sync):
                              for i in range(NDT):
                                  hxc6[i] = R_ld.issue(
                                      sync, xc_sb[:, i, :],
                                      xc_sp[i * 128:(i + 1) * 128, toff:toff + CW])
                              for i in range(NDT):
                                  sync.wait_ge(s_v.h, v0 + VS1 + VSN + i + 1)
                                  hy6[i] = R_ld.issue(
                                      sync,
                                      y_sp[gtc, i * 128:(i + 1) * 128,
                                           (tc * CW) % TPC:(tc * CW) % TPC + CW],
                                      ybuf[:, i % 2, :])

                          @blk.tensor
                          def _(t):
                              for i in range(NDT):
                                  for th in range(C6):
                                      if i >= 1:
                                          t.wait_ge(s_sc.h, sc0 + 2 * ((i - 1) * C6 + th) + 1)
                                      for rc in range(RC):
                                          mm = t.matmul(ps[th][:, :],
                                                        dtprojT[:, rc, i, :],
                                                        dtrT_b[:, rc, tc * CW + th * 512:
                                                               tc * CW + (th + 1) * 512],
                                                        start=(rc == 0), stop=(rc == RC - 1))
                                      mm.then_inc(s_pe.h, 1)
                              for n in range(N):
                                  if n >= 1:
                                      t.wait_ge(s_v.h, v0 + VS1 + (n - 1) * VPN + 1)
                                  for th in range(C6):
                                      rhs6 = bcT_b[:, tc * CW + th * 512:
                                                   tc * CW + (th + 1) * 512]
                                      t.matmul(ps[2 + th][:, :],
                                               selmat[:, n * 128:(n + 1) * 128],
                                               rhs6, start=True, stop=True)
                                      mm = t.matmul(ps[4 + th][:, :],
                                                    selmat[:, (N + n) * 128:(N + n + 1) * 128],
                                                    rhs6, start=True, stop=True)
                                  mm.then_inc(s_pe.h, 1)
                              s_pe.v = pe0 + NDT * C6 + N

                          @blk.scalar
                          def _(s):
                              for i in range(NDT):
                                  for th in range(C6):
                                      kk1 = i * C6 + th
                                      s.wait_ge(s_pe.h, pe0 + kk1 + 1)
                                      if kk1 >= 1:
                                          pi, pth = (i, th - 1) if th >= 1 else (i - 1, C6 - 1)
                                          s.wait_ge(s_v.h, v0 + pi * (2 * C6 + 1) + 2 * pth + 2)
                                      s.activation(sp_a[:, :], ps[th][:, :], ACT.Relu,
                                                   bias=dtbias_sb[:, i:i + 1])
                                      s.activation(sp_b[:, :], ps[th][:, :], ACT.Relu,
                                                   scale=-1.0, bias=dtbias_ng[:, i:i + 1]
                                                   ).then_inc(s_sc.h, 1)
                                      s.wait_ge(s_v.h, v0 + i * (2 * C6 + 1) + 2 * th + 1)
                                      edge(s, s.activation(sp_d[:, :], sp_u[:, :], ACT.Exp,
                                                           scale=-1.0))
                                      s.activation(sp_e[:, :], sp_d[:, :], ACT.Ln,
                                                   bias=1.0).then_inc(s_sc.h, 1)
                              for n in range(N):
                                  for i in range(NDT):
                                      kk = n * NDT + i
                                      s.wait_ge(s_v.h, v0 + i * (2 * C6 + 1) + 2 * C6)
                                      if kk >= 2:
                                          s.wait_ge(s_v.h,
                                                    v0 + VS1 + scanmark(kk - 2))
                                      s.activation(dabuf[:, kk % 2, :], dt_sb[:, i, :],
                                                   ACT.Exp,
                                                   scale=a_sb[:, i * N + n:i * N + n + 1]
                                                   ).then_inc(s_sc.h, 1)
                              s_sc.v = sc0 + NDT * C6 * 2 + N * NDT

                          @blk.vector
                          def _(v):
                              for i in range(NDT):
                                  for th in range(C6):
                                      v.wait_ge(s_sc.h, sc0 + 2 * (i * C6 + th) + 1)
                                      v.tensor_tensor(sp_u[:, :], sp_a[:, :], sp_b[:, :],
                                                      ALU.add).then_inc(s_v.h, 1)
                                      v.wait_ge(s_sc.h, sc0 + 2 * (i * C6 + th) + 2)
                                      v.wait_ge(s_v.h, v0 + i * (2 * C6 + 1) + 2 * th + 1)
                                      v.tensor_tensor(dt_sb[:, i, th * 512:(th + 1) * 512],
                                                      sp_a[:, :], sp_e[:, :], ALU.add
                                                      ).then_inc(s_v.h, 1)
                                  v.wait_ge(*hxc6[i])
                                  v.wait_ge(s_v.h, v0 + i * (2 * C6 + 1) + 2 * C6)
                                  v.tensor_tensor(dtx_sb[:, i, :], dt_sb[:, i, :],
                                                  xc_sb[:, i, :], ALU.mult
                                                  ).then_inc(s_v.h, 1)
                              for n in range(N):
                                  v.wait_ge(s_pe.h, pe0 + NDT * C6 + n + 1)
                                  if n >= 1:
                                      v.wait_ge(s_v.h, v0 + VS1 + n * VPN)
                                      v.wait_ge(s_g.h, g0 + n * NDT)
                                  for th in range(C6):
                                      v.tensor_copy(bbc[:, th * 512:(th + 1) * 512],
                                                    ps[2 + th][:, :])
                                      cpop = v.tensor_copy(cbc[:, th * 512:(th + 1) * 512],
                                                           ps[4 + th][:, :])
                                  cpop.then_inc(s_v.h, 1)
                                  for i in range(NDT):
                                      kk = n * NDT + i
                                      v.wait_ge(s_sc.h,
                                                sc0 + NDT * C6 * 2 + kk + 1)
                                      v.wait_ge(s_v.h, v0 + VS1 + n * VPN + 1)
                                      v.wait_ge(s_g.h, g0 + kk + 1)
                                      edge(v, v.tensor_tensor_scan(hbuf[:, :], dabuf[:, kk % 2, :],
                                                                   dbxbuf[:, kk % 2, :],
                                                                   st_sb[:, i * N + n:i * N + n + 1],
                                                                   ALU.mult, ALU.add))
                                      v.tensor_copy(st_sb[:, i * N + n:i * N + n + 1],
                                                    hbuf[:, CW - 1:CW])
                                      if n == 0:
                                          v.tensor_tensor(ys_sb[:, 0, i, :], hbuf[:, :],
                                                          cbc[:, :], ALU.mult
                                                          ).then_inc(s_v.h, 1)
                                      else:
                                          edge(v, v.tensor_tensor(tmpbuf[:, :], hbuf[:, :],
                                                                  cbc[:, :], ALU.mult))
                                          v.wait_ge(s_v.h,
                                                    v0 + VS1 + (n - 1) * VPN + 1 + i + 1)
                                          v.tensor_tensor(ys_sb[:, n % 2, i, :],
                                                          ys_sb[:, (n - 1) % 2, i, :],
                                                          tmpbuf[:, :], ALU.add
                                                          ).then_inc(s_v.h, 1)
                              for i in range(NDT):
                                  v.wait_ge(s_v.h, v0 + VS1 + (N - 1) * VPN + 1 + i + 1)
                                  if i >= 2:
                                      v.wait_ge(*hy6[i - 2])
                                  elif i in hy6_prev_cap:
                                      v.wait_ge(*hy6_prev_cap[i])
                                  v.scalar_tensor_tensor(ybuf[:, i % 2, :],
                                                         xc_sb[:, i, :],
                                                         dparam_sb[:, i:i + 1],
                                                         ys_sb[:, (N - 1) % 2, i, :],
                                                         ALU.mult, ALU.add
                                                         ).then_inc(s_v.h, 1)
                              s_v.v = v0 + VS1 + VSN + NDT
                      hy6_prev = {}
                      for k in range(min(2, NDT)):
                          ii = NDT - 1 - k
                          hy6_prev[ii % 2] = hy6[ii]

              stop_after(11)
              with nc.Block("coll4") as blk:
                  @blk.gpsimd
                  def _(g):
                      R_ld.wait_all(g)
                      with allow_ici_all_to_all():
                          g.collective_compute(
                              "AllToAll", ALU.bypass, replica_groups=RG,
                              ins=[y_sp.ap().opt()], outs=[y_own.ap().opt()],
                          ).then_inc(s_cc.h, 1)
                      s_cc.v += 1
                      g.wait_ge(s_cc.h, s_cc.v)

          stop_after(12)
          with nc.sbuf_tensor("y_sb", [128, NDG, TPC], F16) as y_sb:
            with (
              nc.sbuf_tensor("yt7", [128, 2, TPC], F16) as yt7,
              nc.sbuf_tensor("zt7", [128, 2, TPC], F16) as zt7,
              nc.sbuf_tensor("zs7", [128, 2, TPC], F16) as zs7,
            ):
              v0, sc0 = s_v.v, s_sc.v
              h7 = {}
              with nc.Block("p7a") as blk:
                  @blk.sync
                  def _(sync):
                      for dtg in range(NDG):
                          if dtg >= 2:
                              sync.wait_ge(s_v.h, v0 + dtg - 1)
                          rsl, roff = divmod(dtg, NDT)
                          hy = R_ld.issue(sync, yt7[:, dtg % 2, :],
                                          y_own[rsl, roff * 128:(roff + 1) * 128, :])
                          hz = R_ld.issue(sync, zt7[:, dtg % 2, :],
                                          z_sp[dtg * 128:(dtg + 1) * 128, :])
                          h7[dtg] = (hy, hz)

                  @blk.scalar
                  def _(s):
                      for dtg in range(NDG):
                          s.wait_ge(*h7[dtg][0])
                          s.wait_ge(*h7[dtg][1])
                          if dtg >= 2:
                              s.wait_ge(s_v.h, v0 + dtg - 1)
                          s.activation(zs7[:, dtg % 2, :], zt7[:, dtg % 2, :],
                                       ACT.Silu).then_inc(s_sc.h, 1)
                      s_sc.v = sc0 + NDG

                  @blk.vector
                  def _(v):
                      for dtg in range(NDG):
                          v.wait_ge(s_sc.h, sc0 + dtg + 1)
                          v.tensor_tensor(y_sb[:, dtg, :], yt7[:, dtg % 2, :],
                                          zs7[:, dtg % 2, :], ALU.mult
                                          ).then_inc(s_v.h, 1)
                      s_v.v = v0 + NDG

            with (
                  nc.sbuf_tensor("wA7", [128, 2, NDG // 2, HCH], F16) as wA7,
                  nc.sbuf_tensor("wB7", [128, NDG // 2, HCH], F16) as wB7,
                  nc.sbuf_tensor("gtile", [128, 2, HCH], F16) as gtile,
                  nc.sbuf_tensor("xtile", [128, 2, HCH], F32) as xtile,
                  nc.sbuf_tensor("sgbuf", [128, HCH], F32) as sgbuf,
                  nc.sbuf_tensor("otile", [128, 2, HCH], F32) as otile,
            ):
                  NDH = NDG // 2
                  hwoA = {}
                  hwoB = {}
                  hout_prev = None

                  def _wslice(dh, hc2):
                      return wout_in[dh * (DIN // 2):(dh + 1) * (DIN // 2),
                                     hc2 * HCH:(hc2 + 1) * HCH].rearrange(
                                         "(g2 p) c -> p g2 c", p=128)

                  for hc in range(NHC):
                      v0, g0, pe0, st0 = s_v.v, s_g.v, s_pe.v, s_st.v
                      hgx = {}
                      hout = {}
                      hout_prev_cap = hout_prev
                      with nc.Block(f"p7b_{hc}") as blk:
                          @blk.gpsimd
                          def _(g):
                              if hc == 0:
                                  hwoA[0] = R_g.issue(g, wA7[:, 0, :, :],
                                                      _wslice(0, 0))
                                  hwoB[0] = R_g.issue(g, wB7[:, :, :],
                                                      _wslice(1, 0))
                                  hwoA[1] = R_g.issue(g, wA7[:, 1, :, :],
                                                      _wslice(0, 1))
                              else:
                                  g.wait_ge(s_pe.h, pe0)
                                  hwoB[hc] = R_g.issue(g, wB7[:, :, :],
                                                       _wslice(1, hc))
                                  if hc + 1 < NHC:
                                      hwoA[hc + 1] = R_g.issue(
                                          g, wA7[:, (hc + 1) % 2, :, :],
                                          _wslice(0, hc + 1))

                          @blk.sync
                          def _(sync):
                              for tt in range(NT):
                                  if tt >= 2:
                                      sync.wait_ge(s_v.h, v0 + 2 * (tt - 1))
                                  hg = R_ld.issue(sync, gtile[:, tt % 2, :],
                                                  g_sp[tt * 128:(tt + 1) * 128,
                                                       hc * HCH:(hc + 1) * HCH])
                                  hx = R_ld.issue(sync, xtile[:, tt % 2, :],
                                                  x_in[tt * 128:(tt + 1) * 128,
                                                       hc * HCH:(hc + 1) * HCH])
                                  hgx[tt] = (hg, hx)
                                  sync.wait_ge(s_v.h, v0 + 2 * (tt + 1))
                                  hout[tt] = R_st.issue(
                                      sync, out_ext[tt * 128:(tt + 1) * 128,
                                                    hc * HCH:(hc + 1) * HCH],
                                      otile[:, tt % 2, :])

                          @blk.tensor
                          def _(t):
                              t.wait_ge(*hwoA[hc])
                              for tt in range(NT):
                                  if tt >= 2:
                                      t.wait_ge(s_v.h, v0 + 2 * (tt - 1))
                                  bank = tt % 2
                                  for dtg in range(NDG):
                                      if tt == 0 and dtg == NDH:
                                          t.wait_ge(*hwoB[hc])
                                      mov = (wA7[:, hc % 2, dtg, :] if dtg < NDH
                                             else wB7[:, dtg - NDH, :])
                                      mm = t.matmul(ps[bank][:, 0:HCH],
                                                    y_sb[:, dtg, tt * 128:(tt + 1) * 128],
                                                    mov,
                                                    start=(dtg == 0), stop=(dtg == NDG - 1))
                                  mm.then_inc(s_pe.h, 1)
                              s_pe.v = pe0 + NT

                          @blk.vector
                          def _(v):
                              for tt in range(NT):
                                  v.wait_ge(*hgx[tt][0])
                                  v.wait_ge(*hgx[tt][1])
                                  v.wait_ge(s_pe.h, pe0 + tt + 1)
                                  if tt >= 2:
                                      v.wait_ge(*hout[tt - 2])
                                  elif hout_prev_cap is not None:
                                      v.wait_ge(*hout_prev_cap[NT - 2 + tt])
                                  if tt >= 1:
                                      v.wait_ge(s_v.h, v0 + 2 * tt)
                                  v.tensor_tensor(sgbuf[:, :], ps[tt % 2][:, 0:HCH],
                                                  gtile[:, tt % 2, :], ALU.mult
                                                  ).then_inc(s_v.h, 1)
                                  v.wait_ge(s_v.h, v0 + 2 * tt + 1)
                                  v.tensor_tensor(otile[:, tt % 2, :], sgbuf[:, :],
                                                  xtile[:, tt % 2, :], ALU.add
                                                  ).then_inc(s_v.h, 1)
                              s_v.v = v0 + 2 * NT
                      hout_prev = hout

          with nc.Block("fin") as blk:
              @blk.sync
              def _(sync):
                  R_st.wait_all(sync)
                  R_ld.wait_all(sync)

      except _Stop:
          pass
    return nc


def _p6_marks(cfg):
    pass


def host_arrange(cfg, inputs, c):
    d0, d1 = c * cfg.DSL, (c + 1) * cfg.DSL
    NDT, N, KC, RC, KH, E, RR = (cfg.NDT, cfg.N, cfg.KC, cfg.RC, cfg.KH,
                                 cfg.E, cfg.RR)

    def dmaj(a):
        return np.ascontiguousarray(
            a.reshape(NDT, 128, *a.shape[1:]).transpose(
                1, 0, *range(2, a.ndim + 1)))

    out = {}
    out["conv_w_sl"] = dmaj(inputs["L0_attn_conv_w"][d0:d1])
    out["conv_b_sl"] = dmaj(inputs["L0_attn_conv_b"][d0:d1])
    out["D_sl"] = dmaj(inputs["L0_attn_D_param"][d0:d1])
    out["dt_bias_sl"] = dmaj(inputs["L0_attn_dt_bias"][d0:d1])
    out["A_log_sl"] = np.ascontiguousarray(
        dmaj(inputs["L0_attn_A_log"][d0:d1]).reshape(128, NDT * N))
    out["dt_proj_sl"] = dmaj(inputs["L0_attn_dt_proj_w"][d0:d1])
    xp = inputs["L0_attn_x_proj_w"][:, d0:d1]
    xpr = np.zeros((3, 128, cfg.DSL), dtype=np.float32)
    xpr[0] = xp[0:128]
    xpr[1] = xp[128:256]
    xpr[2, 0:2 * N] = xp[256:256 + 2 * N]
    out["x_proj_sl"] = np.ascontiguousarray(xpr.transpose(1, 0, 2))
    out["dt_ln_w"] = np.ascontiguousarray(
        inputs["L0_attn_dt_ln_w"].reshape(RC, 128).T)
    out["W_thr"] = np.ascontiguousarray(
        inputs["L0_ffn_W_thr"].reshape(KH, 128).T)
    out["A_w"] = np.ascontiguousarray(
        inputs["L0_ffn_A_w"].reshape(E * RR, KH, 128))
    out["B_w"] = np.ascontiguousarray(
        inputs["L0_ffn_B_w"].reshape(E, KH, 128, RR).transpose(2, 1, 0, 3))
    return out


def prep_core_inputs(cfg, inputs, c, x2d, bcln, aw2d):
    t0, t1 = c * cfg.TPC, (c + 1) * cfg.TPC
    m = {
        "x": x2d[t0:t1],
        "in_proj_w": inputs["L0_attn_in_proj_w"],
        "out_proj_w": inputs.get("_WoT", np.ascontiguousarray(
            inputs["L0_attn_out_proj_w"].T)),
        "bc_ln_w": np.ascontiguousarray(np.concatenate(
            [bcln, np.zeros(128 - bcln.shape[0], dtype=np.float32)])),
        "W_base": inputs.get("_WbT", np.ascontiguousarray(
            inputs["L0_ffn_W_base"].T)),
        "b_base": np.ascontiguousarray(np.broadcast_to(
            inputs["L0_ffn_b_base"], (128, inputs["L0_ffn_b_base"].shape[0]))),
        "W_gate": np.ascontiguousarray(np.concatenate(
            [inputs["L0_ffn_W_gate"],
             np.zeros((128 - inputs["L0_ffn_W_gate"].shape[0],
                       inputs["L0_ffn_W_gate"].shape[1]), dtype=np.float32)],
            axis=0)),
        "b_thr": np.ascontiguousarray(
            np.broadcast_to(inputs["L0_ffn_b_thr"].reshape(1), (128,))),
    }
    m.update(host_arrange(cfg, inputs, c))
    return m


def kernel(**inputs):
    cfg = CFG_FULL
    hs = inputs["L0_attn_hidden_states"]
    x2d = np.ascontiguousarray(hs.reshape(cfg.NTOK, cfg.H))
    bcln = np.ascontiguousarray(
        np.concatenate([inputs["L0_attn_B_ln_w"], inputs["L0_attn_C_ln_w"]]))
    aw2d = np.ascontiguousarray(
        inputs["L0_ffn_A_w"].reshape(cfg.E * cfg.RR, cfg.H))

    nc = build_nc(cfg)

    inputs = dict(inputs)
    inputs["_WoT"] = np.ascontiguousarray(inputs["L0_attn_out_proj_w"].T)
    inputs["_WbT"] = np.ascontiguousarray(inputs["L0_ffn_W_base"].T)
    in_maps = []
    for c in range(NCORES):
        t0, t1 = c * cfg.TPC, (c + 1) * cfg.TPC
        d0, d1 = c * cfg.DSL, (c + 1) * cfg.DSL
        in_maps.append(prep_core_inputs(cfg, inputs, c, x2d, bcln, aw2d))

    res = run_bass_kernel_spmd(nc, in_maps, core_ids=list(range(NCORES)))
    out = np.concatenate([r["out"] for r in res.results], axis=0)
    return out.reshape(cfg.NB, cfg.LB, cfg.H).astype(np.float32)


# version 07
# speedup vs baseline: 1.0323x; 1.0081x over previous
import contextlib
import os
import numpy as np
import concourse.bass as bass
import concourse.mybir as mybir
from concourse.bass_utils import run_bass_kernel_spmd
from concourse.aot_env import allow_ici_all_to_all

F32 = mybir.dt.float32
F16 = mybir.dt.float16
ALU = mybir.AluOpType
ACT = mybir.ActivationFunctionType
AXL = mybir.AxisListType

NCORES = 8
EPS_LN = 1e-5
EPS_RMS = 1e-6
MAX_THR = 0.125


class Cfg:
    def __init__(self, H=4096, DIN=8192, NTOK=8192, R=256, N=16, E=8, RR=16,
                 KC=4, NB=2):
        self.H, self.DIN, self.NTOK, self.R, self.N = H, DIN, NTOK, R, N
        self.E, self.RR, self.KC, self.NB = E, RR, KC, NB
        self.TPC = NTOK // NCORES
        self.DSL = DIN // NCORES
        self.LB = NTOK // NB
        self.NT = self.TPC // 128
        self.NDT = self.DSL // 128
        self.KH = H // 128
        self.CPB = self.LB // self.TPC
        self.NCOL = 2 * DIN
        self.ERT = E * RR
        self.RC = R // 128
        self.BCW = 2 * N
        self.TH = self.TPC // 512
        self.CW = min(1024, self.TPC)
        self.NTC = self.LB // self.CW
        self.C6 = self.CW // 512
        self.HCH = 256
        self.NHC = H // self.HCH
        self.NDG = DIN // 128
        assert self.TPC % 512 == 0 and self.DSL % 128 == 0 and self.ERT == 128


CFG_FULL = Cfg()


def build_nc(cfg: Cfg):
    H, DIN, R, N, E, RR, KC = cfg.H, cfg.DIN, cfg.R, cfg.N, cfg.E, cfg.RR, cfg.KC
    TPC, DSL, NT, NDT, KH = cfg.TPC, cfg.DSL, cfg.NT, cfg.NDT, cfg.KH
    NB, CPB, NCOL, ERT, RC, BCW = cfg.NB, cfg.CPB, cfg.NCOL, cfg.ERT, cfg.RC, cfg.BCW
    TH, CW, NTC, C6, HCH, NHC = cfg.TH, cfg.CW, cfg.NTC, cfg.C6, cfg.HCH, cfg.NHC
    NTOK, LB, NDG = cfg.NTOK, cfg.LB, cfg.NDG
    W288 = R + 2 * N
    NCC = NCOL // 512
    EW3 = [128, 128, BCW]

    nc = bass.Bass("TRN2", target_bir_lowering=False, debug=False)

    x_in = nc.dram_tensor("x", [TPC, H], F32, kind="ExternalInput")
    win_in = nc.dram_tensor("in_proj_w", [H, NCOL], F32, kind="ExternalInput")
    convw_in = nc.dram_tensor("conv_w_sl", [128, NDT, KC], F32, kind="ExternalInput")
    convb_in = nc.dram_tensor("conv_b_sl", [128, NDT], F32, kind="ExternalInput")
    xproj_in = nc.dram_tensor("x_proj_sl", [128, 3, DSL], F32, kind="ExternalInput")
    dtproj_in = nc.dram_tensor("dt_proj_sl", [128, NDT, R], F32, kind="ExternalInput")
    dtbias_in = nc.dram_tensor("dt_bias_sl", [128, NDT], F32, kind="ExternalInput")
    alog_in = nc.dram_tensor("A_log_sl", [128, NDT * N], F32, kind="ExternalInput")
    dparam_in = nc.dram_tensor("D_sl", [128, NDT], F32, kind="ExternalInput")
    wout_in = nc.dram_tensor("out_proj_w", [DIN, H], F32, kind="ExternalInput")
    dtlnw_in = nc.dram_tensor("dt_ln_w", [128, RC], F32, kind="ExternalInput")
    bclnw_in = nc.dram_tensor("bc_ln_w", [128], F32, kind="ExternalInput")
    wbase_in = nc.dram_tensor("W_base", [H, H], F32, kind="ExternalInput")
    bbase_in = nc.dram_tensor("b_base", [128, H], F32, kind="ExternalInput")
    wgate_in = nc.dram_tensor("W_gate", [128, H], F32, kind="ExternalInput")
    wthr_in = nc.dram_tensor("W_thr", [128, KH], F32, kind="ExternalInput")
    bthr_in = nc.dram_tensor("b_thr", [128], F32, kind="ExternalInput")
    aw_in = nc.dram_tensor("A_w", [128, KH, 128], F32, kind="ExternalInput")
    bw_in = nc.dram_tensor("B_w", [128, KH, E, RR], F32, kind="ExternalInput")
    out_ext = nc.dram_tensor("out", [TPC, H], F32, kind="ExternalOutput")

    xh_sp = nc.dram_tensor("xh_sp", [DIN, TPC], F16, kind="Internal")
    z_sp = nc.dram_tensor("z_sp", [DIN, TPC], F16, kind="Internal")
    xh_scan = nc.dram_tensor("xh_scan", [NCORES, DSL, TPC], F16, kind="Internal")
    xc_sp = nc.dram_tensor("xc_sp", [DSL, NTOK], F16, kind="Internal")
    ssm_sp = nc.dram_tensor("ssm_sp", [NTOK, W288], F32, kind="Internal")
    ssm_red = nc.dram_tensor("ssm_red", [TPC, W288], F32, kind="Internal")
    dtrT_loc = nc.dram_tensor("dtrT_loc", [W288, TPC], F16, kind="Internal")
    dtr_all = nc.dram_tensor("dtr_all", [NCORES, W288, TPC], F16, kind="Internal")
    g_sp = nc.dram_tensor("g_sp", [TPC, H], F16, kind="Internal")
    y_sp = nc.dram_tensor("y_sp", [NCORES, DSL, TPC], F16, kind="Internal")
    y_own = nc.dram_tensor("y_own", [NCORES, DSL, TPC], F16, kind="Internal")

    ident16 = nc.inline_tensor(np.eye(128, dtype=np.float16), name="ident16")
    selm = np.zeros((128, BCW * 128), dtype=np.float16)
    for _n in range(BCW):
        selm[_n, _n * 128:(_n + 1) * 128] = 1.0
    selmat_c = nc.inline_tensor(selm, name="selmat_c")
    ident32 = nc.inline_tensor(np.eye(128, dtype=np.float32), name="ident32")

    A = nc.alloc_sbuf_tensor
    id16 = A("id16", [128, 128], F16)
    id32 = A("id32", [128, 128], F32)
    bthr_bc = A("bthr_bc", [128, 1], F32)
    convw_sb = A("convw_sb", [128, NDT, KC], F32)
    convb_sb = A("convb_sb", [128, NDT], F32)
    dparam_sb = A("dparam_sb", [128, NDT], F32)
    dtbias_sb = A("dtbias_sb", [128, NDT], F32)
    dtbias_ng = A("dtbias_ng", [128, NDT], F32)
    a_sb = A("a_sb", [128, NDT * N], F32)
    xprojT = A("xprojT", [128, NDT, W288], F16)
    dtprojT = A("dtprojT", [128, RC, NDT, 128], F16)
    dtlnw_sb = A("dtlnw_sb", [128, RC], F32)
    bclnw_sb = A("bclnw_sb", [128, 1], F32)
    bthr_ng = A("bthr_ng", [128, 1], F32)
    selmat = A("selmat", [128, BCW * 128], F16)

    ctx = contextlib.ExitStack()

    class Sem:
        def __init__(self, name):
            self.h = ctx.enter_context(nc.semaphore(name))
            self.v = 0

    const_vals = [EPS_LN, EPS_RMS, -0.5, -1.0, 1.0 / R, 1.0 / N]
    const_tensors = []
    for ci, cv in enumerate(const_vals):
        cta = A(f"cst{ci}", [128, 1], F32)
        const_tensors.append((cta, float(cv)))
        nc.const_aps.register(F32, float(cv), cta[:, :])

    s_ld = Sem("s_ld")
    s_g = Sem("s_g")
    s_pe = Sem("s_pe")
    s_v = Sem("s_v")
    s_sc = Sem("s_sc")
    s_st = Sem("s_st")
    s_cc = Sem("s_cc")
    s_xv = Sem("s_xv")
    s_xs = Sem("s_xs")

    class Ring:

        def __init__(self, name, n):
            self.sems = [Sem(f"{name}{i}") for i in range(n)]
            self.n = n
            self.i = 0

        def issue(self, eng, out, in_, **kw):
            s = self.sems[self.i % self.n]
            if s.v > 0:
                eng.wait_ge(s.h, s.v)
            eng.dma_start(out=out, in_=in_, **kw).then_inc(s.h, 16)
            s.v += 16
            self.i += 1
            return (s.h, s.v)

        def wait_all(self, eng):
            for s in self.sems:
                if s.v > 0:
                    eng.wait_ge(s.h, s.v)

    R_ld = Ring("rld", 8)
    R_g = Ring("rg", 4)
    R_st = Ring("rst", 4)

    def edge(eng, instr):
        sx = s_xs if eng.engine == mybir.EngineType.Activation else s_xv
        sx.v += 1
        instr.then_inc(sx.h, 1)
        eng.wait_ge(sx.h, sx.v)

    GQ = 4

    def gdma(g, out, in_, **kw):
        if s_g.v >= GQ * 16:
            g.wait_ge(s_g.h, s_g.v - (GQ - 1) * 16)
        g.dma_start(out=out, in_=in_, **kw).then_inc(s_g.h, 16)
        s_g.v += 16

    ps = [ctx.enter_context(nc.psum_tensor(f"ps{i}", [128, 512], F32))
          for i in range(7)]
    pst = ctx.enter_context(nc.psum_tensor("pst", [128, 512], F16))

    RG = [list(range(NCORES))]

    KPH = int(os.environ.get("KPH", "99"))

    class _Stop(Exception):
        pass

    def stop_after(n):
        if KPH <= n:
            raise _Stop()

    with ctx:
      try:
          with nc.Block("p0_load") as blk:
              @blk.vector
              def _(v):
                  for cta, cv in const_tensors:
                      v.memset(cta[:, :], cv)

              @blk.sync
              def _(sync):
                  for dst, src in [
                      (id16[:, :], ident16[:, :]),
                      (id32[:, :], ident32[:, :]),
                      (bthr_bc[:, :], bthr_in[:].unsqueeze(1)),
                      (selmat[:, :], selmat_c[:, :]),
                      (convw_sb[:, :, :], convw_in[:, :, :]),
                      (convb_sb[:, :], convb_in[:, :]),
                      (dparam_sb[:, :], dparam_in[:, :]),
                      (dtbias_sb[:, :], dtbias_in[:, :]),
                      (dtlnw_sb[:, :], dtlnw_in[:, :]),
                      (bclnw_sb[:, :], bclnw_in[:].unsqueeze(1)),
                  ][:int(os.environ.get("KL0", "99"))]:
                      R_ld.issue(sync, dst, src)
                  R_ld.wait_all(sync)

          stop_after(-3)
          ctx2 = contextlib.ExitStack()
          xnT = ctx2.enter_context(nc.sbuf_tensor("xnT", [128, KH, TPC], F16))
          awT = ctx2.enter_context(nc.sbuf_tensor("awT", [128, KH, ERT], F16))
          bwT = ctx2.enter_context(nc.sbuf_tensor("bwT", [ERT, H], F16))
          wg9T = ctx2.enter_context(nc.sbuf_tensor("wg9T", [128, KH, E + 1], F16))
          bbase_sb = ctx2.enter_context(nc.sbuf_tensor("bbase_sb", [128, H], F16))
          hwT = ctx2.enter_context(nc.sbuf_tensor("hwT", [128, TPC], F16))

          with ctx2:
              with (
                  nc.sbuf_tensor("xp_raw", [128, 3, DSL], F16) as xp_raw,
                  nc.sbuf_tensor("dtp_raw", [128, NDT, R], F16) as dtp_raw,
                  nc.sbuf_tensor("aw_raw", [128, KH, 128], F16) as aw_raw,
                  nc.sbuf_tensor("bw_raw", [128, KH, E, RR], F16) as bw_raw,
                  nc.sbuf_tensor("wg_raw", [128, H], F16) as wg_raw,
                  nc.sbuf_tensor("wthr_raw", [128, KH], F16) as wthr_raw,
                  nc.sbuf_tensor("alog_raw", [128, NDT * N], F32) as alog_raw,
                  nc.sbuf_tensor("a_tmp", [128, NDT * N], F32) as a_tmp,
              ):
                  with nc.Block("p0_rawload") as blk:
                      @blk.gpsimd
                      def _(g):
                          for dst, src in [
                              (xp_raw[:, :, :], xproj_in[:, :, :]),
                              (dtp_raw[:, :, :], dtproj_in[:, :, :]),
                              (aw_raw[:, :, :], aw_in[:, :, :]),
                              (bw_raw[:, :, :, :], bw_in[:, :, :, :]),
                              (wg_raw[:, :], wgate_in[:, :]),
                              (wthr_raw[:, :], wthr_in[:, :]),
                              (bbase_sb[:, :], bbase_in[:, :]),
                          ][:int(os.environ.get("KL1", "99"))]:
                              R_g.issue(g, dst, src)
                          R_g.wait_all(g)

                      @blk.sync
                      def _(sync):
                          R_ld.issue(sync, alog_raw[:, :], alog_in[:, :])
                          R_ld.wait_all(sync)

                  stop_after(-2)
                  jobs = []
                  for ec in range(3):
                      er = EW3[ec]
                      for i in range(NDT):
                          jobs.append((xp_raw[:, ec, i * 128:(i + 1) * 128],
                                       xprojT[:, i, ec * 128:ec * 128 + er], None))
                  for i in range(NDT):
                      for rc in range(RC):
                          jobs.append((dtp_raw[:, i, rc * 128:(rc + 1) * 128],
                                       dtprojT[:, rc, i, :], ("pp", dtlnw_sb, rc)))
                  for k in range(KH):
                      jobs.append((aw_raw[:, k, :], awT[:, k, :], None))
                  for ob in range(KH):
                      jobs.append((bw_raw[:, ob, :, :].rearrange("p a b -> p (a b)"),
                                   bwT[:, ob * 128:(ob + 1) * 128], ("c", 2.0)))
                  for k in range(KH):
                      jobs.append((wg_raw[:, k * 128:(k + 1) * 128],
                                   wg9T[:, k, 0:E], None))
                  jobs = jobs[:int(os.environ.get("KJOBS", "9999"))]
                  NJ = len(jobs)
                  pe0, v0, sc0 = s_pe.v, s_v.v, s_sc.v

                  with nc.Block("p0_tr") as blk:
                      @blk.tensor
                      def _(t):
                          for j, (src, dst, scl) in enumerate(jobs):
                              assert src.shape[0] == 128 and src.shape[-1] == 128
                              if j >= 4:
                                  t.wait_ge(s_v.h, v0 + j - 3)
                              sl = j % 4
                              t.transpose(ps[sl][:, :].bitcast(F16)[:, 0:128],
                                          src, id16[:, :]).then_inc(s_pe.h, 1)
                          s_pe.v = pe0 + NJ

                      @blk.scalar
                      def _(s):
                          if os.environ.get("KSC", "1") == "1":
                              s.activation(a_tmp[:, :], alog_raw[:, :],
                                           ACT.Exp).then_inc(s_sc.h, 1)
                          else:
                              s.nop().then_inc(s_sc.h, 1)
                          s_sc.v = sc0 + 1

                      @blk.vector
                      def _(v):
                          for j, (src, dst, scl) in enumerate(jobs):
                              v.wait_ge(s_pe.h, pe0 + j + 1)
                              sl = j % 4
                              blkap = ps[sl][:, :].bitcast(F16)[:, 0:dst.shape[-1]]
                              if scl is None:
                                  v.tensor_copy(dst, blkap).then_inc(s_v.h, 1)
                              elif scl[0] == "c":
                                  v.tensor_scalar(dst, blkap, scl[1], None,
                                                  ALU.mult).then_inc(s_v.h, 1)
                              else:
                                  _, tsb, rc = scl
                                  v.tensor_scalar(dst, blkap, tsb[:, rc:rc + 1], None,
                                                  ALU.mult).then_inc(s_v.h, 1)
                          if os.environ.get("KVT", "1") == "1":
                              for k in range(KH):
                                  v.tensor_copy(wg9T[:, k, E:E + 1], wthr_raw[:, k:k + 1])
                              v.tensor_scalar(dtbias_ng[:, :], dtbias_sb[:, :], -1.0, None,
                                              ALU.mult)
                              v.tensor_scalar(bthr_ng[:, :], bthr_bc[:, :], -1.0, None,
                                              ALU.mult)
                              v.wait_ge(s_sc.h, sc0 + 1)
                              v.tensor_scalar(a_sb[:, :], a_tmp[:, :], -1.0, None, ALU.mult)
                          s_v.v = v0 + NJ

              stop_after(1)
              with (
                  nc.sbuf_tensor("xt1", [128, H], F32) as xt1,
                  nc.sbuf_tensor("sq1", [128, H], F32) as sq1,
                  nc.sbuf_tensor("xn16", [128, H], F16) as xn16,
                  nc.sbuf_tensor("mbuf", [128, 8], F32) as mbuf,
              ):
                  for tt in range(NT):
                      v0, sc0, pe0 = s_v.v, s_sc.v, s_pe.v
                      hx = {}
                      with nc.Block(f"p1_{tt}") as blk:
                          @blk.sync
                          def _(sync):
                              hx["x"] = R_ld.issue(sync, xt1[:, :],
                                                   x_in[tt * 128:(tt + 1) * 128, :])

                          @blk.scalar
                          def _(s):
                              s.wait_ge(*hx["x"])
                              s.activation(sq1[:, :], xt1[:, :],
                                           ACT.Square).then_inc(s_sc.h, 1)
                              s.wait_ge(s_v.h, v0 + 2)
                              edge(s, s.activation(mbuf[:, 6:7], mbuf[:, 5:6], ACT.Ln,
                                                   bias=float(EPS_LN)))
                              s.activation(mbuf[:, 7:8], mbuf[:, 6:7], ACT.Exp,
                                           scale=-0.5).then_inc(s_sc.h, 1)
                              s_sc.v = sc0 + 2

                          @blk.vector
                          def _(v):
                              v.wait_ge(*hx["x"])
                              v.tensor_reduce(mbuf[:, 0:1], xt1[:, :], AXL.X, ALU.add)
                              v.wait_ge(s_sc.h, sc0 + 1)
                              v.tensor_reduce(mbuf[:, 1:2], sq1[:, :], AXL.X, ALU.add
                                              ).then_inc(s_v.h, 1)
                              v.wait_ge(s_v.h, v0 + 1)
                              edge(v, v.tensor_scalar(mbuf[:, 2:3], mbuf[:, 0:1], 1.0 / H, None, ALU.mult))
                              edge(v, v.tensor_scalar(mbuf[:, 3:4], mbuf[:, 1:2], 1.0 / H, None, ALU.mult))
                              edge(v, v.tensor_tensor(mbuf[:, 4:5], mbuf[:, 2:3], mbuf[:, 2:3], ALU.mult))
                              v.tensor_tensor(mbuf[:, 5:6], mbuf[:, 3:4], mbuf[:, 4:5],
                                              ALU.subtract).then_inc(s_v.h, 1)
                              v.wait_ge(s_v.h, v0 + 2)
                              v.wait_ge(s_sc.h, sc0 + 2)
                              v.tensor_scalar(xn16[:, :], xt1[:, :],
                                              mbuf[:, 2:3], mbuf[:, 7:8],
                                              ALU.subtract, ALU.mult).then_inc(s_v.h, 1)
                              for kk in range(KH):
                                  v.wait_ge(s_pe.h, pe0 + kk + 1)
                                  sl = kk % 4
                                  v.tensor_copy(xnT[:, kk, tt * 128:(tt + 1) * 128],
                                                ps[sl][:, :].bitcast(F16)[:, 0:128]
                                                ).then_inc(s_v.h, 1)
                              s_v.v = v0 + 3 + KH

                          @blk.tensor
                          def _(t):
                              t.wait_ge(s_v.h, v0 + 3)
                              for kk in range(KH):
                                  if kk >= 4:
                                      t.wait_ge(s_v.h, v0 + 3 + kk - 3)
                                  t.transpose(ps[kk % 4][:, :].bitcast(F16)[:, 0:128],
                                              xn16[:, kk * 128:(kk + 1) * 128],
                                              id16[:, :]).then_inc(s_pe.h, 1)
                              s_pe.v = pe0 + KH

              stop_after(2)
              with (
                  nc.sbuf_tensor("slab", [128, 2, KH, 512], F16) as slab,
                  nc.sbuf_tensor("stage2", [128, 2, TPC], F16) as stage2,
              ):
                  v0, g0, pe0 = s_v.v, s_g.v, s_pe.v
                  hslab = {}
                  hspill = {}
                  with nc.Block("p2") as blk:
                      @blk.gpsimd
                      def _(g):
                          for cc in range(NCC):
                              if cc >= 2:
                                  g.wait_ge(s_pe.h, pe0 + (cc - 1) * 4 * TH)
                              for k in range(KH):
                                  hslab[(cc, k)] = R_g.issue(
                                      g, slab[:, cc % 2, k, :],
                                      win_in[k * 128:(k + 1) * 128,
                                             cc * 512:(cc + 1) * 512])

                      @blk.sync
                      def _(sync):
                          for j2 in range(NCC * 4):
                              sync.wait_ge(s_v.h, v0 + (j2 + 1) * TH)
                              col0 = j2 * 128
                              if col0 < DIN:
                                  dst = xh_sp[col0:col0 + 128, :]
                              else:
                                  dst = z_sp[col0 - DIN:col0 - DIN + 128, :]
                              hspill[j2] = R_ld.issue(sync, dst, stage2[:, j2 % 2, :])

                      @blk.tensor
                      def _(t):
                          for cc in range(NCC):
                              for k in range(KH):
                                  t.wait_ge(*hslab[(cc, k)])
                              for ct in range(4):
                                  for th in range(TH):
                                      idx = (cc * 4 + ct) * TH + th
                                      bank = idx % 4
                                      if idx >= 4:
                                          t.wait_ge(s_v.h, v0 + idx - 3)
                                      for k in range(KH):
                                          mm = t.matmul(ps[bank][:, :],
                                                        slab[:, cc % 2, k, ct * 128:(ct + 1) * 128],
                                                        xnT[:, k, th * 512:(th + 1) * 512],
                                                        start=(k == 0), stop=(k == KH - 1))
                                      mm.then_inc(s_pe.h, 1)
                          s_pe.v = pe0 + NCC * 4 * TH

                      @blk.vector
                      def _(v):
                          for cc in range(NCC):
                              for ct in range(4):
                                  j2 = cc * 4 + ct
                                  for th in range(TH):
                                      idx = j2 * TH + th
                                      v.wait_ge(s_pe.h, pe0 + idx + 1)
                                      if th == 0 and j2 >= 2:
                                          v.wait_ge(*hspill[j2 - 2])
                                      v.tensor_copy(stage2[:, j2 % 2, th * 512:(th + 1) * 512],
                                                    ps[idx % 4][:, :]).then_inc(s_v.h, 1)
                          s_v.v = v0 + NCC * 4 * TH

              stop_after(3)
              with nc.Block("coll1") as blk:
                  @blk.gpsimd
                  def _(g):
                      R_ld.wait_all(g)
                      with allow_ici_all_to_all():
                          g.collective_compute(
                              "AllToAll", ALU.bypass, replica_groups=RG,
                              ins=[xh_sp.ap().opt()], outs=[xh_scan.ap().opt()],
                          ).then_inc(s_cc.h, 1)
                      s_cc.v += 1
                      g.wait_ge(s_cc.h, s_cc.v)

              stop_after(4)
              with (
                  nc.sbuf_tensor("xpad", [128, NDT, 3 + TPC], F16) as xpad,
                  nc.sbuf_tensor("xcb", [128, NDT, TPC], F16) as xcb,
                  nc.sbuf_tensor("cva", [128, TPC], F32) as cva,
                  nc.sbuf_tensor("cvb", [128, TPC], F32) as cvb,
                  nc.sbuf_tensor("ssm_ev", [128, 3, TPC], F32) as ssm_ev,
                  nc.sbuf_tensor("ssm_tr", [128, NT, W288], F32) as ssm_tr,
              ):
                  hxcsp_prev = [None] * NDT
                  hssm_prev = None
                  for r in range(NCORES):
                      v0, g0, pe0, sc0 = s_v.v, s_g.v, s_pe.v, s_sc.v
                      halo = (r % CPB != 0)
                      VC = NDT + 3 * TH + NT * 3
                      hload = {}
                      hxcsp = {}
                      hssm = {}
                      hssm_prev_cap = hssm_prev
                      with nc.Block(f"p3_{r}") as blk:
                          @blk.sync
                          def _(sync):
                              for i in range(NDT):
                                  h1 = R_ld.issue(sync, xpad[:, i, 3:],
                                                  xh_scan[r, i * 128:(i + 1) * 128, :])
                                  if halo:
                                      h2 = R_ld.issue(sync, xpad[:, i, 0:3],
                                                      xh_scan[r - 1, i * 128:(i + 1) * 128,
                                                              TPC - 3:TPC])
                                      hload[i] = [h1, h2]
                                  else:
                                      hload[i] = [h1]
                              for i in range(NDT):
                                  sync.wait_ge(s_sc.h, sc0 + i + 1)
                                  hxcsp[i] = R_ld.issue(
                                      sync, xc_sp[i * 128:(i + 1) * 128,
                                                  r * TPC:(r + 1) * TPC],
                                      xcb[:, i, :])
                              sync.wait_ge(s_v.h, v0 + VC)
                              hssm[0] = R_ld.issue(
                                  sync,
                                  ssm_sp[r * TPC:(r + 1) * TPC, :].rearrange(
                                      "(tt p) e -> p tt e", p=128),
                                  ssm_tr[:, :, :])

                          @blk.vector
                          def _(v):
                              for i in range(NDT):
                                  for hh in hload[i]:
                                      v.wait_ge(*hh)
                                  if i >= 1:
                                      v.wait_ge(s_sc.h, sc0 + i)
                                  if r == 0 and i == 0:
                                      v.memset(ssm_ev[:, 2, :], 0.0)
                                  xp = xpad[:, i, :]
                                  if not halo:
                                      edge(v, v.memset(xp[:, 0:3], 0.0))
                                  w = [convw_sb[:, i, kk:kk + 1] for kk in range(KC)]
                                  edge(v, v.tensor_scalar(cva[:, :], xp[:, 0:TPC], w[0],
                                                          convb_sb[:, i:i + 1], ALU.mult, ALU.add))
                                  edge(v, v.scalar_tensor_tensor(cvb[:, :], xp[:, 1:1 + TPC], w[1],
                                                                 cva[:, :], ALU.mult, ALU.add))
                                  edge(v, v.scalar_tensor_tensor(cva[:, :], xp[:, 2:2 + TPC], w[2],
                                                                 cvb[:, :], ALU.mult, ALU.add))
                                  v.scalar_tensor_tensor(cvb[:, :], xp[:, 3:3 + TPC], w[3],
                                                         cva[:, :], ALU.mult, ALU.add
                                                         ).then_inc(s_v.h, 1)
                              v.wait_ge(s_pe.h, pe0 + 1)
                              for ec in range(3):
                                  for th in range(TH):
                                      v.tensor_copy(ssm_ev[0:EW3[ec], ec, th * 512:(th + 1) * 512],
                                                    ps[ec * TH + th][0:EW3[ec], :]
                                                    ).then_inc(s_v.h, 1)
                              if hssm_prev_cap is not None:
                                  v.wait_ge(*hssm_prev_cap)
                              for tt in range(NT):
                                  for ec in range(3):
                                      jj = tt * 3 + ec
                                      v.wait_ge(s_pe.h, pe0 + 1 + jj + 1)
                                      rb = (ps[6][:, 0:EW3[ec]] if jj % 2 == 0 else
                                            pst[:, :].bitcast(F32)[:, 0:EW3[ec]])
                                      v.tensor_copy(ssm_tr[:, tt, ec * 128:ec * 128 + EW3[ec]],
                                                    rb).then_inc(s_v.h, 1)
                              s_v.v = v0 + VC

                          @blk.scalar
                          def _(s):
                              for i in range(NDT):
                                  s.wait_ge(s_v.h, v0 + i + 1)
                                  if hxcsp_prev[i] is not None:
                                      s.wait_ge(*hxcsp_prev[i])
                                  s.activation(xcb[:, i, :], cvb[:, :],
                                               ACT.Silu).then_inc(s_sc.h, 1)
                              s_sc.v = sc0 + NDT

                          @blk.tensor
                          def _(t):
                              for i in range(NDT):
                                  t.wait_ge(s_sc.h, sc0 + i + 1)
                                  for ec in range(3):
                                      for th in range(TH):
                                          mm = t.matmul(ps[ec * TH + th][0:EW3[ec], :],
                                                        xprojT[:, i, ec * 128:ec * 128 + EW3[ec]],
                                                        xcb[:, i, th * 512:(th + 1) * 512],
                                                        start=(i == 0), stop=(i == NDT - 1))
                              mm.then_inc(s_pe.h, 1)
                              for tt in range(NT):
                                  for ec in range(3):
                                      jj = tt * 3 + ec
                                      t.wait_ge(s_v.h, v0 + NDT + ec * TH + TH)
                                      if jj >= 2:
                                          t.wait_ge(s_v.h, v0 + NDT + 3 * TH + jj - 1)
                                      tb = (ps[6][:, 0:128] if jj % 2 == 0 else
                                            pst[:, :].bitcast(F32)[:, 0:128])
                                      t.transpose(
                                          tb,
                                          ssm_ev[:, ec, tt * 128:(tt + 1) * 128],
                                          id32[:, :]
                                      ).then_inc(s_pe.h, 1)
                              s_pe.v = pe0 + 1 + NT * 3
                      hxcsp_prev = [hxcsp[i] for i in range(NDT)]
                      hssm_prev = hssm[0]

              stop_after(5)
              with nc.Block("coll2") as blk:
                  @blk.gpsimd
                  def _(g):
                      R_ld.wait_all(g)
                      g.collective_compute(
                          "ReduceScatter", ALU.add, replica_groups=RG,
                          ins=[ssm_sp.ap().opt()], outs=[ssm_red.ap().opt()],
                      ).then_inc(s_cc.h, 1)
                      s_cc.v += 1
              cc2val = s_cc.v

              stop_after(6)
              with (
                  nc.sbuf_tensor("gl", [128, E + 1], F32) as gl,
                  nc.sbuf_tensor("gw", [128, E], F32) as gw,
                  nc.sbuf_tensor("gwe", [128, E], F32) as gwe,
                  nc.sbuf_tensor("gw3", [128, E], F32) as gw3,
                  nc.sbuf_tensor("gm", [128, 8], F32) as gm,
                  nc.sbuf_tensor("hw16", [128, ERT], F16) as hw16,
              ):
                  for tt in range(NT):
                      v0, sc0, pe0 = s_v.v, s_sc.v, s_pe.v
                      with nc.Block(f"p5a_{tt}") as blk:
                          @blk.tensor
                          def _(t):
                              for k in range(KH):
                                  mm = t.matmul(ps[0][:, 0:E + 1],
                                                xnT[:, k, tt * 128:(tt + 1) * 128],
                                                wg9T[:, k, :],
                                                start=(k == 0), stop=(k == KH - 1))
                              mm.then_inc(s_pe.h, 1)
                              for k in range(KH):
                                  mm = t.matmul(ps[1][:, 0:ERT],
                                                xnT[:, k, tt * 128:(tt + 1) * 128],
                                                awT[:, k, :],
                                                start=(k == 0), stop=(k == KH - 1))
                              mm.then_inc(s_pe.h, 1)
                              t.wait_ge(s_v.h, v0 + 10)
                              t.transpose(pst[:, 0:128], hw16[:, :], id16[:, :]
                                          ).then_inc(s_pe.h, 1)
                              s_pe.v = pe0 + 3

                          @blk.vector
                          def _(v):
                              v.wait_ge(s_pe.h, pe0 + 1)
                              edge(v, v.tensor_copy(gl[:, :], ps[0][:, 0:E + 1]))
                              v.tensor_reduce(gm[:, 0:1], gl[:, 0:E], AXL.X, ALU.max
                                              ).then_inc(s_v.h, 1)
                              v.wait_ge(s_v.h, v0 + 1)
                              v.tensor_scalar(gw[:, :], gl[:, 0:E], gm[:, 0:1], None,
                                              ALU.subtract).then_inc(s_v.h, 1)
                              v.wait_ge(s_sc.h, sc0 + 1)
                              v.tensor_reduce(gm[:, 1:2], gwe[:, :], AXL.X, ALU.add
                                              ).then_inc(s_v.h, 1)
                              v.wait_ge(s_v.h, v0 + 3)
                              v.reciprocal(gm[:, 2:3], gm[:, 1:2])
                              v.tensor_scalar(gm[:, 4:5], gm[:, 3:4], 1.0, None, ALU.add
                                              ).then_inc(s_v.h, 1)
                              v.wait_ge(s_v.h, v0 + 4)
                              v.reciprocal(gm[:, 5:6], gm[:, 4:5]).then_inc(s_v.h, 1)
                              v.wait_ge(s_v.h, v0 + 5)
                              v.tensor_scalar(gm[:, 6:7], gm[:, 5:6], MAX_THR, None,
                                              ALU.mult).then_inc(s_v.h, 1)
                              v.wait_ge(s_v.h, v0 + 6)
                              edge(v, v.tensor_scalar(gw3[:, :], gwe[:, :], gm[:, 2:3], gm[:, 6:7],
                                                      ALU.mult, ALU.subtract))
                              edge(v, v.tensor_scalar_max(gw[:, :], gw3[:, :], 0.0))
                              v.tensor_reduce(gm[:, 1:2], gw[:, :], AXL.X, ALU.add
                                              ).then_inc(s_v.h, 1)
                              v.wait_ge(s_v.h, v0 + 7)
                              edge(v, v.tensor_single_scalar(gm[:, 4:5], gm[:, 1:2], 0.0,
                                                             ALU.is_equal))
                              v.tensor_tensor(gm[:, 5:6], gm[:, 1:2], gm[:, 4:5], ALU.add
                                              ).then_inc(s_v.h, 1)
                              v.wait_ge(s_v.h, v0 + 8)
                              v.reciprocal(gm[:, 2:3], gm[:, 5:6]).then_inc(s_v.h, 1)
                              v.wait_ge(s_v.h, v0 + 9)
                              v.wait_ge(s_pe.h, pe0 + 2)
                              for e in range(E):
                                  hh16 = v.tensor_scalar(hw16[:, e * RR:(e + 1) * RR],
                                                         ps[1][:, e * RR:(e + 1) * RR],
                                                         gw[:, e:e + 1], gm[:, 2:3],
                                                         ALU.mult, ALU.mult)
                              hh16.then_inc(s_v.h, 1)
                              v.wait_ge(s_pe.h, pe0 + 3)
                              v.tensor_copy(hwT[:, tt * 128:(tt + 1) * 128], pst[:, 0:128])
                              s_v.v = v0 + 10

                          @blk.scalar
                          def _(s):
                              s.wait_ge(s_v.h, v0 + 2)
                              s.activation(gwe[:, :], gw[:, :], ACT.Exp)
                              s.activation(gm[:, 3:4], gl[:, E:E + 1], ACT.Exp,
                                           scale=-1.0, bias=bthr_ng[:, 0:1]
                                           ).then_inc(s_sc.h, 1)
                              s_sc.v = sc0 + 1

              stop_after(9)
              with (
                  nc.sbuf_tensor("wslab", [128, 2, KH, 512], F16) as wslab,
                  nc.sbuf_tensor("gstage", [128, 2, 512], F16) as gstage,
              ):
                  hgsp_prev = None
                  hwb = {}
                  for oc in range(H // 512):
                      v0, g0, pe0, sc0 = s_v.v, s_g.v, s_pe.v, s_sc.v
                      hgsp = {}
                      hgsp_prev_cap = hgsp_prev
                      with nc.Block(f"p5b_{oc}") as blk:
                          @blk.gpsimd
                          def _(g):
                              for ld_oc in [x for x in
                                            ([0, 1] if oc == 0 else [oc + 1])
                                            if x < H // 512]:
                                  if ld_oc >= 2:
                                      g.wait_ge(s_pe.h, pe0)
                                  hwb[ld_oc] = R_g.issue(
                                      g, wslab[:, ld_oc % 2, :, :],
                                      wbase_in[:, ld_oc * 512:(ld_oc + 1) * 512]
                                      .rearrange("(k p) c -> p k c", p=128))

                          @blk.sync
                          def _(sync):
                              for tt in range(NT):
                                  sync.wait_ge(s_sc.h, sc0 + tt + 1)
                                  hgsp[tt] = R_ld.issue(
                                      sync, g_sp[tt * 128:(tt + 1) * 128,
                                                 oc * 512:(oc + 1) * 512],
                                      gstage[:, tt % 2, :])

                          @blk.tensor
                          def _(t):
                              t.wait_ge(*hwb[oc])
                              for tt in range(NT):
                                  if tt >= 2:
                                      t.wait_ge(s_sc.h, sc0 + tt - 1)
                                  bank = tt % 2
                                  for k in range(KH):
                                      t.matmul(ps[bank][:, :],
                                               xnT[:, k, tt * 128:(tt + 1) * 128],
                                               wslab[:, oc % 2, k, :],
                                               start=(k == 0), stop=False)
                                  t.matmul(ps[bank][:, :], id16[:, :],
                                           bbase_sb[:, oc * 512:(oc + 1) * 512],
                                           start=False, stop=False)
                                  t.matmul(ps[bank][:, :],
                                           hwT[:, tt * 128:(tt + 1) * 128],
                                           bwT[:, oc * 512:(oc + 1) * 512],
                                           start=False, stop=True).then_inc(s_pe.h, 1)
                              s_pe.v = pe0 + NT

                          @blk.scalar
                          def _(s):
                              if hgsp_prev_cap is not None:
                                  for hh in list(hgsp_prev_cap.values())[-2:]:
                                      s.wait_ge(*hh)
                              for tt in range(NT):
                                  s.wait_ge(s_pe.h, pe0 + tt + 1)
                                  if tt >= 2:
                                      s.wait_ge(*hgsp[tt - 2])
                                  s.activation(gstage[:, tt % 2, :], ps[tt % 2][:, :],
                                               ACT.Silu).then_inc(s_sc.h, 1)
                              s_sc.v = sc0 + NT
                      hgsp_prev = hgsp

              with (
                  nc.sbuf_tensor("p4x", [128, W288], F32) as p4x,
                  nc.sbuf_tensor("p4sq", [128, W288], F32) as p4sq,
                  nc.sbuf_tensor("p4o", [128, W288], F16) as p4o,
                  nc.sbuf_tensor("p4m", [128, 8], F32) as p4m,
                  nc.sbuf_tensor("p4st", [128, 3 * 128], F16) as p4st,
              ):
                  GRP = [(0, R, R), (R, R + N, N), (R + N, W288, N)]
                  hst_prev = None
                  for tt in range(NT):
                      v0, sc0, pe0 = s_v.v, s_sc.v, s_pe.v
                      hx4 = {}
                      hst4 = {}
                      hst_prev_cap = hst_prev
                      with nc.Block(f"p4_{tt}") as blk:
                          @blk.sync
                          def _(sync):
                              if tt == 0:
                                  sync.wait_ge(s_cc.h, cc2val)
                              hx4["x"] = R_ld.issue(sync, p4x[:, :],
                                                    ssm_red[tt * 128:(tt + 1) * 128, :])
                              sync.wait_ge(s_v.h, v0 + 2 + 3)
                              for ec in range(3):
                                  hst4[ec] = R_ld.issue(
                                      sync,
                                      dtrT_loc[ec * 128:ec * 128 + EW3[ec],
                                               tt * 128:(tt + 1) * 128],
                                      p4st[0:EW3[ec], ec * 128:(ec + 1) * 128])

                          @blk.scalar
                          def _(s):
                              s.wait_ge(*hx4["x"])
                              s.activation(p4sq[:, :], p4x[:, :],
                                           ACT.Square).then_inc(s_sc.h, 1)
                              s.wait_ge(s_v.h, v0 + 1)
                              for gi, (a0, b0, w0) in enumerate(GRP):
                                  edge(s, s.activation(p4m[:, 6:7], p4m[:, gi:gi + 1],
                                                       ACT.Ln, bias=float(EPS_RMS), scale=1.0 / w0))
                                  ii = s.activation(p4m[:, 3 + gi:4 + gi],
                                                    p4m[:, 6:7],
                                                    ACT.Exp, scale=-0.5)
                                  if gi < 2:
                                      edge(s, ii)
                              ii.then_inc(s_sc.h, 1)
                              s_sc.v = sc0 + 2

                          @blk.vector
                          def _(v):
                              if hst_prev_cap is not None:
                                  for hh in hst_prev_cap.values():
                                      v.wait_ge(*hh)
                              v.wait_ge(s_sc.h, sc0 + 1)
                              for gi, (a0, b0, w0) in enumerate(GRP):
                                  rr = v.tensor_reduce(p4m[:, gi:gi + 1], p4sq[:, a0:b0],
                                                       AXL.X, ALU.add)
                              rr.then_inc(s_v.h, 1)
                              v.wait_ge(s_sc.h, sc0 + 2)
                              v.wait_ge(s_v.h, v0 + 1)
                              for gi, (a0, b0, w0) in enumerate(GRP):
                                  vv = v.tensor_scalar(p4o[:, a0:b0], p4x[:, a0:b0],
                                                       p4m[:, 3 + gi:4 + gi], None,
                                                       ALU.mult)
                              vv.then_inc(s_v.h, 1)
                              for ec in range(3):
                                  v.wait_ge(s_pe.h, pe0 + ec + 1)
                                  v.tensor_copy(p4st[0:EW3[ec], ec * 128:(ec + 1) * 128],
                                                ps[ec][:, :].bitcast(F16)[0:EW3[ec], 0:128]
                                                ).then_inc(s_v.h, 1)
                              s_v.v = v0 + 5

                          @blk.tensor
                          def _(t):
                              t.wait_ge(s_v.h, v0 + 2)
                              for ec in range(3):
                                  t.transpose(ps[ec][:, :].bitcast(F16)[0:EW3[ec], 0:128],
                                              p4o[:, ec * 128:ec * 128 + EW3[ec]],
                                              id16[:, :]).then_inc(s_pe.h, 1)
                              s_pe.v = pe0 + 3
                      hst_prev = hst4

              stop_after(7)
              with nc.Block("coll3") as blk:
                  @blk.gpsimd
                  def _(g):
                      R_ld.wait_all(g)
                      g.collective_compute(
                          "AllGather", ALU.bypass, replica_groups=RG,
                          ins=[dtrT_loc.ap().opt()], outs=[dtr_all.ap().opt()],
                      ).then_inc(s_cc.h, 1)
                      s_cc.v += 1
                      g.wait_ge(s_cc.h, s_cc.v)

              stop_after(8)

          stop_after(10)
          ctx6 = contextlib.ExitStack()
          _e6 = ctx6.enter_context
          dtrT_b = _e6(nc.sbuf_tensor("dtrT_b", [128, RC, LB], F16))
          bcT_raw = _e6(nc.sbuf_tensor("bcT_raw", [128, LB], F16))
          bcT_b = _e6(nc.sbuf_tensor("bcT_b", [128, LB], F16))
          st_sb = _e6(nc.sbuf_tensor("st_sb", [128, NDT * N], F32))
          dt_sb = _e6(nc.sbuf_tensor("dt_sb", [128, NDT, CW], F16))
          dtx_sb = _e6(nc.sbuf_tensor("dtx_sb", [128, NDT, CW], F16))
          xc_sb = _e6(nc.sbuf_tensor("xc_sb", [128, NDT, CW], F16))
          ys_sb = _e6(nc.sbuf_tensor("ys_sb", [128, 2, NDT, CW], F16))
          bbc = _e6(nc.sbuf_tensor("bbc", [128, CW], F16))
          cbc = _e6(nc.sbuf_tensor("cbc", [128, CW], F16))
          dabuf = _e6(nc.sbuf_tensor("dabuf", [128, 2, CW], F16))
          dbxbuf = _e6(nc.sbuf_tensor("dbxbuf", [128, 2, CW], F16))
          hbuf = _e6(nc.sbuf_tensor("hbuf", [128, CW], F16))
          tmpbuf = _e6(nc.sbuf_tensor("tmpbuf", [128, CW], F16))
          ybuf = _e6(nc.sbuf_tensor("ybuf", [128, 2, CW], F16))
          sp_a = _e6(nc.sbuf_tensor("sp_a", [128, 512], F32))
          sp_b = _e6(nc.sbuf_tensor("sp_b", [128, 512], F32))
          sp_u = _e6(nc.sbuf_tensor("sp_u", [128, 512], F32))
          sp_d = _e6(nc.sbuf_tensor("sp_d", [128, 512], F32))
          sp_e = _e6(nc.sbuf_tensor("sp_e", [128, 512], F32))
          with ctx6:
              hy6_prev = {}
              for b in range(NB):
                  v0 = s_v.v
                  h6l = []
                  with nc.Block(f"p6load_{b}") as blk:
                      @blk.sync
                      def _(sync):
                          for j in range(CPB):
                              src = dtr_all[b * CPB + j, :, :]
                              h6l.append(R_ld.issue(sync, dtrT_b[:, 0, j * TPC:(j + 1) * TPC],
                                                    src[0:128, :]))
                              h6l.append(R_ld.issue(sync, dtrT_b[:, 1, j * TPC:(j + 1) * TPC],
                                                    src[128:256, :]))
                              h6l.append(R_ld.issue(sync, bcT_raw[0:BCW, j * TPC:(j + 1) * TPC],
                                                    src[256:W288, :]))

                      @blk.vector
                      def _(v):
                          if b == 0:
                              edge(v, v.memset(bcT_raw[32:64, :], 0.0))
                              edge(v, v.memset(bcT_raw[64:128, :], 0.0))
                          for hh in h6l:
                              v.wait_ge(*hh)
                          v.tensor_scalar(bcT_b[:, :], bcT_raw[:, :],
                                          bclnw_sb[:, 0:1], None, ALU.mult)
                          v.memset(st_sb[:, :], 0.0)
                          s_v.v = v0

                  for tc in range(NTC):
                      v0, pe0, sc0, g0 = s_v.v, s_pe.v, s_sc.v, s_g.v
                      hxc6 = {}
                      hy6 = {}
                      hy6_prev_cap = dict(hy6_prev)
                      gtc = b * CPB + (tc * CW) // TPC
                      toff = b * LB + tc * CW
                      VS1 = NDT * (C6 * 2 + 1)
                      VPN = 1 + NDT
                      VSN = N * VPN

                      def scanmark(kk):
                          return (kk // NDT) * VPN + 1 + (kk % NDT) + 1

                      with nc.Block(f"p6_{b}_{tc}") as blk:
                          @blk.gpsimd
                          def _(g):
                              for n in range(N):
                                  for i in range(NDT):
                                      kk = n * NDT + i
                                      g.wait_ge(s_v.h, v0 + VS1 + n * VPN + 1)
                                      if kk >= 2:
                                          g.wait_ge(s_v.h,
                                                    v0 + VS1 + scanmark(kk - 2))
                                      g.tensor_tensor(dbxbuf[:, kk % 2, :],
                                                      dtx_sb[:, i, :], bbc[:, :],
                                                      ALU.mult).then_inc(s_g.h, 1)
                              s_g.v = g0 + N * NDT

                          @blk.sync
                          def _(sync):
                              for i in range(NDT):
                                  hxc6[i] = R_ld.issue(
                                      sync, xc_sb[:, i, :],
                                      xc_sp[i * 128:(i + 1) * 128, toff:toff + CW])
                              for i in range(NDT):
                                  sync.wait_ge(s_v.h, v0 + VS1 + VSN + i + 1)
                                  hy6[i] = R_ld.issue(
                                      sync,
                                      y_sp[gtc, i * 128:(i + 1) * 128,
                                           (tc * CW) % TPC:(tc * CW) % TPC + CW],
                                      ybuf[:, i % 2, :])

                          @blk.tensor
                          def _(t):
                              for i in range(NDT):
                                  for th in range(C6):
                                      if i >= 1:
                                          t.wait_ge(s_sc.h, sc0 + 2 * ((i - 1) * C6 + th) + 1)
                                      for rc in range(RC):
                                          mm = t.matmul(ps[th][:, :],
                                                        dtprojT[:, rc, i, :],
                                                        dtrT_b[:, rc, tc * CW + th * 512:
                                                               tc * CW + (th + 1) * 512],
                                                        start=(rc == 0), stop=(rc == RC - 1))
                                      mm.then_inc(s_pe.h, 1)
                              for n in range(N):
                                  if n >= 1:
                                      t.wait_ge(s_v.h, v0 + VS1 + (n - 1) * VPN + 1)
                                  for th in range(C6):
                                      rhs6 = bcT_b[:, tc * CW + th * 512:
                                                   tc * CW + (th + 1) * 512]
                                      t.matmul(ps[2 + th][:, :],
                                               selmat[:, n * 128:(n + 1) * 128],
                                               rhs6, start=True, stop=True)
                                      mm = t.matmul(ps[4 + th][:, :],
                                                    selmat[:, (N + n) * 128:(N + n + 1) * 128],
                                                    rhs6, start=True, stop=True)
                                  mm.then_inc(s_pe.h, 1)
                              s_pe.v = pe0 + NDT * C6 + N

                          @blk.scalar
                          def _(s):
                              for i in range(NDT):
                                  for th in range(C6):
                                      kk1 = i * C6 + th
                                      s.wait_ge(s_pe.h, pe0 + kk1 + 1)
                                      if kk1 >= 1:
                                          pi, pth = (i, th - 1) if th >= 1 else (i - 1, C6 - 1)
                                          s.wait_ge(s_v.h, v0 + pi * (2 * C6 + 1) + 2 * pth + 2)
                                      s.activation(sp_a[:, :], ps[th][:, :], ACT.Relu,
                                                   bias=dtbias_sb[:, i:i + 1])
                                      s.activation(sp_b[:, :], ps[th][:, :], ACT.Relu,
                                                   scale=-1.0, bias=dtbias_ng[:, i:i + 1]
                                                   ).then_inc(s_sc.h, 1)
                                      s.wait_ge(s_v.h, v0 + i * (2 * C6 + 1) + 2 * th + 1)
                                      edge(s, s.activation(sp_d[:, :], sp_u[:, :], ACT.Exp,
                                                           scale=-1.0))
                                      s.activation(sp_e[:, :], sp_d[:, :], ACT.Ln,
                                                   bias=1.0).then_inc(s_sc.h, 1)
                              for n in range(N):
                                  for i in range(NDT):
                                      kk = n * NDT + i
                                      s.wait_ge(s_v.h, v0 + i * (2 * C6 + 1) + 2 * C6)
                                      if kk >= 2:
                                          s.wait_ge(s_v.h,
                                                    v0 + VS1 + scanmark(kk - 2))
                                      s.activation(dabuf[:, kk % 2, :], dt_sb[:, i, :],
                                                   ACT.Exp,
                                                   scale=a_sb[:, i * N + n:i * N + n + 1]
                                                   ).then_inc(s_sc.h, 1)
                              s_sc.v = sc0 + NDT * C6 * 2 + N * NDT

                          @blk.vector
                          def _(v):
                              for i in range(NDT):
                                  for th in range(C6):
                                      v.wait_ge(s_sc.h, sc0 + 2 * (i * C6 + th) + 1)
                                      v.tensor_tensor(sp_u[:, :], sp_a[:, :], sp_b[:, :],
                                                      ALU.add).then_inc(s_v.h, 1)
                                      v.wait_ge(s_sc.h, sc0 + 2 * (i * C6 + th) + 2)
                                      v.wait_ge(s_v.h, v0 + i * (2 * C6 + 1) + 2 * th + 1)
                                      v.tensor_tensor(dt_sb[:, i, th * 512:(th + 1) * 512],
                                                      sp_a[:, :], sp_e[:, :], ALU.add
                                                      ).then_inc(s_v.h, 1)
                                  v.wait_ge(*hxc6[i])
                                  v.wait_ge(s_v.h, v0 + i * (2 * C6 + 1) + 2 * C6)
                                  v.tensor_tensor(dtx_sb[:, i, :], dt_sb[:, i, :],
                                                  xc_sb[:, i, :], ALU.mult
                                                  ).then_inc(s_v.h, 1)
                              for n in range(N):
                                  v.wait_ge(s_pe.h, pe0 + NDT * C6 + n + 1)
                                  if n >= 1:
                                      v.wait_ge(s_v.h, v0 + VS1 + n * VPN)
                                      v.wait_ge(s_g.h, g0 + n * NDT)
                                  for th in range(C6):
                                      v.tensor_copy(bbc[:, th * 512:(th + 1) * 512],
                                                    ps[2 + th][:, :])
                                      cpop = v.tensor_copy(cbc[:, th * 512:(th + 1) * 512],
                                                           ps[4 + th][:, :])
                                  cpop.then_inc(s_v.h, 1)
                                  for i in range(NDT):
                                      kk = n * NDT + i
                                      v.wait_ge(s_sc.h,
                                                sc0 + NDT * C6 * 2 + kk + 1)
                                      v.wait_ge(s_v.h, v0 + VS1 + n * VPN + 1)
                                      v.wait_ge(s_g.h, g0 + kk + 1)
                                      edge(v, v.tensor_tensor_scan(hbuf[:, :], dabuf[:, kk % 2, :],
                                                                   dbxbuf[:, kk % 2, :],
                                                                   st_sb[:, i * N + n:i * N + n + 1],
                                                                   ALU.mult, ALU.add))
                                      v.tensor_copy(st_sb[:, i * N + n:i * N + n + 1],
                                                    hbuf[:, CW - 1:CW])
                                      if n == 0:
                                          v.tensor_tensor(ys_sb[:, 0, i, :], hbuf[:, :],
                                                          cbc[:, :], ALU.mult
                                                          ).then_inc(s_v.h, 1)
                                      else:
                                          edge(v, v.tensor_tensor(tmpbuf[:, :], hbuf[:, :],
                                                                  cbc[:, :], ALU.mult))
                                          v.wait_ge(s_v.h,
                                                    v0 + VS1 + (n - 1) * VPN + 1 + i + 1)
                                          v.tensor_tensor(ys_sb[:, n % 2, i, :],
                                                          ys_sb[:, (n - 1) % 2, i, :],
                                                          tmpbuf[:, :], ALU.add
                                                          ).then_inc(s_v.h, 1)
                              for i in range(NDT):
                                  v.wait_ge(s_v.h, v0 + VS1 + (N - 1) * VPN + 1 + i + 1)
                                  if i >= 2:
                                      v.wait_ge(*hy6[i - 2])
                                  elif i in hy6_prev_cap:
                                      v.wait_ge(*hy6_prev_cap[i])
                                  v.scalar_tensor_tensor(ybuf[:, i % 2, :],
                                                         xc_sb[:, i, :],
                                                         dparam_sb[:, i:i + 1],
                                                         ys_sb[:, (N - 1) % 2, i, :],
                                                         ALU.mult, ALU.add
                                                         ).then_inc(s_v.h, 1)
                              s_v.v = v0 + VS1 + VSN + NDT
                      hy6_prev = {}
                      for k in range(min(2, NDT)):
                          ii = NDT - 1 - k
                          hy6_prev[ii % 2] = hy6[ii]

              stop_after(11)
              with nc.Block("coll4") as blk:
                  @blk.gpsimd
                  def _(g):
                      R_ld.wait_all(g)
                      with allow_ici_all_to_all():
                          g.collective_compute(
                              "AllToAll", ALU.bypass, replica_groups=RG,
                              ins=[y_sp.ap().opt()], outs=[y_own.ap().opt()],
                          ).then_inc(s_cc.h, 1)
                      s_cc.v += 1
                      g.wait_ge(s_cc.h, s_cc.v)

          stop_after(12)
          with nc.sbuf_tensor("y_sb", [128, NDG, TPC], F16) as y_sb:
            with (
              nc.sbuf_tensor("yt7", [128, 2, TPC], F16) as yt7,
              nc.sbuf_tensor("zt7", [128, 2, TPC], F16) as zt7,
              nc.sbuf_tensor("zs7", [128, 2, TPC], F16) as zs7,
            ):
              v0, sc0 = s_v.v, s_sc.v
              h7 = {}
              with nc.Block("p7a") as blk:
                  @blk.sync
                  def _(sync):
                      for dtg in range(NDG):
                          if dtg >= 2:
                              sync.wait_ge(s_v.h, v0 + dtg - 1)
                          rsl, roff = divmod(dtg, NDT)
                          hy = R_ld.issue(sync, yt7[:, dtg % 2, :],
                                          y_own[rsl, roff * 128:(roff + 1) * 128, :])
                          hz = R_ld.issue(sync, zt7[:, dtg % 2, :],
                                          z_sp[dtg * 128:(dtg + 1) * 128, :])
                          h7[dtg] = (hy, hz)

                  @blk.scalar
                  def _(s):
                      for dtg in range(NDG):
                          s.wait_ge(*h7[dtg][0])
                          s.wait_ge(*h7[dtg][1])
                          if dtg >= 2:
                              s.wait_ge(s_v.h, v0 + dtg - 1)
                          s.activation(zs7[:, dtg % 2, :], zt7[:, dtg % 2, :],
                                       ACT.Silu).then_inc(s_sc.h, 1)
                      s_sc.v = sc0 + NDG

                  @blk.vector
                  def _(v):
                      for dtg in range(NDG):
                          v.wait_ge(s_sc.h, sc0 + dtg + 1)
                          v.tensor_tensor(y_sb[:, dtg, :], yt7[:, dtg % 2, :],
                                          zs7[:, dtg % 2, :], ALU.mult
                                          ).then_inc(s_v.h, 1)
                      s_v.v = v0 + NDG

            with (
                  nc.sbuf_tensor("wA7", [128, 2, NDG // 2, HCH], F16) as wA7,
                  nc.sbuf_tensor("wB7", [128, NDG // 2, HCH], F16) as wB7,
                  nc.sbuf_tensor("gtile", [128, 2, HCH], F16) as gtile,
                  nc.sbuf_tensor("xtile", [128, 2, HCH], F32) as xtile,
                  nc.sbuf_tensor("sgbuf", [128, HCH], F32) as sgbuf,
                  nc.sbuf_tensor("otile", [128, 2, HCH], F32) as otile,
            ):
                  NDH = NDG // 2
                  hwoA = {}
                  hwoB = {}
                  hout_prev = None

                  def _wslice(dh, hc2):
                      return wout_in[dh * (DIN // 2):(dh + 1) * (DIN // 2),
                                     hc2 * HCH:(hc2 + 1) * HCH].rearrange(
                                         "(g2 p) c -> p g2 c", p=128)

                  v_base, pe_base = s_v.v, s_pe.v
                  hgx = {}
                  hout = {}
                  with nc.Block("p7b") as blk:
                      @blk.gpsimd
                      def _(g):
                          for hc in range(NHC):
                              pe0 = pe_base + hc * NT
                              if hc == 0:
                                  hwoA[0] = R_g.issue(g, wA7[:, 0, :, :],
                                                      _wslice(0, 0))
                                  hwoB[0] = R_g.issue(g, wB7[:, :, :],
                                                      _wslice(1, 0))
                                  hwoA[1] = R_g.issue(g, wA7[:, 1, :, :],
                                                      _wslice(0, 1))
                              else:
                                  g.wait_ge(s_pe.h, pe0)
                                  hwoB[hc] = R_g.issue(g, wB7[:, :, :],
                                                       _wslice(1, hc))
                                  if hc + 1 < NHC:
                                      hwoA[hc + 1] = R_g.issue(
                                          g, wA7[:, (hc + 1) % 2, :, :],
                                          _wslice(0, hc + 1))

                      @blk.sync
                      def _(sync):
                          for hc in range(NHC):
                              v0 = v_base + hc * 2 * NT
                              for tt in range(NT):
                                  if tt >= 2:
                                      sync.wait_ge(s_v.h, v0 + 2 * (tt - 1))
                                  elif hc >= 1:
                                      sync.wait_ge(s_v.h, v_base + (hc - 1) * 2 * NT
                                                   + 2 * (NT - 2 + tt) + 1)
                                  hg = R_ld.issue(sync, gtile[:, tt % 2, :],
                                                  g_sp[tt * 128:(tt + 1) * 128,
                                                       hc * HCH:(hc + 1) * HCH])
                                  hx = R_ld.issue(sync, xtile[:, tt % 2, :],
                                                  x_in[tt * 128:(tt + 1) * 128,
                                                       hc * HCH:(hc + 1) * HCH])
                                  hgx[(hc, tt)] = (hg, hx)
                                  sync.wait_ge(s_v.h, v0 + 2 * (tt + 1))
                                  hout[(hc, tt)] = R_st.issue(
                                      sync, out_ext[tt * 128:(tt + 1) * 128,
                                                    hc * HCH:(hc + 1) * HCH],
                                      otile[:, tt % 2, :])

                      @blk.tensor
                      def _(t):
                          for hc in range(NHC):
                              v0 = v_base + hc * 2 * NT
                              t.wait_ge(*hwoA[hc])
                              for tt in range(NT):
                                  if tt >= 2:
                                      t.wait_ge(s_v.h, v0 + 2 * (tt - 1))
                                  elif hc >= 1:
                                      t.wait_ge(s_v.h, v_base + (hc - 1) * 2 * NT
                                                + 2 * (NT - 2 + tt) + 1)
                                  bank = tt % 2
                                  for dtg in range(NDG):
                                      if tt == 0 and dtg == NDH:
                                          t.wait_ge(*hwoB[hc])
                                      mov = (wA7[:, hc % 2, dtg, :] if dtg < NDH
                                             else wB7[:, dtg - NDH, :])
                                      mm = t.matmul(ps[bank][:, 0:HCH],
                                                    y_sb[:, dtg, tt * 128:(tt + 1) * 128],
                                                    mov,
                                                    start=(dtg == 0), stop=(dtg == NDG - 1))
                                  mm.then_inc(s_pe.h, 1)

                      @blk.vector
                      def _(v):
                          for hc in range(NHC):
                              v0 = v_base + hc * 2 * NT
                              pe0 = pe_base + hc * NT
                              for tt in range(NT):
                                  v.wait_ge(*hgx[(hc, tt)][0])
                                  v.wait_ge(*hgx[(hc, tt)][1])
                                  v.wait_ge(s_pe.h, pe0 + tt + 1)
                                  if tt >= 2:
                                      v.wait_ge(*hout[(hc, tt - 2)])
                                  elif hc >= 1:
                                      v.wait_ge(*hout[(hc - 1, NT - 2 + tt)])
                                  if tt >= 1:
                                      v.wait_ge(s_v.h, v0 + 2 * tt)
                                  v.tensor_tensor(sgbuf[:, :], ps[tt % 2][:, 0:HCH],
                                                  gtile[:, tt % 2, :], ALU.mult
                                                  ).then_inc(s_v.h, 1)
                                  v.wait_ge(s_v.h, v0 + 2 * tt + 1)
                                  v.tensor_tensor(otile[:, tt % 2, :], sgbuf[:, :],
                                                  xtile[:, tt % 2, :], ALU.add
                                                  ).then_inc(s_v.h, 1)
                  s_v.v = v_base + NHC * 2 * NT
                  s_pe.v = pe_base + NHC * NT

          with nc.Block("fin") as blk:
              @blk.sync
              def _(sync):
                  R_st.wait_all(sync)
                  R_ld.wait_all(sync)

      except _Stop:
          pass
    return nc


def _p6_marks(cfg):
    pass


def host_arrange(cfg, inputs, c):
    d0, d1 = c * cfg.DSL, (c + 1) * cfg.DSL
    NDT, N, KC, RC, KH, E, RR = (cfg.NDT, cfg.N, cfg.KC, cfg.RC, cfg.KH,
                                 cfg.E, cfg.RR)

    def dmaj(a):
        return np.ascontiguousarray(
            a.reshape(NDT, 128, *a.shape[1:]).transpose(
                1, 0, *range(2, a.ndim + 1)))

    out = {}
    out["conv_w_sl"] = dmaj(inputs["L0_attn_conv_w"][d0:d1])
    out["conv_b_sl"] = dmaj(inputs["L0_attn_conv_b"][d0:d1])
    out["D_sl"] = dmaj(inputs["L0_attn_D_param"][d0:d1])
    out["dt_bias_sl"] = dmaj(inputs["L0_attn_dt_bias"][d0:d1])
    out["A_log_sl"] = np.ascontiguousarray(
        dmaj(inputs["L0_attn_A_log"][d0:d1]).reshape(128, NDT * N))
    out["dt_proj_sl"] = dmaj(inputs["L0_attn_dt_proj_w"][d0:d1])
    xp = inputs["L0_attn_x_proj_w"][:, d0:d1]
    xpr = np.zeros((3, 128, cfg.DSL), dtype=np.float32)
    xpr[0] = xp[0:128]
    xpr[1] = xp[128:256]
    xpr[2, 0:2 * N] = xp[256:256 + 2 * N]
    out["x_proj_sl"] = np.ascontiguousarray(xpr.transpose(1, 0, 2))
    out["dt_ln_w"] = np.ascontiguousarray(
        inputs["L0_attn_dt_ln_w"].reshape(RC, 128).T)
    out["W_thr"] = np.ascontiguousarray(
        inputs["L0_ffn_W_thr"].reshape(KH, 128).T)
    out["A_w"] = np.ascontiguousarray(
        inputs["L0_ffn_A_w"].reshape(E * RR, KH, 128))
    out["B_w"] = np.ascontiguousarray(
        inputs["L0_ffn_B_w"].reshape(E, KH, 128, RR).transpose(2, 1, 0, 3))
    return out


def prep_core_inputs(cfg, inputs, c, x2d, bcln, aw2d):
    t0, t1 = c * cfg.TPC, (c + 1) * cfg.TPC
    m = {
        "x": x2d[t0:t1],
        "in_proj_w": inputs["L0_attn_in_proj_w"],
        "out_proj_w": inputs.get("_WoT", np.ascontiguousarray(
            inputs["L0_attn_out_proj_w"].T)),
        "bc_ln_w": np.ascontiguousarray(np.concatenate(
            [bcln, np.zeros(128 - bcln.shape[0], dtype=np.float32)])),
        "W_base": inputs.get("_WbT", np.ascontiguousarray(
            inputs["L0_ffn_W_base"].T)),
        "b_base": np.ascontiguousarray(np.broadcast_to(
            inputs["L0_ffn_b_base"], (128, inputs["L0_ffn_b_base"].shape[0]))),
        "W_gate": np.ascontiguousarray(np.concatenate(
            [inputs["L0_ffn_W_gate"],
             np.zeros((128 - inputs["L0_ffn_W_gate"].shape[0],
                       inputs["L0_ffn_W_gate"].shape[1]), dtype=np.float32)],
            axis=0)),
        "b_thr": np.ascontiguousarray(
            np.broadcast_to(inputs["L0_ffn_b_thr"].reshape(1), (128,))),
    }
    m.update(host_arrange(cfg, inputs, c))
    return m


def kernel(**inputs):
    cfg = CFG_FULL
    hs = inputs["L0_attn_hidden_states"]
    x2d = np.ascontiguousarray(hs.reshape(cfg.NTOK, cfg.H))
    bcln = np.ascontiguousarray(
        np.concatenate([inputs["L0_attn_B_ln_w"], inputs["L0_attn_C_ln_w"]]))
    aw2d = np.ascontiguousarray(
        inputs["L0_ffn_A_w"].reshape(cfg.E * cfg.RR, cfg.H))

    nc = build_nc(cfg)

    inputs = dict(inputs)
    inputs["_WoT"] = np.ascontiguousarray(inputs["L0_attn_out_proj_w"].T)
    inputs["_WbT"] = np.ascontiguousarray(inputs["L0_ffn_W_base"].T)
    in_maps = []
    for c in range(NCORES):
        t0, t1 = c * cfg.TPC, (c + 1) * cfg.TPC
        d0, d1 = c * cfg.DSL, (c + 1) * cfg.DSL
        in_maps.append(prep_core_inputs(cfg, inputs, c, x2d, bcln, aw2d))

    res = run_bass_kernel_spmd(nc, in_maps, core_ids=list(range(NCORES)))
    out = np.concatenate([r["out"] for r in res.results], axis=0)
    return out.reshape(cfg.NB, cfg.LB, cfg.H).astype(np.float32)
